# version 30
# baseline (speedup 1.0000x reference)
import numpy as np
from contextlib import ExitStack

import concourse.bass as bass
import concourse.mybir as mybir
from concourse.bass_utils import run_bass_kernel_spmd

F32 = mybir.dt.float32
BF16 = mybir.dt.bfloat16
AF = mybir.ActivationFunctionType
ALU = mybir.AluOpType
AX = mybir.AxisListType.X

ENGS = ("T", "A", "V", "S", "G")

CFG_FULL = dict(D=1024, H=16, NSEQ=4096, QH=2048, MB=512, NE=8, HE=64)
CFG_MINI = dict(D=256, H=4, NSEQ=512, QH=256, MB=256, NE=4, HE=64)


class Chan:
    __slots__ = ("name", "eng", "step", "n", "sem")

    def __init__(self, name, eng, step):
        self.name, self.eng, self.step, self.n, self.sem = name, eng, step, 0, None


class Prog:
    def __init__(self):
        self.items = {e: [] for e in ENGS}
        self.chans = []
        self.lastw = {e: {} for e in ENGS}

    def chan(self, name, eng, dma=False):
        c = Chan(name, eng, 16 if dma else 1)
        self.chans.append(c)
        return c

    def op(self, eng, fn):
        if eng in ("V", "A"):
            self.items[eng].append(("drain",))
        self.items[eng].append(("op", fn))

    def dep(self, eng):
        self.items[eng].append(("drain",))

    def inc(self, ch):
        ch.n += ch.step
        self.items[ch.eng].append(("inc", ch))
        return (ch, ch.n)

    def wait(self, eng, ev):
        if ev is None:
            return
        ch, n = ev
        if n <= 0:
            return
        assert n <= ch.n, (eng, ch.name, n, ch.n)
        if ch.eng == eng and eng in ("T", "A", "V"):
            return
        if self.lastw[eng].get(ch, 0) >= n:
            return
        self.lastw[eng][ch] = n
        self.items[eng].append(("wait", ch, n))

    def waits(self, eng, evs):
        for ev in evs:
            self.wait(eng, ev)

    def now(self, ch):
        return (ch, ch.n)


def build_program(cfg):
    D = cfg["D"]
    H = cfg["H"]
    HE = cfg["HE"]
    NSEQ = cfg["NSEQ"]
    QH = cfg["QH"]
    MB = cfg["MB"]
    NE = cfg["NE"]
    DC = D // 128
    NP = H // 2
    NB = NSEQ // MB
    MC = MB // 128
    NT = QH // 128
    NQW = QH // MB
    QB = min(512, QH)
    NQB = QH // QB
    OH = min(512, D)
    NHALF = D // OH
    MBH = MB // OH if MB >= OH else 1
    assert HE == 64 and D % 128 == 0 and MB % 128 == 0

    nc = bass.Bass(trn_type="TRN2", target_bir_lowering=False)

    xin = nc.dram_tensor("xin", [NSEQ, D], F32, kind="ExternalInput").ap()
    wq = nc.dram_tensor("wq", [D, D], F32, kind="ExternalInput").ap()
    wk = nc.dram_tensor("wk", [D, D], F32, kind="ExternalInput").ap()
    wv = nc.dram_tensor("wv", [D, D], F32, kind="ExternalInput").ap()
    wp = nc.dram_tensor("wp", [D, D], F32, kind="ExternalInput").ap()
    bp = nc.dram_tensor("bp", [1, D], F32, kind="ExternalInput").ap()
    gw = nc.dram_tensor("gw", [D, NE], F32, kind="ExternalInput").ap()
    gb = nc.dram_tensor("gb", [1, NE], F32, kind="ExternalInput").ap()
    we = nc.dram_tensor("we", [NE, D, D], F32, kind="ExternalInput").ap()
    be = nc.dram_tensor("be", [NE, D], F32, kind="ExternalInput").ap()
    idn = nc.dram_tensor("idn", [128, 128], F32, kind="ExternalInput").ap()
    e2d = nc.dram_tensor("e2d", [2, 128], F32, kind="ExternalInput").ap()
    y = nc.dram_tensor("y", [QH, D], F32, kind="ExternalOutput").ap()

    p = Prog()
    chT = p.chan("chT", "T")
    chA = p.chan("chA", "A")
    chV = p.chan("chV", "V")
    chX0 = p.chan("chX0", "S", dma=True)
    chX1 = p.chan("chX1", "S", dma=True)
    chW0 = p.chan("chW0", "S", dma=True)
    chW1 = p.chan("chW1", "S", dma=True)
    chM = p.chan("chM", "S", dma=True)
    chE0 = p.chan("chE0", "S", dma=True)
    chE1 = p.chan("chE1", "S", dma=True)
    chG = p.chan("chG", "G", dma=True)
    chG1 = p.chan("chG1", "G", dma=True)
    chGb = p.chan("chGb", "G", dma=True)
    chC = [p.chan("chC%d" % i, "S", dma=True) for i in range(4)]
    chGc = p.chan("chGc", "G")
    chO = p.chan("chO", "S", dma=True)
    chX = (chX0, chX1)
    chW = (chW0, chW1)
    chE = (chE0, chE1)

    es = ExitStack()
    with es:
        def sb(name, shape, dt=F32, side=None):
            kw = {} if side is None else {"side": side}
            return es.enter_context(nc.sbuf_tensor(name, shape, dt, **kw))

        sid = sb("sid", [128, 128])
        sidb = sb("sidb", [128, 128], BF16)
        cns = sb("cns", [128, 8])
        gstg = sb("gstg", [128, DC * NE])
        gwh = sb("gwh", [128, DC * NE], BF16)
        gwl = sb("gwl", [128, DC * NE], BF16)
        sTb = sb("sTb", [128, 128])
        s_all = sb("s_all", [128, NT * NE])
        gsc = sb("gsc", [128, 6 * NE + 8])
        stats = sb("stats", [128, 16])
        gbsb = sb("gbsb", [1, NE])
        gbr = sb("gbr", [128, NE])
        ones1 = sb("ones1", [1, 128])
        E2 = sb("E2", [2, 128])

        def ps(name, w=1024):
            return es.enter_context(nc.psum_tensor(name, [128, w], F32))

        psW = [ps("psW0"), ps("psW1"), ps("psT")]
        psPV = [ps("psPVa", 512), ps("psPVb", 512)]
        ps_free = {}

        def ps_wait(eng, t, halves=(0, 1)):
            for hf in halves:
                p.waits(eng, ps_free.get((t.name, hf), ()))

        def ps_read(t, ev, halves=(0, 1)):
            for hf in halves:
                ps_free[(t.name, hf)] = [ev]

        def cvt(dst_hi, dst_lo, src_f32, n, evsrc, extra_wait=None):
            p.wait("A", evsrc)
            if extra_wait is not None:
                p.wait("A", extra_wait)
            p.op("A", lambda e: e.copy(dst_hi, src_f32))
            ev_h = p.inc(chA)
            p.wait("V", ev_h)
            p.op("V", lambda e: e.tensor_tensor(dst_lo, src_f32, dst_hi,
                                                ALU.subtract))
            ev_l = p.inc(chV)
            return ev_h, ev_l

        def mm3(out_ap, lh, ll, rh, rl, start, stop):
            p.op("T", lambda e: e.matmul(out_ap, lh, rh, start=start, stop=False))
            p.op("T", lambda e: e.matmul(out_ap, lh, rl, start=False, stop=False))
            p.op("T", lambda e: e.matmul(out_ap, ll, rh, start=False, stop=stop))

        p.op("S", lambda e: e.dma_start(sid[:, :], idn[:, :]))
        evI = p.inc(chM)
        p.op("S", lambda e: e.dma_start(E2[:, :], e2d[:, :]))
        p.inc(chM)
        p.op("S", lambda e: e.dma_start(
            gstg[:, :].rearrange("p (c g) -> p c g", c=DC),
            gw[:, :].rearrange("(c dp) g -> dp c g", dp=128)))
        p.inc(chM)
        p.op("S", lambda e: e.dma_start(gbsb[:, :], gb[:, :]))
        p.inc(chM)
        evM0 = p.now(chM)

        p.op("V", lambda e: e.memset(cns[:, 0:1], -16.0))
        p.op("V", lambda e: e.memset(cns[:, 1:2], 1e-5))
        p.op("V", lambda e: e.memset(ones1[:, :], 1.0))
        p.op("V", lambda e: e.memset(den[:, :], 0.0))
        evZ = p.inc(chV)

        cvt(gwh[:, :], gwl[:, :], gstg[:, :], DC * NE, evM0)
        p.wait("A", evI)
        p.op("A", lambda e: e.copy(sidb[:, :], sid[:, :]))
        evIb = p.inc(chA)

        p.wait("T", evM0)
        p.wait("T", evZ)
        p.op("T", lambda e: e.matmul(psW[1][:, 0:NE], ones1[0:1, :],
                                     gbsb[0:1, :], start=True, stop=True))
        evT = p.inc(chT)
        p.wait("A", evT)
        p.op("A", lambda e: e.copy(gbr[:, :], psW[1][:, 0:NE]))
        evA_gbr = p.inc(chA)
        ps_read(psW[1], evA_gbr)

        lnsc_free = {}

        def ln_tile(xap, dst_writer, lnsc, x_ev):
            p.wait("V", x_ev)
            p.wait("V", lnsc_free.get(lnsc.name))
            p.op("V", lambda e: e.tensor_mul(lnsc[:, :], xap, xap))
            p.op("V", lambda e: e.reduce_sum(stats[:, 0:1], xap, axis=AX))
            p.op("V", lambda e: e.reduce_sum(stats[:, 1:2], lnsc[:, :], axis=AX))
            p.op("V", lambda e: e.tensor_scalar(stats[:, 2:3], stats[:, 0:1],
                                                1.0 / D, None, ALU.mult))
            p.op("V", lambda e: e.tensor_mul(stats[:, 3:4], stats[:, 2:3],
                                             stats[:, 2:3]))
            p.op("V", lambda e: e.scalar_tensor_tensor(
                stats[:, 4:5], stats[:, 1:2], 1.0 / D, stats[:, 3:4],
                ALU.mult, ALU.subtract))
            p.wait("V", evZ)
            p.op("V", lambda e: e.tensor_add(stats[:, 7:8], stats[:, 4:5],
                                             cns[:, 1:2]))
            p.op("V", lambda e: e.reciprocal(stats[:, 8:9], stats[:, 7:8]))
            evv = p.inc(chV)
            p.wait("A", evv)
            p.op("A", lambda e: e.activation(stats[:, 5:6], stats[:, 8:9],
                                             AF.Sqrt))
            eva = p.inc(chA)
            p.wait("V", eva)
            p.op("V", lambda e: e.scalar_tensor_tensor(
                stats[:, 6:7], stats[:, 2:3], -1.0, stats[:, 5:6],
                ALU.mult, ALU.mult))
            p.op("V", lambda e: e.tensor_scalar(lnsc[:, :], xap, stats[:, 5:6],
                                                stats[:, 6:7], ALU.mult, ALU.add))
            evln = p.inc(chV)
            p.wait("T", evln)
            p.wait("T", evI)
            for r0 in range(0, DC, 4):
                hf = (r0 // 4) % 2
                pst = psW[2]
                chunks = list(range(r0, min(r0 + 4, DC)))
                ps_wait("T", pst, (hf,))
                for j, c in enumerate(chunks):
                    p.op("T", lambda e, c=c, j=j, pst=pst, hf=hf: e.transpose(
                        pst[:, hf * 512 + j * 128:hf * 512 + (j + 1) * 128],
                        lnsc[:, c * 128:(c + 1) * 128], sid[:, :]))
                evt = p.inc(chT)
                evr = dst_writer(chunks[0], pst, len(chunks), evt, hf)
                ps_read(pst, evr, (hf,))
            lnsc_free[lnsc.name] = p.now(chT)
            return evln

        HB = QB // 2
        den = sb("den", [128, HB], side="right")
        rdbuf = sb("rdbuf", [128, 2 * QB], side="right")

        with ExitStack() as es1:
            big1 = es1.enter_context(nc.sbuf_tensor("big1", [128, NP * QH], F32))
            p.op("V", lambda e: e.memset(big1[:, :], 0.0))
            evZ2 = p.inc(chV)

            with ExitStack() as esab:
                def sba(name, shape, dt=F32):
                    return esab.enter_context(nc.sbuf_tensor(name, shape, dt))

                xst = sba("xst", [128, D])
                q_h = sba("q_h", [128, NP * QH], BF16)
                q_l = sba("q_l", [128, NP * QH], BF16)
                ln_h = sba("ln_h", [128, DC * MB], BF16)
                ln_l = sba("ln_l", [128, DC * MB], BF16)
                wstg = sba("wstg", [128, DC * 128])
                wkh = sba("wkh", [128, DC * 128], BF16)
                wkl = sba("wkl", [128, DC * 128], BF16)
                wvh = sba("wvh", [128, DC * 128], BF16)
                wvl = sba("wvl", [128, DC * 128], BF16)
                kh2 = [[sba("kh%d%d" % (g, s), [128, MB], BF16)
                        for s in range(2)] for g in range(2)]
                kl2 = [[sba("kl%d%d" % (g, s), [128, MB], BF16)
                        for s in range(2)] for g in range(2)]
                vstg = sba("vstg", [128, MB])
                VS = 68
                vah2 = [[sba("vah%d%d" % (g, s), [128, MC * VS], BF16)
                         for s in range(2)] for g in range(2)]
                val2 = [[sba("val%d%d" % (g, s), [128, MC * VS], BF16)
                         for s in range(2)] for g in range(2)]
                ptf = [sba("ptf%d" % i, [128, QB]) for i in range(2)]
                pth = [[sba("pth%d%d" % (s, j), [128, QB], BF16)
                        for j in range(2)] for s in range(2)]
                ptl = [[sba("ptl%d%d" % (s, j), [128, QB], BF16)
                        for j in range(2)] for s in range(2)]
                stage = [sba("stg%d" % i, [128, 2 * QB]) for i in range(2)]
                lnsc = sba("lnsc_ab", [128, D])

                for g in range(2):
                    for s in range(2):
                        p.op("V", lambda e, g=g, s=s: e.memset(
                            vah2[g][s][:, :], 1.0))
                        p.op("V", lambda e, g=g, s=s: e.memset(
                            val2[g][s][:, :], 0.0))
                        rows0 = slice(64, 128) if s == 0 else slice(0, 64)
                        p.op("V", lambda e, g=g, s=s, rows0=rows0: e.memset(
                            kh2[g][s][rows0, :], 0.0))
                        p.op("V", lambda e, g=g, s=s, rows0=rows0: e.memset(
                            kl2[g][s][rows0, :], 0.0))
                ev_va_init = p.inc(chV)

                xst_free = [None, None]
                wstg_free = [None]
                stage_free = [[], []]
                lnwin_free = []
                q_ready = None
                ptf_free = [None, None]
                pthl_free = [[None, None], [None, None]]

                def load_x_tile(r0, slot_i):
                    p.wait("S", xst_free[0])
                    p.op("S", lambda e, r0=r0: e.dma_start(
                        xst[:, :], xin[r0:r0 + 128, :]))
                    return p.inc(chX[0])

                def ln_window_cbs(row0, nw_tiles):
                    evs = []

                    def mk(tt):
                        def cb():
                            if tt == 0 and lnwin_free:
                                p.wait("A", lnwin_free[-1])
                                p.wait("V", lnwin_free[-1])
                                del lnwin_free[:]
                            evx = load_x_tile(row0 + tt * 128, 0)
                            xap = xst[:, :]

                            def wr(c0, pst, n, evt, hf, tt=tt):
                                hi = bass.AP(ln_h, c0 * MB + tt * 128,
                                             [[DC * MB, 128], [MB, n], [1, 128]])
                                lo = bass.AP(ln_l, c0 * MB + tt * 128,
                                             [[DC * MB, 128], [MB, n], [1, 128]])
                                _, evl = cvt(hi, lo,
                                             pst[:, hf * 512:hf * 512 + n * 128],
                                             n * 128, evt)
                                return evl

                            evln = ln_tile(xap, wr, lnsc, evx)
                            xst_free[0] = evln
                            evs.append(p.now(chV))
                        return cb

                    return evs, [mk(tt) for tt in range(nw_tiles)]

                def ln_window(row0, nw_tiles):
                    evs, cbs = ln_window_cbs(row0, nw_tiles)
                    for cb in cbs:
                        cb()
                    return evs

                def load_w_slice(dram, pcol, slot_i):
                    p.wait("S", wstg_free[0])
                    src = dram[:, pcol * 128:(pcol + 1) * 128].rearrange(
                        "(c dp) e -> dp c e", dp=128)
                    dstv = wstg[:, :].rearrange("p (c e) -> p c e", c=DC)
                    p.op("S", lambda e, src=src, dstv=dstv: e.dma_start(dstv, src))
                    return p.inc(chW[slot_i])

                def gen_mm(pst, wh, wl, w_evl, lnwin_evs, base=0):
                    ps_wait("T", pst, (base // 512,) if MB <= 512 else (0, 1))
                    p.wait("T", w_evl)
                    p.waits("T", lnwin_evs)
                    for c in range(DC):
                        for h in range(MBH):
                            o0 = h * OH
                            w0 = min(OH, MB)
                            mm3(pst[:, base + o0:base + o0 + w0],
                                wh[:, c * 128:(c + 1) * 128],
                                wl[:, c * 128:(c + 1) * 128],
                                bass.AP(ln_h, c * MB + o0,
                                        [[DC * MB, 128], [1, w0]]),
                                bass.AP(ln_l, c * MB + o0,
                                        [[DC * MB, 128], [1, w0]]),
                                start=(c == 0), stop=(c == DC - 1))
                    ev = p.inc(chT)
                    lnwin_free.append(ev)
                    return ev

                for w in range(NQW):
                    lnev = ln_window(w * MB, MB // 128)
                    for pi in range(NP):
                        slot_i = pi % 2
                        wev = load_w_slice(wq, pi, slot_i)
                        wh, wl = (wkh, wkl) if slot_i == 0 else (wvh, wvl)
                        _, wevl = cvt(wh[:, :], wl[:, :], wstg[:, :], DC * 128,
                                      wev)
                        wstg_free[0] = wevl
                        pst = psW[slot_i]
                        evg = gen_mm(pst, wh, wl, wevl, lnev)
                        _, evql = cvt(
                            q_h[:, pi * QH + w * MB: pi * QH + w * MB + MB],
                            q_l[:, pi * QH + w * MB: pi * QH + w * MB + MB],
                            pst[:, 0:MB], MB, evg)
                        ps_read(pst, evql, (0,) if MB <= 512 else (0, 1))
                q_ready = p.now(chV)

                pair_acc_ev = [[None, None] for _ in range(NP)]
                evkl_set = [None, None]
                ev_va_set = [[None, None], [None, None]]
                kread_free = [[], []]
                psT_tr_free = [[], []]

                def make_gen_cbs(mb, pi, seti, lnev):
                    st = {}
                    kh, kl = kh2[seti], kl2[seti]
                    vah, val = vah2[seti], val2[seti]

                    def cb_kload():
                        st["wkev"] = load_w_slice(wk, pi, 0)

                    def cb_kcvt():
                        _, st["wkevl"] = cvt(wkh[:, :], wkl[:, :], wstg[:, :],
                                             DC * 128, st["wkev"])
                        wstg_free[0] = st["wkevl"]

                    def cb_kgen():
                        st["evk"] = gen_mm(psW[2], wkh, wkl, st["wkevl"],
                                           lnev, base=0)

                    def cb_kevac():
                        p.waits("A", kread_free[seti])
                        p.waits("V", kread_free[seti])
                        _, ev1 = cvt(kh[0][0:64, :], kl[0][0:64, :],
                                     psW[2][0:64, 0:MB], MB, st["evk"])
                        _, evkl = cvt(kh[1][64:128, :], kl[1][64:128, :],
                                      psW[2][64:128, 0:MB], MB, st["evk"])
                        ps_read(psW[2], evkl, (0,))
                        evkl_set[seti] = evkl

                    def cb_vload():
                        st["wvev"] = load_w_slice(wv, pi, 1)

                    def cb_vcvt():
                        _, st["wvevl"] = cvt(wvh[:, :], wvl[:, :], wstg[:, :],
                                             DC * 128, st["wvev"])
                        wstg_free[0] = st["wvevl"]

                    def cb_vgen():
                        st["evv"] = gen_mm(psW[2], wvh, wvl, st["wvevl"],
                                           lnev, base=512)

                    def cb_vevac():
                        p.wait("A", st["evv"])
                        p.op("A", lambda e: e.copy(vstg[:, :],
                                                   psW[2][:, 512:512 + MB]))
                        st["evvs"] = p.inc(chA)
                        ps_read(psW[2], st["evvs"], (1,))

                    def mk_vtrans(s):
                        def cb():
                            off = s * 512
                            ps_wait("T", psW[2], (s,))
                            p.wait("T", st["evvs"])
                            p.waits("T", psT_tr_free[s])
                            for c in range(MC):
                                p.op("T", lambda e, s=s, c=c, off=off:
                                     e.transpose(
                                         psW[2][:, off + c * 64:
                                                off + (c + 1) * 64],
                                         vstg[s * 64:(s + 1) * 64,
                                              c * 128:(c + 1) * 128],
                                         sid[s * 64:(s + 1) * 64,
                                             s * 64:(s + 1) * 64]))
                            evt = p.inc(chT)
                            p.waits("A", kread_free[seti])
                            p.waits("V", kread_free[seti])
                            hiap = bass.AP(vah[s], 0,
                                           [[MC * VS, 128], [VS, MC], [1, 64]])
                            loap = bass.AP(val[s], 0,
                                           [[MC * VS, 128], [VS, MC], [1, 64]])
                            _, eva = cvt(hiap, loap,
                                         psW[2][:, off:off + MC * 64],
                                         MC * 64, evt)
                            ps_read(psW[2], eva, (s,))
                            ev_va_set[seti][s] = eva
                        return cb

                    return [cb_kload, cb_kcvt, cb_kgen, cb_kevac,
                            cb_vload, cb_vcvt, cb_vgen, cb_vevac,
                            mk_vtrans(0), mk_vtrans(1)]

                units = [(mb, pi) for mb in range(NB) for pi in range(NP)]
                lnev_cur = None
                lnev_next = None
                for ui, (mb, pi) in enumerate(units):
                    seti = ui % 2
                    if pi == 0:
                        if mb == 0:
                            lnev_cur = ln_window(mb * MB, MB // 128)
                            for cb in make_gen_cbs(mb, 0, seti, lnev_cur):
                                cb()
                        else:
                            lnev_cur = lnev_next
                    evkl = evkl_set[seti]
                    ev_va = ev_va_set[seti]
                    k_h, k_l = kh2[seti], kl2[seti]
                    vah, val = vah2[seti], val2[seti]
                    inj_step = 3
                    if pi + 1 < NP:
                        gen_cbs = make_gen_cbs(mb, pi + 1, (ui + 1) % 2,
                                               lnev_cur)
                    elif mb + 1 < NB:
                        lnev_next, ln_cbs = ln_window_cbs((mb + 1) * MB,
                                                          MB // 128)
                        gen_cbs = ln_cbs + make_gen_cbs(mb + 1, 0,
                                                        (ui + 1) % 2,
                                                        lnev_next)
                        inj_step = 2
                    else:
                        gen_cbs = []

                    pv_done = [None, None]

                    def emit_S_chain(i, qb, c, s):
                        hs = qb * MC + c
                        par = hs % 2
                        gpar = i % 2
                        off = par * 512
                        pss = psW[s]
                        q0 = pi * QH + qb * QB
                        ps_wait("T", pss, (par,))
                        p.wait("T", evkl)
                        p.wait("T", q_ready)
                        mm3(pss[:, off:off + QB],
                            k_h[s][:, c * 128:(c + 1) * 128],
                            k_l[s][:, c * 128:(c + 1) * 128],
                            q_h[:, q0:q0 + QB],
                            q_l[:, q0:q0 + QB],
                            start=True, stop=True)
                        evs_mm = p.inc(chT)
                        p.wait("A", evs_mm)
                        p.wait("A", evZ)
                        p.wait("A", ptf_free[gpar])
                        p.op("A", lambda e, gpar=gpar, pss=pss, off=off:
                             e.activation(ptf[gpar][:, :],
                                          pss[:, off:off + QB], AF.Exp,
                                          bias=cns[:, 0:1], scale=0.125))
                        ev_exp = p.inc(chA)
                        ps_read(pss, ev_exp, (par,))
                        p.wait("V", ev_exp)
                        p.wait("V", pthl_free[s][par])
                        p.op("V", lambda e, s=s, par=par, gpar=gpar:
                             e.tensor_copy(pth[s][par][:, :],
                                           ptf[gpar][:, :]))
                        p.op("V", lambda e, s=s, par=par, gpar=gpar:
                             e.tensor_tensor(ptl[s][par][:, :],
                                             ptf[gpar][:, :],
                                             pth[s][par][:, :],
                                             ALU.subtract))
                        ev_l = p.inc(chV)
                        ptf_free[gpar] = ev_l
                        return ev_l, par

                    def flush_slot(qb, s, ev_pv):
                        sg = qb % 2
                        chGs = (chG, chG1)[sg]
                        ce = "A" if s == 0 else "V"
                        p.waits(ce, stage_free[sg])
                        p.wait(ce, ev_pv)
                        if s == 0:
                            p.op("A", lambda e, sg=sg: e.copy(
                                stage[sg][0:65, 0:QB], psPV[0][0:65, 0:QB]))
                            evc = p.inc(chA)
                        else:
                            p.op("V", lambda e, sg=sg: e.tensor_copy(
                                stage[sg][0:65, QB:2 * QB], psPV[1][0:65, 0:QB]))
                            evc = p.inc(chV)
                        ps_read(psPV[s], evc, (0,))
                        p.wait("G", evc)
                        p.wait("G", ev_va_init)
                        p.wait("G", evZ2)
                        r0, r1 = (0, 64) if s == 0 else (64, 128)
                        p.op("G", lambda e, pi=pi, qb=qb, sg=sg, s=s,
                             r0=r0, r1=r1: e.dma_start(
                                 big1[r0:r1, pi * QH + qb * QB:
                                      pi * QH + (qb + 1) * QB],
                                 stage[sg][0:64, s * QB:(s + 1) * QB],
                                 accum_op=ALU.add))
                        p.inc(chGs)
                        hh0 = pi if s == 0 else pi + NP
                        p.op("G", lambda e, qb=qb, sg=sg, s=s, hh0=hh0:
                             e.dma_start(
                                 bass.AP(den, (hh0 * NQB + qb) * 2 * HB,
                                         [[HB, 2], [1, HB]]),
                                 stage[sg][64:65, s * QB:(s + 1) * QB],
                                 accum_op=ALU.add))
                        evg = p.inc(chGs)
                        if s == 1:
                            stage_free[sg] = [evg]
                            pair_acc_ev[pi][sg] = evg

                    def emit_PV(st2, chain_ev, par):
                        qb, c, s = st2
                        if c == 0:
                            ps_wait("T", psPV[s], (0,))
                        p.wait("T", chain_ev)
                        p.wait("T", ev_va[s])
                        mm3(psPV[s][0:65, 0:QB],
                            vah[s][:, c * VS:c * VS + 65],
                            val[s][:, c * VS:c * VS + 65],
                            pth[s][par][:, :],
                            ptl[s][par][:, :],
                            start=(c == 0), stop=(c == MC - 1))
                        ev_pv = p.inc(chT)
                        pthl_free[s][par] = ev_pv
                        if c == MC - 1:
                            pv_done[s] = ev_pv
                            flush_slot(qb, s, ev_pv)

                    steps = [(qb, c, s) for qb in range(NQB)
                             for c in range(MC) for s in range(2)]
                    pend = []
                    for i, st2 in enumerate(steps):
                        ch_ev, par = emit_S_chain(i, *st2)
                        pend.append((st2, ch_ev, par))
                        if len(pend) > 3:
                            emit_PV(*pend.pop(0))
                        if gen_cbs and i >= 2 and (i - 2) % inj_step == 0:
                            gen_cbs.pop(0)()
                    while gen_cbs:
                        gen_cbs.pop(0)()
                    while pend:
                        emit_PV(*pend.pop(0))
                    kread_free[seti] = [p.now(chT)]
                    psT_tr_free[0] = [p.now(chT)]
                    psT_tr_free[1] = [p.now(chT)]

                bar_pre = [p.now(chT), p.now(chA), p.now(chV), p.now(chG),
                           p.now(chG1)]
                ev_mult_qb = [None] * NQB
                slot_free = [None] * 2
                sli = 0
                for qb in range(NQB):
                    for pi in range(NP):
                        sl = sli % 2
                        hf = sli % 2
                        sli += 1
                        c0 = sl * QB
                        p.waits("S", pair_acc_ev[pi])
                        p.wait("S", slot_free[sl])
                        p.op("S", lambda e, pi=pi, qb=qb, c0=c0: e.dma_start(
                            rdbuf[0:1, c0:c0 + QB],
                            bass.AP(den, (pi * NQB + qb) * 2 * HB,
                                    [[HB, 2], [1, HB]])))
                        p.inc(chC[sl])
                        p.op("S", lambda e, pi=pi, qb=qb, c0=c0: e.dma_start(
                            rdbuf[1:2, c0:c0 + QB],
                            bass.AP(den, ((pi + NP) * NQB + qb) * 2 * HB,
                                    [[HB, 2], [1, HB]])))
                        evd = p.inc(chC[sl])
                        p.wait("V", evd)
                        p.op("V", lambda e, c0=c0: e.reciprocal(
                            rdbuf[0:2, c0:c0 + QB], rdbuf[0:2, c0:c0 + QB]))
                        ev_rc = p.inc(chV)
                        ps_wait("T", psW[2], (hf,))
                        p.wait("T", ev_rc)
                        p.wait("T", evM0)
                        p.op("T", lambda e, c0=c0, hf=hf: e.matmul(
                            psW[2][:, hf * QB:(hf + 1) * QB], E2[0:2, :],
                            rdbuf[0:2, c0:c0 + QB], start=True, stop=True))
                        ev_rep = p.inc(chT)
                        slot_free[sl] = ev_rep
                        p.wait("V", ev_rep)
                        p.op("V", lambda e, pi=pi, qb=qb, hf=hf: e.tensor_mul(
                            big1[:, pi * QH + qb * QB: pi * QH + (qb + 1) * QB],
                            big1[:, pi * QH + qb * QB: pi * QH + (qb + 1) * QB],
                            psW[2][:, hf * QB:(hf + 1) * QB]))
                        ev_mult = p.inc(chV)
                        ps_read(psW[2], ev_mult, (hf,))
                    ev_mult_qb[qb] = ev_mult
                feats_ready = ev_mult

            out_acc = sb("out_acc", [128, NT * D], side="right")
            with ExitStack() as espd:
                def sbp(name, shape, dt=F32):
                    return espd.enter_context(nc.sbuf_tensor(name, shape, dt))

                wph = sbp("wph", [128, DC * D], BF16)
                wpl = sbp("wpl", [128, DC * D], BF16)
                wpstg = sbp("wpstg", [128, 2 * D])
                bpr = sbp("bpr", [128, D])
                bpsb = sbp("bpsb", [1, D])
                xst2 = sbp("xst2", [128, 2 * D])
                fch = [sbp("fch%d" % i, [128, 128], BF16) for i in range(2)]
                fcl = [sbp("fcl%d" % i, [128, 128], BF16) for i in range(2)]

                p.waits("S", bar_pre)
                wps_free = [None, None]
                for c in range(DC):
                    sl = c % 2
                    p.wait("S", wps_free[sl])
                    p.op("S", lambda e, c=c, sl=sl: e.dma_start(
                        wpstg[:, sl * D:(sl + 1) * D],
                        wp[c * 128:(c + 1) * 128, :]))
                    evw = p.inc(chM if sl == 0 else chW1)
                    _, evwl = cvt(wph[:, c * D:(c + 1) * D],
                                  wpl[:, c * D:(c + 1) * D],
                                  wpstg[:, sl * D:(sl + 1) * D], D, evw)
                    wps_free[sl] = evwl
                ev_wp = evwl
                p.op("S", lambda e: e.dma_start(bpsb[:, :], bp[:, :]))
                evbp = p.inc(chW0)
                p.wait("T", evbp)
                p.wait("T", evZ)
                for hh in range(NHALF):
                    pst = psW[hh % 2]
                    ps_wait("T", pst)
                    p.op("T", lambda e, hh=hh, pst=pst: e.matmul(
                        pst[:, 0:OH], ones1[0:1, :],
                        bpsb[0:1, hh * OH:(hh + 1) * OH], start=True,
                        stop=True))
                    evt = p.inc(chT)
                    p.wait("A", evt)
                    p.op("A", lambda e, hh=hh, pst=pst: e.copy(
                        bpr[:, hh * OH:(hh + 1) * OH], pst[:, 0:OH]))
                    eva = p.inc(chA)
                    ps_read(pst, eva)
                ev_bpr = p.now(chA)

                xst2_free = [None, None]
                fc_free = [None, None]
                for t in range(NT):
                    pst = psW[t % 2]
                    ps_wait("T", pst)
                    p.wait("T", ev_wp)
                    for c in range(DC):
                        sl = c % 2
                        p.wait("A", fc_free[sl])
                        _, evf = cvt(fch[sl][:, :], fcl[sl][:, :],
                                     big1[:, c * QH + t * 128:
                                          c * QH + t * 128 + 128],
                                     128, ev_mult_qb[t // (QB // 128)])
                        p.wait("T", evf)
                        for hh in range(NHALF):
                            mm3(pst[:, hh * OH:(hh + 1) * OH],
                                fch[sl][:, :], fcl[sl][:, :],
                                wph[:, c * D + hh * OH: c * D + (hh + 1) * OH],
                                wpl[:, c * D + hh * OH: c * D + (hh + 1) * OH],
                                start=(c == 0), stop=(c == DC - 1))
                        evt = p.inc(chT)
                        fc_free[sl] = evt
                    ev_proj = p.now(chT)
                    sl2 = t % 2
                    p.wait("S", xst2_free[sl2])
                    p.op("S", lambda e, t=t, sl2=sl2: e.dma_start(
                        xst2[:, sl2 * D:(sl2 + 1) * D],
                        xin[t * 128:(t + 1) * 128, :]))
                    evx = p.inc(chX[sl2])
                    oa = out_acc[:, t * D:(t + 1) * D]
                    p.wait("V", ev_proj)
                    p.wait("V", ev_bpr)
                    p.op("V", lambda e, oa=oa, pst=pst: e.tensor_add(
                        oa, pst[0:128, 0:D], bpr[:, :]))
                    ev1 = p.inc(chV)
                    ps_read(pst, ev1)
                    p.wait("V", evx)
                    p.op("V", lambda e, oa=oa, sl2=sl2: e.tensor_add(
                        oa, oa, xst2[:, sl2 * D:(sl2 + 1) * D]))
                    ev_x2 = p.inc(chV)
                    xst2_free[sl2] = ev_x2
                x2_done_V = p.now(chV)
                x2_done_T = p.now(chT)

        with ExitStack() as es2:
            def sb2(name, shape, dt=F32):
                return es2.enter_context(nc.sbuf_tensor(name, shape, dt))

            ln2h = sb2("ln2h", [128, NP * QH], BF16)
            ln2l = sb2("ln2l", [128, NP * QH], BF16)
            besb = sb2("besb", [128, D])
            lnsc2 = sb2("lnsc_d", [128, D])

            p.waits("S", [x2_done_V, x2_done_T])
            p.wait("V", x2_done_T)
            p.wait("A", x2_done_T)
            p.op("S", lambda e: e.dma_start(besb[0:NE, :], be[:, :]))
            ev_be = p.inc(chM)

            espe = es2.enter_context(ExitStack())

            def sbe(name, shape, dt=F32):
                return espe.enter_context(nc.sbuf_tensor(name, shape, dt))

            estg = sbe("estg", [128, DC * OH])
            weh = [sbe("weh%d" % i, [128, DC * OH], BF16) for i in range(2)]
            wel = [sbe("wel%d" % i, [128, DC * OH], BF16) for i in range(2)]
            tasks = [(ei, hh) for ei in range(NE) for hh in range(NHALF)]
            web_free = [None, None]
            estg_free = [None]
            ev_we = [None, None]
            ev_ln2_t = [None] * NT

            def load_task(ti, sl):
                ei, hh = tasks[ti]
                p.wait("S", estg_free[0])
                src3 = we[ei, :, :].rearrange("(c dp) o -> dp c o", dp=128)
                src3 = src3[:, :, hh * OH:(hh + 1) * OH]
                p.op("S", lambda e, src3=src3: e.dma_start(
                    estg[:, :].rearrange("p (c o) -> p c o", c=DC), src3))
                evl = p.inc(chE[sl])
                p.waits("A", [web_free[sl]])
                p.waits("V", [web_free[sl]])
                _, evcl = cvt(weh[sl][:, :], wel[sl][:, :], estg[:, :],
                              DC * OH, evl)
                estg_free[0] = evcl
                ev_we[sl] = evcl

            def emit_task_tile(ti, sl, t, ev_ln2_dep):
                ei, hh = tasks[ti]
                pst = psW[t % 3]
                ps_wait("T", pst)
                p.wait("T", ev_we[sl])
                p.wait("T", ev_ln2_dep)
                for c in range(DC):
                    mm3(pst[:, 0:OH],
                        ln2h[:, c * QH + t * 128: c * QH + t * 128 + 128],
                        ln2l[:, c * QH + t * 128: c * QH + t * 128 + 128],
                        weh[sl][:, c * OH:(c + 1) * OH],
                        wel[sl][:, c * OH:(c + 1) * OH],
                        start=(c == 0), stop=(c == DC - 1))
                ev_y = p.inc(chT)
                p.wait("V", ev_y)
                p.op("V", lambda e, hh=hh, t=t, ei=ei, pst=pst:
                     e.scalar_tensor_tensor(
                         out_acc[:, t * D + hh * OH: t * D + (hh + 1) * OH],
                         pst[:, 0:OH],
                         s_all[:, t * NE + ei:t * NE + ei + 1],
                         out_acc[:, t * D + hh * OH: t * D + (hh + 1) * OH],
                         ALU.mult, ALU.add))
                ev_o = p.inc(chV)
                ps_read(pst, ev_o)
                if t == NT - 1:
                    web_free[sl] = p.now(chT)
                if ei == NE - 1:
                    p.wait("S", ev_o)
                    p.op("S", lambda e, t=t, hh=hh: e.dma_start(
                        y[t * 128:(t + 1) * 128, hh * OH:(hh + 1) * OH],
                        out_acc[:, t * D + hh * OH: t * D + (hh + 1) * OH]))
                    p.inc(chO)

            load_task(0, 0)
            for t in range(NT):
                oa = out_acc[:, t * D:(t + 1) * D]

                def wr2(c0, pst, n, evt, hf, t=t):
                    hi = bass.AP(ln2h, c0 * QH + t * 128,
                                 [[NP * QH, 128], [QH, n], [1, 128]])
                    lo = bass.AP(ln2l, c0 * QH + t * 128,
                                 [[NP * QH, 128], [QH, n], [1, 128]])
                    _, evl = cvt(hi, lo,
                                 pst[:, hf * 512:hf * 512 + n * 128],
                                 n * 128, evt)
                    return evl

                ln_tile(oa, wr2, lnsc2, x2_done_V)
                ev_ln2 = p.now(chV)
                pst = psW[1]
                ps_wait("T", pst)
                p.wait("T", ev_ln2)
                p.wait("T", p.now(chA))
                for c in range(DC):
                    mm3(pst[:, 0:NE],
                        ln2h[:, c * QH + t * 128: c * QH + t * 128 + 128],
                        ln2l[:, c * QH + t * 128: c * QH + t * 128 + 128],
                        gwh[:, c * NE:(c + 1) * NE],
                        gwl[:, c * NE:(c + 1) * NE],
                        start=(c == 0), stop=(c == DC - 1))
                ev_gmm = p.inc(chT)
                z0, e0, g0, sl0, w0, sc0 = 0, NE, 2 * NE, 3 * NE, 4 * NE, 6 * NE
                p.wait("V", ev_gmm)
                p.wait("V", evA_gbr)
                p.op("V", lambda e, pst=pst: e.tensor_add(
                    gsc[:, z0:z0 + NE], pst[:, 0:NE], gbr[:, :]))
                ev_z = p.inc(chV)
                ps_read(pst, ev_z)
                p.op("V", lambda e: e.reduce_max(gsc[:, sc0:sc0 + 1],
                                                 gsc[:, z0:z0 + NE], axis=AX,
                                                 negate=True))
                ev_nm = p.inc(chV)
                p.wait("A", ev_nm)
                p.op("A", lambda e: e.activation(gsc[:, e0:e0 + NE],
                                                 gsc[:, z0:z0 + NE], AF.Exp,
                                                 bias=gsc[:, sc0:sc0 + 1],
                                                 scale=1.0))
                ev_ge = p.inc(chA)
                p.wait("V", ev_ge)
                p.op("V", lambda e: e.reduce_sum(gsc[:, sc0 + 1:sc0 + 2],
                                                 gsc[:, e0:e0 + NE], axis=AX))
                p.op("V", lambda e: e.reciprocal(gsc[:, sc0 + 2:sc0 + 3],
                                                 gsc[:, sc0 + 1:sc0 + 2]))
                p.op("V", lambda e: e.tensor_scalar(gsc[:, g0:g0 + NE],
                                                    gsc[:, e0:e0 + NE],
                                                    gsc[:, sc0 + 2:sc0 + 3],
                                                    None, ALU.mult))
                p.op("V", lambda e: e.tensor_single_scalar(
                    gsc[:, sl0:sl0 + NE], gsc[:, g0:g0 + NE], 0.1, ALU.is_ge))
                p.op("V", lambda e: e.tensor_mul(gsc[:, w0:w0 + NE],
                                                 gsc[:, g0:g0 + NE],
                                                 gsc[:, sl0:sl0 + NE]))
                p.op("V", lambda e: e.reduce_sum(gsc[:, sc0 + 3:sc0 + 4],
                                                 gsc[:, w0:w0 + NE], axis=AX))
                p.op("V", lambda e: e.reciprocal(gsc[:, sc0 + 4:sc0 + 5],
                                                 gsc[:, sc0 + 3:sc0 + 4]))
                p.op("V", lambda e, t=t: e.tensor_scalar(
                    s_all[:, t * NE:(t + 1) * NE], gsc[:, w0:w0 + NE],
                    gsc[:, sc0 + 4:sc0 + 5], None, ALU.mult))
                ev_s = p.inc(chV)
                ps_wait("T", psW[2], (0,))
                p.wait("T", ev_s)
                p.op("T", lambda e, t=t: e.transpose(
                    psW[2][0:NE, 0:128], s_all[:, t * NE:(t + 1) * NE],
                    sid[:, :]))
                ev_st = p.inc(chT)
                p.wait("A", ev_st)
                p.op("A", lambda e: e.copy(sTb[0:NE, :], psW[2][0:NE, 0:128]))
                ev_stc = p.inc(chA)
                ps_read(psW[2], ev_stc)
                p.wait("T", ev_stc)
                p.wait("T", ev_be)
                for hh in range(NHALF):
                    pst2 = psPV[hh % 2]
                    ps_wait("T", pst2, (0,))
                    p.op("T", lambda e, hh=hh, pst2=pst2: e.matmul(
                        pst2[:, 0:OH], sTb[0:NE, :],
                        besb[0:NE, hh * OH:(hh + 1) * OH],
                        start=True, stop=True))
                    evb = p.inc(chT)
                    p.wait("V", evb)
                    p.op("V", lambda e, hh=hh, t=t, pst2=pst2: e.tensor_add(
                        out_acc[:, t * D + hh * OH: t * D + (hh + 1) * OH],
                        out_acc[:, t * D + hh * OH: t * D + (hh + 1) * OH],
                        pst2[:, 0:OH]))
                    evv = p.inc(chV)
                    ps_read(pst2, evv, (0,))
                ev_ln2_t[t] = ev_ln2
                emit_task_tile(0, 0, t, ev_ln2)
            ln2_all = p.now(chV)

            load_task(1, 1)
            for ti in range(1, len(tasks)):
                sl = ti % 2
                if ti + 1 < len(tasks):
                    load_task(ti + 1, (ti + 1) % 2)
                for t in range(NT):
                    emit_task_tile(ti, sl, t, ln2_all)

        ev_out = p.now(chO)

        for c in p.chans:
            c.sem = es.enter_context(nc.semaphore(c.name))
        block = es.enter_context(nc.Block())
        eng_map = {"T": block.tensor, "A": block.scalar, "V": block.vector,
                   "S": block.sync, "G": block.gpsimd}

        def runner(eng_name):
            def run(engobj):
                last = None
                for it in p.items[eng_name]:
                    if it[0] == "op":
                        r = it[1](engobj)
                        if r is not None:
                            last = r
                    elif it[0] == "inc":
                        last.then_inc(it[1].sem, it[1].step)
                    elif it[0] == "drain":
                        engobj.drain()
                    else:
                        engobj.wait_ge(it[1].sem, it[2])
                if eng_name == "S":
                    engobj.wait_ge(ev_out[0].sem, ev_out[1])
            return run

        for name, deco in eng_map.items():
            deco(runner(name))
    return nc


def _prep_inputs(cfg, L0_attn_query, Wq, Wk, Wv, Wp, bp, gate_w, gate_b,
                 expert_w, expert_b):
    D, H, NSEQ, QH, NE = (cfg["D"], cfg["H"], cfg["NSEQ"], cfg["QH"], cfg["NE"])
    NP = H // 2
    x = np.ascontiguousarray(np.asarray(L0_attn_query, dtype=np.float32))
    B = x.shape[0]
    head_order = []
    for pi in range(NP):
        head_order += [pi, pi + NP]
    wq_h = np.ascontiguousarray(
        np.asarray(Wq, np.float32).transpose(1, 0, 2)[:, head_order, :].reshape(D, D))
    wk_h = np.ascontiguousarray(
        np.asarray(Wk, np.float32).transpose(1, 0, 2)[:, head_order, :].reshape(D, D))
    wv_h = np.ascontiguousarray(
        np.asarray(Wv, np.float32).transpose(1, 0, 2)[:, head_order, :].reshape(D, D))
    idx = np.empty(D, dtype=np.int64)
    for pi in range(NP):
        for s in range(2):
            h = pi + NP * s
            for e in range(64):
                idx[pi * 128 + s * 64 + e] = e * H + h
    wp_h = np.ascontiguousarray(np.asarray(Wp, np.float32)[idx])
    in_maps = []
    ncore = 2 * B
    for c in range(ncore):
        b, half = c // 2, c % 2
        ours = x[b, half * QH:(half + 1) * QH]
        other = x[b, (1 - half) * QH:(2 - half) * QH]
        xkv = np.ascontiguousarray(np.concatenate([ours, other], axis=0))
        in_maps.append({
            "xin": xkv,
            "wq": wq_h, "wk": wk_h, "wv": wv_h, "wp": wp_h,
            "bp": np.asarray(bp, np.float32).reshape(1, D),
            "gw": np.ascontiguousarray(np.asarray(gate_w, np.float32)),
            "gb": np.asarray(gate_b, np.float32).reshape(1, NE),
            "we": np.ascontiguousarray(np.asarray(expert_w, np.float32)),
            "be": np.ascontiguousarray(np.asarray(expert_b, np.float32)),
            "idn": np.eye(128, dtype=np.float32),
            "e2d": np.repeat(np.eye(2, dtype=np.float32), 64, axis=1),
        })
    return in_maps


def kernel(L0_attn_query, L0_attn_Wq, L0_attn_Wk, L0_attn_Wv, L0_attn_Wp,
           L0_attn_bp, L0_ffn_gate_w, L0_ffn_gate_b, L0_ffn_expert_w,
           L0_ffn_expert_b):
    cfg = CFG_FULL
    in_maps = _prep_inputs(cfg, L0_attn_query, L0_attn_Wq, L0_attn_Wk,
                           L0_attn_Wv, L0_attn_Wp, L0_attn_bp, L0_ffn_gate_w,
                           L0_ffn_gate_b, L0_ffn_expert_w, L0_ffn_expert_b)
    nc = build_program(cfg)
    res = run_bass_kernel_spmd(nc, in_maps, core_ids=list(range(len(in_maps))))
    B, NSEQ, D, QH = 4, cfg["NSEQ"], cfg["D"], cfg["QH"]
    out = np.empty((B, NSEQ, D), dtype=np.float32)
    for c, r in enumerate(res.results):
        b, half = c // 2, c % 2
        out[b, half * QH:(half + 1) * QH] = np.asarray(r["y"], dtype=np.float32)
    return out


# version 31
# speedup vs baseline: 1.0451x; 1.0451x over previous
import numpy as np
from contextlib import ExitStack

import concourse.bass as bass
import concourse.mybir as mybir
from concourse.bass_utils import run_bass_kernel_spmd

F32 = mybir.dt.float32
BF16 = mybir.dt.bfloat16
AF = mybir.ActivationFunctionType
ALU = mybir.AluOpType
AX = mybir.AxisListType.X

ENGS = ("T", "A", "V", "S", "G")

CFG_FULL = dict(D=1024, H=16, NSEQ=4096, QH=2048, MB=512, NE=8, HE=64)
CFG_MINI = dict(D=256, H=4, NSEQ=512, QH=256, MB=256, NE=4, HE=64)


class Chan:
    __slots__ = ("name", "eng", "step", "n", "sem")

    def __init__(self, name, eng, step):
        self.name, self.eng, self.step, self.n, self.sem = name, eng, step, 0, None


class Prog:
    def __init__(self):
        self.items = {e: [] for e in ENGS}
        self.chans = []
        self.lastw = {e: {} for e in ENGS}

    def chan(self, name, eng, dma=False):
        c = Chan(name, eng, 16 if dma else 1)
        self.chans.append(c)
        return c

    def op(self, eng, fn):
        if eng in ("V", "A"):
            self.items[eng].append(("drain",))
        self.items[eng].append(("op", fn))

    def dep(self, eng):
        self.items[eng].append(("drain",))

    def inc(self, ch):
        ch.n += ch.step
        self.items[ch.eng].append(("inc", ch))
        return (ch, ch.n)

    def wait(self, eng, ev):
        if ev is None:
            return
        ch, n = ev
        if n <= 0:
            return
        assert n <= ch.n, (eng, ch.name, n, ch.n)
        if ch.eng == eng and eng in ("T", "A", "V"):
            return
        if self.lastw[eng].get(ch, 0) >= n:
            return
        self.lastw[eng][ch] = n
        self.items[eng].append(("wait", ch, n))

    def waits(self, eng, evs):
        for ev in evs:
            self.wait(eng, ev)

    def now(self, ch):
        return (ch, ch.n)


def build_program(cfg):
    D = cfg["D"]
    H = cfg["H"]
    HE = cfg["HE"]
    NSEQ = cfg["NSEQ"]
    QH = cfg["QH"]
    MB = cfg["MB"]
    NE = cfg["NE"]
    DC = D // 128
    NP = H // 2
    NB = NSEQ // MB
    MC = MB // 128
    NT = QH // 128
    NQW = QH // MB
    QB = min(512, QH)
    NQB = QH // QB
    OH = min(512, D)
    NHALF = D // OH
    MBH = MB // OH if MB >= OH else 1
    assert HE == 64 and D % 128 == 0 and MB % 128 == 0

    nc = bass.Bass(trn_type="TRN2", target_bir_lowering=False)

    xin = nc.dram_tensor("xin", [NSEQ, D], F32, kind="ExternalInput").ap()
    wq = nc.dram_tensor("wq", [D, D], F32, kind="ExternalInput").ap()
    wk = nc.dram_tensor("wk", [D, D], F32, kind="ExternalInput").ap()
    wv = nc.dram_tensor("wv", [D, D], F32, kind="ExternalInput").ap()
    wp = nc.dram_tensor("wp", [D, D], F32, kind="ExternalInput").ap()
    bp = nc.dram_tensor("bp", [1, D], F32, kind="ExternalInput").ap()
    gw = nc.dram_tensor("gw", [D, NE], F32, kind="ExternalInput").ap()
    gb = nc.dram_tensor("gb", [1, NE], F32, kind="ExternalInput").ap()
    we = nc.dram_tensor("we", [NE, D, D], F32, kind="ExternalInput").ap()
    be = nc.dram_tensor("be", [NE, D], F32, kind="ExternalInput").ap()
    idn = nc.dram_tensor("idn", [128, 128], F32, kind="ExternalInput").ap()
    e2d = nc.dram_tensor("e2d", [2, 128], F32, kind="ExternalInput").ap()
    y = nc.dram_tensor("y", [QH, D], F32, kind="ExternalOutput").ap()

    p = Prog()
    chT = p.chan("chT", "T")
    chA = p.chan("chA", "A")
    chV = p.chan("chV", "V")
    chX0 = p.chan("chX0", "S", dma=True)
    chX1 = p.chan("chX1", "S", dma=True)
    chW0 = p.chan("chW0", "S", dma=True)
    chW1 = p.chan("chW1", "S", dma=True)
    chM = p.chan("chM", "S", dma=True)
    chE0 = p.chan("chE0", "S", dma=True)
    chE1 = p.chan("chE1", "S", dma=True)
    chG = p.chan("chG", "G", dma=True)
    chG1 = p.chan("chG1", "G", dma=True)
    chGb = p.chan("chGb", "G", dma=True)
    chC = [p.chan("chC%d" % i, "S", dma=True) for i in range(4)]
    chGc = p.chan("chGc", "G")
    chO = p.chan("chO", "S", dma=True)
    chX = (chX0, chX1)
    chW = (chW0, chW1)
    chE = (chE0, chE1)

    es = ExitStack()
    with es:
        def sb(name, shape, dt=F32, side=None):
            kw = {} if side is None else {"side": side}
            return es.enter_context(nc.sbuf_tensor(name, shape, dt, **kw))

        sid = sb("sid", [128, 128])
        sidb = sb("sidb", [128, 128], BF16)
        cns = sb("cns", [128, 8])
        gstg = sb("gstg", [128, DC * NE])
        gwh = sb("gwh", [128, DC * NE], BF16)
        gwl = sb("gwl", [128, DC * NE], BF16)
        sTb = sb("sTb", [128, 128])
        s_all = sb("s_all", [128, NT * NE])
        gsc = sb("gsc", [128, 6 * NE + 8])
        stats = sb("stats", [128, 16])
        gbsb = sb("gbsb", [1, NE])
        gbr = sb("gbr", [128, NE])
        ones1 = sb("ones1", [1, 128])
        E2 = sb("E2", [2, 128])

        def ps(name, w=1024):
            return es.enter_context(nc.psum_tensor(name, [128, w], F32))

        psW = [ps("psW0"), ps("psW1"), ps("psT")]
        psPV = [ps("psPVa", 512), ps("psPVb", 512)]
        ps_free = {}

        def ps_wait(eng, t, halves=(0, 1)):
            for hf in halves:
                p.waits(eng, ps_free.get((t.name, hf), ()))

        def ps_read(t, ev, halves=(0, 1)):
            for hf in halves:
                ps_free[(t.name, hf)] = [ev]

        def cvt(dst_hi, dst_lo, src_f32, n, evsrc, extra_wait=None):
            p.wait("A", evsrc)
            if extra_wait is not None:
                p.wait("A", extra_wait)
            p.op("A", lambda e: e.copy(dst_hi, src_f32))
            ev_h = p.inc(chA)
            p.wait("V", ev_h)
            p.op("V", lambda e: e.tensor_tensor(dst_lo, src_f32, dst_hi,
                                                ALU.subtract))
            ev_l = p.inc(chV)
            return ev_h, ev_l

        def mm3(out_ap, lh, ll, rh, rl, start, stop):
            p.op("T", lambda e: e.matmul(out_ap, lh, rh, start=start, stop=False))
            p.op("T", lambda e: e.matmul(out_ap, lh, rl, start=False, stop=False))
            p.op("T", lambda e: e.matmul(out_ap, ll, rh, start=False, stop=stop))

        p.op("S", lambda e: e.dma_start(sid[:, :], idn[:, :]))
        evI = p.inc(chM)
        p.op("S", lambda e: e.dma_start(E2[:, :], e2d[:, :]))
        p.inc(chM)
        p.op("S", lambda e: e.dma_start(
            gstg[:, :].rearrange("p (c g) -> p c g", c=DC),
            gw[:, :].rearrange("(c dp) g -> dp c g", dp=128)))
        p.inc(chM)
        p.op("S", lambda e: e.dma_start(gbsb[:, :], gb[:, :]))
        p.inc(chM)
        evM0 = p.now(chM)

        p.op("V", lambda e: e.memset(cns[:, 0:1], -16.0))
        p.op("V", lambda e: e.memset(cns[:, 1:2], 1e-5))
        p.op("V", lambda e: e.memset(ones1[:, :], 1.0))
        p.op("V", lambda e: e.memset(den[:, :], 0.0))
        evZ = p.inc(chV)

        cvt(gwh[:, :], gwl[:, :], gstg[:, :], DC * NE, evM0)
        p.wait("A", evI)
        p.op("A", lambda e: e.copy(sidb[:, :], sid[:, :]))
        evIb = p.inc(chA)

        p.wait("T", evM0)
        p.wait("T", evZ)
        p.op("T", lambda e: e.matmul(psW[1][:, 0:NE], ones1[0:1, :],
                                     gbsb[0:1, :], start=True, stop=True))
        evT = p.inc(chT)
        p.wait("A", evT)
        p.op("A", lambda e: e.copy(gbr[:, :], psW[1][:, 0:NE]))
        evA_gbr = p.inc(chA)
        ps_read(psW[1], evA_gbr)

        lnsc_free = {}

        def ln_tile(xap, dst_writer, lnsc, x_ev):
            p.wait("V", x_ev)
            p.wait("V", lnsc_free.get(lnsc.name))
            p.op("V", lambda e: e.tensor_mul(lnsc[:, :], xap, xap))
            p.op("V", lambda e: e.reduce_sum(stats[:, 0:1], xap, axis=AX))
            p.op("V", lambda e: e.reduce_sum(stats[:, 1:2], lnsc[:, :], axis=AX))
            p.op("V", lambda e: e.tensor_scalar(stats[:, 2:3], stats[:, 0:1],
                                                1.0 / D, None, ALU.mult))
            p.op("V", lambda e: e.tensor_mul(stats[:, 3:4], stats[:, 2:3],
                                             stats[:, 2:3]))
            p.op("V", lambda e: e.scalar_tensor_tensor(
                stats[:, 4:5], stats[:, 1:2], 1.0 / D, stats[:, 3:4],
                ALU.mult, ALU.subtract))
            p.wait("V", evZ)
            p.op("V", lambda e: e.tensor_add(stats[:, 7:8], stats[:, 4:5],
                                             cns[:, 1:2]))
            p.op("V", lambda e: e.reciprocal(stats[:, 8:9], stats[:, 7:8]))
            evv = p.inc(chV)
            p.wait("A", evv)
            p.op("A", lambda e: e.activation(stats[:, 5:6], stats[:, 8:9],
                                             AF.Sqrt))
            eva = p.inc(chA)
            p.wait("V", eva)
            p.op("V", lambda e: e.scalar_tensor_tensor(
                stats[:, 6:7], stats[:, 2:3], -1.0, stats[:, 5:6],
                ALU.mult, ALU.mult))
            p.op("V", lambda e: e.tensor_scalar(lnsc[:, :], xap, stats[:, 5:6],
                                                stats[:, 6:7], ALU.mult, ALU.add))
            evln = p.inc(chV)
            p.wait("T", evln)
            p.wait("T", evI)
            for r0 in range(0, DC, 4):
                hf = (r0 // 4) % 2
                pst = psW[2]
                chunks = list(range(r0, min(r0 + 4, DC)))
                ps_wait("T", pst, (hf,))
                for j, c in enumerate(chunks):
                    p.op("T", lambda e, c=c, j=j, pst=pst, hf=hf: e.transpose(
                        pst[:, hf * 512 + j * 128:hf * 512 + (j + 1) * 128],
                        lnsc[:, c * 128:(c + 1) * 128], sid[:, :]))
                evt = p.inc(chT)
                evr = dst_writer(chunks[0], pst, len(chunks), evt, hf)
                ps_read(pst, evr, (hf,))
            lnsc_free[lnsc.name] = p.now(chT)
            return evln

        HB = QB // 2
        den = sb("den", [128, HB], side="right")
        rdbuf = sb("rdbuf", [128, 2 * QB], side="right")

        with ExitStack() as es1:
            big1 = es1.enter_context(nc.sbuf_tensor("big1", [128, NP * QH], F32))
            p.op("V", lambda e: e.memset(big1[:, :], 0.0))
            evZ2 = p.inc(chV)

            with ExitStack() as esab:
                def sba(name, shape, dt=F32):
                    return esab.enter_context(nc.sbuf_tensor(name, shape, dt))

                xst = sba("xst", [128, D])
                q_h = sba("q_h", [128, NP * QH], BF16)
                q_l = sba("q_l", [128, NP * QH], BF16)
                ln_h = sba("ln_h", [128, DC * MB], BF16)
                ln_l = sba("ln_l", [128, DC * MB], BF16)
                wstg = sba("wstg", [128, DC * 128])
                wkh = sba("wkh", [128, DC * 128], BF16)
                wkl = sba("wkl", [128, DC * 128], BF16)
                wvh = sba("wvh", [128, DC * 128], BF16)
                wvl = sba("wvl", [128, DC * 128], BF16)
                kh2 = [[sba("kh%d%d" % (g, s), [128, MB], BF16)
                        for s in range(2)] for g in range(2)]
                kl2 = [[sba("kl%d%d" % (g, s), [128, MB], BF16)
                        for s in range(2)] for g in range(2)]
                vstg = sba("vstg", [128, MB])
                VS = 68
                vah2 = [[sba("vah%d%d" % (g, s), [128, MC * VS], BF16)
                         for s in range(2)] for g in range(2)]
                val2 = [[sba("val%d%d" % (g, s), [128, MC * VS], BF16)
                         for s in range(2)] for g in range(2)]
                ptf = [sba("ptf%d" % i, [128, QB]) for i in range(2)]
                pth = [[sba("pth%d%d" % (s, j), [128, QB], BF16)
                        for j in range(2)] for s in range(2)]
                ptl = [[sba("ptl%d%d" % (s, j), [128, QB], BF16)
                        for j in range(2)] for s in range(2)]
                stage = [sba("stg%d" % i, [128, 2 * QB]) for i in range(2)]
                lnsc = sba("lnsc_ab", [128, D])

                for g in range(2):
                    for s in range(2):
                        p.op("V", lambda e, g=g, s=s: e.memset(
                            vah2[g][s][:, :], 1.0))
                        p.op("V", lambda e, g=g, s=s: e.memset(
                            val2[g][s][:, :], 0.0))
                        rows0 = slice(64, 128) if s == 0 else slice(0, 64)
                        p.op("V", lambda e, g=g, s=s, rows0=rows0: e.memset(
                            kh2[g][s][rows0, :], 0.0))
                        p.op("V", lambda e, g=g, s=s, rows0=rows0: e.memset(
                            kl2[g][s][rows0, :], 0.0))
                ev_va_init = p.inc(chV)

                xst_free = [None, None]
                wstg_free = [None]
                stage_free = [[], []]
                lnwin_free = []
                q_ready = None
                ptf_free = [None, None]
                pthl_free = [[None, None], [None, None]]

                def load_x_tile(r0, slot_i):
                    p.wait("S", xst_free[0])
                    p.op("S", lambda e, r0=r0: e.dma_start(
                        xst[:, :], xin[r0:r0 + 128, :]))
                    return p.inc(chX[0])

                def ln_window_cbs(row0, nw_tiles):
                    evs = []

                    def mk(tt):
                        def cb():
                            if tt == 0 and lnwin_free:
                                p.wait("A", lnwin_free[-1])
                                p.wait("V", lnwin_free[-1])
                                del lnwin_free[:]
                            evx = load_x_tile(row0 + tt * 128, 0)
                            xap = xst[:, :]

                            def wr(c0, pst, n, evt, hf, tt=tt):
                                hi = bass.AP(ln_h, c0 * MB + tt * 128,
                                             [[DC * MB, 128], [MB, n], [1, 128]])
                                lo = bass.AP(ln_l, c0 * MB + tt * 128,
                                             [[DC * MB, 128], [MB, n], [1, 128]])
                                _, evl = cvt(hi, lo,
                                             pst[:, hf * 512:hf * 512 + n * 128],
                                             n * 128, evt)
                                return evl

                            evln = ln_tile(xap, wr, lnsc, evx)
                            xst_free[0] = evln
                            evs.append(p.now(chV))
                        return cb

                    return evs, [mk(tt) for tt in range(nw_tiles)]

                def ln_window(row0, nw_tiles):
                    evs, cbs = ln_window_cbs(row0, nw_tiles)
                    for cb in cbs:
                        cb()
                    return evs

                def load_w_slice(dram, pcol, slot_i):
                    p.wait("S", wstg_free[0])
                    src = dram[:, pcol * 128:(pcol + 1) * 128].rearrange(
                        "(c dp) e -> dp c e", dp=128)
                    dstv = wstg[:, :].rearrange("p (c e) -> p c e", c=DC)
                    p.op("S", lambda e, src=src, dstv=dstv: e.dma_start(dstv, src))
                    return p.inc(chW[slot_i])

                def gen_mm(pst, wh, wl, w_evl, lnwin_evs, base=0):
                    ps_wait("T", pst, (base // 512,) if MB <= 512 else (0, 1))
                    p.wait("T", w_evl)
                    p.waits("T", lnwin_evs)
                    for c in range(DC):
                        for h in range(MBH):
                            o0 = h * OH
                            w0 = min(OH, MB)
                            mm3(pst[:, base + o0:base + o0 + w0],
                                wh[:, c * 128:(c + 1) * 128],
                                wl[:, c * 128:(c + 1) * 128],
                                bass.AP(ln_h, c * MB + o0,
                                        [[DC * MB, 128], [1, w0]]),
                                bass.AP(ln_l, c * MB + o0,
                                        [[DC * MB, 128], [1, w0]]),
                                start=(c == 0), stop=(c == DC - 1))
                    ev = p.inc(chT)
                    lnwin_free.append(ev)
                    return ev

                for w in range(NQW):
                    lnev = ln_window(w * MB, MB // 128)
                    for pi in range(NP):
                        slot_i = pi % 2
                        wev = load_w_slice(wq, pi, slot_i)
                        wh, wl = (wkh, wkl) if slot_i == 0 else (wvh, wvl)
                        _, wevl = cvt(wh[:, :], wl[:, :], wstg[:, :], DC * 128,
                                      wev)
                        wstg_free[0] = wevl
                        pst = psW[slot_i]
                        evg = gen_mm(pst, wh, wl, wevl, lnev)
                        _, evql = cvt(
                            q_h[:, pi * QH + w * MB: pi * QH + w * MB + MB],
                            q_l[:, pi * QH + w * MB: pi * QH + w * MB + MB],
                            pst[:, 0:MB], MB, evg)
                        ps_read(pst, evql, (0,) if MB <= 512 else (0, 1))
                q_ready = p.now(chV)

                pair_acc_ev = [[None, None] for _ in range(NP)]
                evkl_set = [None, None]
                ev_va_set = [[None, None], [None, None]]
                kread_free = [[], []]
                psT_tr_free = [[], []]

                def make_gen_cbs(mb, pi, seti, lnev):
                    st = {}
                    kh, kl = kh2[seti], kl2[seti]
                    vah, val = vah2[seti], val2[seti]

                    def cb_kload():
                        st["wkev"] = load_w_slice(wk, pi, 0)

                    def cb_kcvt():
                        _, st["wkevl"] = cvt(wkh[:, :], wkl[:, :], wstg[:, :],
                                             DC * 128, st["wkev"])
                        wstg_free[0] = st["wkevl"]

                    def cb_kgen():
                        st["evk"] = gen_mm(psW[2], wkh, wkl, st["wkevl"],
                                           lnev, base=0)

                    def cb_kevac():
                        p.waits("A", kread_free[seti])
                        p.waits("V", kread_free[seti])
                        _, ev1 = cvt(kh[0][0:64, :], kl[0][0:64, :],
                                     psW[2][0:64, 0:MB], MB, st["evk"])
                        _, evkl = cvt(kh[1][64:128, :], kl[1][64:128, :],
                                      psW[2][64:128, 0:MB], MB, st["evk"])
                        ps_read(psW[2], evkl, (0,))
                        evkl_set[seti] = evkl

                    def cb_vload():
                        st["wvev"] = load_w_slice(wv, pi, 1)

                    def cb_vcvt():
                        _, st["wvevl"] = cvt(wvh[:, :], wvl[:, :], wstg[:, :],
                                             DC * 128, st["wvev"])
                        wstg_free[0] = st["wvevl"]

                    def cb_vgen():
                        st["evv"] = gen_mm(psW[2], wvh, wvl, st["wvevl"],
                                           lnev, base=512)

                    def cb_vevac():
                        p.wait("A", st["evv"])
                        p.op("A", lambda e: e.copy(vstg[:, :],
                                                   psW[2][:, 512:512 + MB]))
                        st["evvs"] = p.inc(chA)
                        ps_read(psW[2], st["evvs"], (1,))

                    def mk_vtrans(s):
                        def cb():
                            off = s * 512
                            ps_wait("T", psW[2], (s,))
                            p.wait("T", st["evvs"])
                            p.waits("T", psT_tr_free[s])
                            for c in range(MC):
                                p.op("T", lambda e, s=s, c=c, off=off:
                                     e.transpose(
                                         psW[2][:, off + c * 64:
                                                off + (c + 1) * 64],
                                         vstg[s * 64:(s + 1) * 64,
                                              c * 128:(c + 1) * 128],
                                         sid[s * 64:(s + 1) * 64,
                                             s * 64:(s + 1) * 64]))
                            evt = p.inc(chT)
                            p.waits("A", kread_free[seti])
                            p.waits("V", kread_free[seti])
                            hiap = bass.AP(vah[s], 0,
                                           [[MC * VS, 128], [VS, MC], [1, 64]])
                            loap = bass.AP(val[s], 0,
                                           [[MC * VS, 128], [VS, MC], [1, 64]])
                            _, eva = cvt(hiap, loap,
                                         psW[2][:, off:off + MC * 64],
                                         MC * 64, evt)
                            ps_read(psW[2], eva, (s,))
                            ev_va_set[seti][s] = eva
                        return cb

                    return [cb_kload, cb_kcvt, cb_kgen, cb_kevac,
                            cb_vload, cb_vcvt, cb_vgen, cb_vevac,
                            mk_vtrans(0), mk_vtrans(1)]

                units = [(mb, pi) for mb in range(NB) for pi in range(NP)]
                lnev_cur = None
                lnev_next = None
                ln_next_cbs = []
                for ui, (mb, pi) in enumerate(units):
                    seti = ui % 2
                    if pi == 0:
                        if mb == 0 or lnev_next is None:
                            lnev_cur = ln_window(mb * MB, MB // 128)
                            for cb in make_gen_cbs(mb, 0, seti, lnev_cur):
                                cb()
                        else:
                            lnev_cur = lnev_next
                    evkl = evkl_set[seti]
                    ev_va = ev_va_set[seti]
                    k_h, k_l = kh2[seti], kl2[seti]
                    vah, val = vah2[seti], val2[seti]
                    if pi + 1 < NP:
                        gen_cbs = make_gen_cbs(mb, pi + 1, (ui + 1) % 2,
                                               lnev_cur)
                        if pi + 2 == NP and mb + 1 < NB and NP >= 2:
                            lnev_next, cbs4 = ln_window_cbs((mb + 1) * MB,
                                                            MB // 128)
                            nh = len(cbs4) // 2
                            gen_cbs = gen_cbs + cbs4[:nh]
                            ln_next_cbs = cbs4[nh:]
                    elif mb + 1 < NB and lnev_next is not None:
                        gen_cbs = ln_next_cbs + make_gen_cbs(mb + 1, 0,
                                                             (ui + 1) % 2,
                                                             lnev_next)
                        ln_next_cbs = []
                    else:
                        gen_cbs = []

                    pv_done = [None, None]

                    def emit_S_chain(i, qb, c, s):
                        hs = qb * MC + c
                        par = hs % 2
                        gpar = i % 2
                        off = par * 512
                        pss = psW[s]
                        q0 = pi * QH + qb * QB
                        ps_wait("T", pss, (par,))
                        p.wait("T", evkl)
                        p.wait("T", q_ready)
                        mm3(pss[:, off:off + QB],
                            k_h[s][:, c * 128:(c + 1) * 128],
                            k_l[s][:, c * 128:(c + 1) * 128],
                            q_h[:, q0:q0 + QB],
                            q_l[:, q0:q0 + QB],
                            start=True, stop=True)
                        evs_mm = p.inc(chT)
                        p.wait("A", evs_mm)
                        p.wait("A", evZ)
                        p.wait("A", ptf_free[gpar])
                        p.op("A", lambda e, gpar=gpar, pss=pss, off=off:
                             e.activation(ptf[gpar][:, :],
                                          pss[:, off:off + QB], AF.Exp,
                                          bias=cns[:, 0:1], scale=0.125))
                        ev_exp = p.inc(chA)
                        ps_read(pss, ev_exp, (par,))
                        p.wait("V", ev_exp)
                        p.wait("V", pthl_free[s][par])
                        p.op("V", lambda e, s=s, par=par, gpar=gpar:
                             e.tensor_copy(pth[s][par][:, :],
                                           ptf[gpar][:, :]))
                        p.op("V", lambda e, s=s, par=par, gpar=gpar:
                             e.tensor_tensor(ptl[s][par][:, :],
                                             ptf[gpar][:, :],
                                             pth[s][par][:, :],
                                             ALU.subtract))
                        ev_l = p.inc(chV)
                        ptf_free[gpar] = ev_l
                        return ev_l, par

                    def flush_slot(qb, s, ev_pv):
                        sg = qb % 2
                        chGs = (chG, chG1)[sg]
                        ce = "A" if s == 0 else "V"
                        p.waits(ce, stage_free[sg])
                        p.wait(ce, ev_pv)
                        if s == 0:
                            p.op("A", lambda e, sg=sg: e.copy(
                                stage[sg][0:65, 0:QB], psPV[0][0:65, 0:QB]))
                            evc = p.inc(chA)
                        else:
                            p.op("V", lambda e, sg=sg: e.tensor_copy(
                                stage[sg][0:65, QB:2 * QB], psPV[1][0:65, 0:QB]))
                            evc = p.inc(chV)
                        ps_read(psPV[s], evc, (0,))
                        p.wait("G", evc)
                        p.wait("G", ev_va_init)
                        p.wait("G", evZ2)
                        r0, r1 = (0, 64) if s == 0 else (64, 128)
                        p.op("G", lambda e, pi=pi, qb=qb, sg=sg, s=s,
                             r0=r0, r1=r1: e.dma_start(
                                 big1[r0:r1, pi * QH + qb * QB:
                                      pi * QH + (qb + 1) * QB],
                                 stage[sg][0:64, s * QB:(s + 1) * QB],
                                 accum_op=ALU.add))
                        p.inc(chGs)
                        hh0 = pi if s == 0 else pi + NP
                        p.op("G", lambda e, qb=qb, sg=sg, s=s, hh0=hh0:
                             e.dma_start(
                                 bass.AP(den, (hh0 * NQB + qb) * 2 * HB,
                                         [[HB, 2], [1, HB]]),
                                 stage[sg][64:65, s * QB:(s + 1) * QB],
                                 accum_op=ALU.add))
                        evg = p.inc(chGs)
                        if s == 1:
                            stage_free[sg] = [evg]
                            pair_acc_ev[pi][sg] = evg

                    def emit_PV(st2, chain_ev, par):
                        qb, c, s = st2
                        if c == 0:
                            ps_wait("T", psPV[s], (0,))
                        p.wait("T", chain_ev)
                        p.wait("T", ev_va[s])
                        mm3(psPV[s][0:65, 0:QB],
                            vah[s][:, c * VS:c * VS + 65],
                            val[s][:, c * VS:c * VS + 65],
                            pth[s][par][:, :],
                            ptl[s][par][:, :],
                            start=(c == 0), stop=(c == MC - 1))
                        ev_pv = p.inc(chT)
                        pthl_free[s][par] = ev_pv
                        if c == MC - 1:
                            pv_done[s] = ev_pv
                            flush_slot(qb, s, ev_pv)

                    steps = [(qb, c, s) for qb in range(NQB)
                             for c in range(MC) for s in range(2)]
                    pend = []
                    for i, st2 in enumerate(steps):
                        ch_ev, par = emit_S_chain(i, *st2)
                        pend.append((st2, ch_ev, par))
                        if len(pend) > 3:
                            emit_PV(*pend.pop(0))
                        if gen_cbs and i >= 2 and (i - 2) % 3 == 0:
                            gen_cbs.pop(0)()
                    while gen_cbs:
                        gen_cbs.pop(0)()
                    while pend:
                        emit_PV(*pend.pop(0))
                    kread_free[seti] = [p.now(chT)]
                    psT_tr_free[0] = [p.now(chT)]
                    psT_tr_free[1] = [p.now(chT)]

                bar_pre = [p.now(chT), p.now(chA), p.now(chV), p.now(chG),
                           p.now(chG1)]
                ev_mult_qb = [None] * NQB
                slot_free = [None] * 2
                sli = 0
                for qb in range(NQB):
                    for pi in range(NP):
                        sl = sli % 2
                        hf = sli % 2
                        sli += 1
                        c0 = sl * QB
                        p.waits("S", pair_acc_ev[pi])
                        p.wait("S", slot_free[sl])
                        p.op("S", lambda e, pi=pi, qb=qb, c0=c0: e.dma_start(
                            rdbuf[0:1, c0:c0 + QB],
                            bass.AP(den, (pi * NQB + qb) * 2 * HB,
                                    [[HB, 2], [1, HB]])))
                        p.inc(chC[sl])
                        p.op("S", lambda e, pi=pi, qb=qb, c0=c0: e.dma_start(
                            rdbuf[1:2, c0:c0 + QB],
                            bass.AP(den, ((pi + NP) * NQB + qb) * 2 * HB,
                                    [[HB, 2], [1, HB]])))
                        evd = p.inc(chC[sl])
                        p.wait("V", evd)
                        p.op("V", lambda e, c0=c0: e.reciprocal(
                            rdbuf[0:2, c0:c0 + QB], rdbuf[0:2, c0:c0 + QB]))
                        ev_rc = p.inc(chV)
                        ps_wait("T", psW[2], (hf,))
                        p.wait("T", ev_rc)
                        p.wait("T", evM0)
                        p.op("T", lambda e, c0=c0, hf=hf: e.matmul(
                            psW[2][:, hf * QB:(hf + 1) * QB], E2[0:2, :],
                            rdbuf[0:2, c0:c0 + QB], start=True, stop=True))
                        ev_rep = p.inc(chT)
                        slot_free[sl] = ev_rep
                        p.wait("V", ev_rep)
                        p.op("V", lambda e, pi=pi, qb=qb, hf=hf: e.tensor_mul(
                            big1[:, pi * QH + qb * QB: pi * QH + (qb + 1) * QB],
                            big1[:, pi * QH + qb * QB: pi * QH + (qb + 1) * QB],
                            psW[2][:, hf * QB:(hf + 1) * QB]))
                        ev_mult = p.inc(chV)
                        ps_read(psW[2], ev_mult, (hf,))
                    ev_mult_qb[qb] = ev_mult
                feats_ready = ev_mult

            out_acc = sb("out_acc", [128, NT * D], side="right")
            with ExitStack() as espd:
                def sbp(name, shape, dt=F32):
                    return espd.enter_context(nc.sbuf_tensor(name, shape, dt))

                wph = sbp("wph", [128, DC * D], BF16)
                wpl = sbp("wpl", [128, DC * D], BF16)
                wpstg = sbp("wpstg", [128, 2 * D])
                bpr = sbp("bpr", [128, D])
                bpsb = sbp("bpsb", [1, D])
                xst2 = sbp("xst2", [128, 2 * D])
                fch = [sbp("fch%d" % i, [128, 128], BF16) for i in range(2)]
                fcl = [sbp("fcl%d" % i, [128, 128], BF16) for i in range(2)]

                p.waits("S", bar_pre)
                wps_free = [None, None]
                for c in range(DC):
                    sl = c % 2
                    p.wait("S", wps_free[sl])
                    p.op("S", lambda e, c=c, sl=sl: e.dma_start(
                        wpstg[:, sl * D:(sl + 1) * D],
                        wp[c * 128:(c + 1) * 128, :]))
                    evw = p.inc(chM if sl == 0 else chW1)
                    _, evwl = cvt(wph[:, c * D:(c + 1) * D],
                                  wpl[:, c * D:(c + 1) * D],
                                  wpstg[:, sl * D:(sl + 1) * D], D, evw)
                    wps_free[sl] = evwl
                ev_wp = evwl
                p.op("S", lambda e: e.dma_start(bpsb[:, :], bp[:, :]))
                evbp = p.inc(chW0)
                p.wait("T", evbp)
                p.wait("T", evZ)
                for hh in range(NHALF):
                    pst = psW[hh % 2]
                    ps_wait("T", pst)
                    p.op("T", lambda e, hh=hh, pst=pst: e.matmul(
                        pst[:, 0:OH], ones1[0:1, :],
                        bpsb[0:1, hh * OH:(hh + 1) * OH], start=True,
                        stop=True))
                    evt = p.inc(chT)
                    p.wait("A", evt)
                    p.op("A", lambda e, hh=hh, pst=pst: e.copy(
                        bpr[:, hh * OH:(hh + 1) * OH], pst[:, 0:OH]))
                    eva = p.inc(chA)
                    ps_read(pst, eva)
                ev_bpr = p.now(chA)

                xst2_free = [None, None]
                fc_free = [None, None]
                for t in range(NT):
                    pst = psW[t % 2]
                    ps_wait("T", pst)
                    p.wait("T", ev_wp)
                    for c in range(DC):
                        sl = c % 2
                        p.wait("A", fc_free[sl])
                        _, evf = cvt(fch[sl][:, :], fcl[sl][:, :],
                                     big1[:, c * QH + t * 128:
                                          c * QH + t * 128 + 128],
                                     128, ev_mult_qb[t // (QB // 128)])
                        p.wait("T", evf)
                        for hh in range(NHALF):
                            mm3(pst[:, hh * OH:(hh + 1) * OH],
                                fch[sl][:, :], fcl[sl][:, :],
                                wph[:, c * D + hh * OH: c * D + (hh + 1) * OH],
                                wpl[:, c * D + hh * OH: c * D + (hh + 1) * OH],
                                start=(c == 0), stop=(c == DC - 1))
                        evt = p.inc(chT)
                        fc_free[sl] = evt
                    ev_proj = p.now(chT)
                    sl2 = t % 2
                    p.wait("S", xst2_free[sl2])
                    p.op("S", lambda e, t=t, sl2=sl2: e.dma_start(
                        xst2[:, sl2 * D:(sl2 + 1) * D],
                        xin[t * 128:(t + 1) * 128, :]))
                    evx = p.inc(chX[sl2])
                    oa = out_acc[:, t * D:(t + 1) * D]
                    p.wait("V", ev_proj)
                    p.wait("V", ev_bpr)
                    p.op("V", lambda e, oa=oa, pst=pst: e.tensor_add(
                        oa, pst[0:128, 0:D], bpr[:, :]))
                    ev1 = p.inc(chV)
                    ps_read(pst, ev1)
                    p.wait("V", evx)
                    p.op("V", lambda e, oa=oa, sl2=sl2: e.tensor_add(
                        oa, oa, xst2[:, sl2 * D:(sl2 + 1) * D]))
                    ev_x2 = p.inc(chV)
                    xst2_free[sl2] = ev_x2
                x2_done_V = p.now(chV)
                x2_done_T = p.now(chT)

        with ExitStack() as es2:
            def sb2(name, shape, dt=F32):
                return es2.enter_context(nc.sbuf_tensor(name, shape, dt))

            ln2h = sb2("ln2h", [128, NP * QH], BF16)
            ln2l = sb2("ln2l", [128, NP * QH], BF16)
            besb = sb2("besb", [128, D])
            lnsc2 = sb2("lnsc_d", [128, D])

            p.waits("S", [x2_done_V, x2_done_T])
            p.wait("V", x2_done_T)
            p.wait("A", x2_done_T)
            p.op("S", lambda e: e.dma_start(besb[0:NE, :], be[:, :]))
            ev_be = p.inc(chM)

            espe = es2.enter_context(ExitStack())

            def sbe(name, shape, dt=F32):
                return espe.enter_context(nc.sbuf_tensor(name, shape, dt))

            estg = sbe("estg", [128, DC * OH])
            weh = [sbe("weh%d" % i, [128, DC * OH], BF16) for i in range(2)]
            wel = [sbe("wel%d" % i, [128, DC * OH], BF16) for i in range(2)]
            tasks = [(ei, hh) for ei in range(NE) for hh in range(NHALF)]
            web_free = [None, None]
            estg_free = [None]
            ev_we = [None, None]
            ev_ln2_t = [None] * NT

            def load_task(ti, sl):
                ei, hh = tasks[ti]
                p.wait("S", estg_free[0])
                src3 = we[ei, :, :].rearrange("(c dp) o -> dp c o", dp=128)
                src3 = src3[:, :, hh * OH:(hh + 1) * OH]
                p.op("S", lambda e, src3=src3: e.dma_start(
                    estg[:, :].rearrange("p (c o) -> p c o", c=DC), src3))
                evl = p.inc(chE[sl])
                p.waits("A", [web_free[sl]])
                p.waits("V", [web_free[sl]])
                _, evcl = cvt(weh[sl][:, :], wel[sl][:, :], estg[:, :],
                              DC * OH, evl)
                estg_free[0] = evcl
                ev_we[sl] = evcl

            def emit_task_tile(ti, sl, t, ev_ln2_dep):
                ei, hh = tasks[ti]
                pst = psW[t % 3]
                ps_wait("T", pst)
                p.wait("T", ev_we[sl])
                p.wait("T", ev_ln2_dep)
                for c in range(DC):
                    mm3(pst[:, 0:OH],
                        ln2h[:, c * QH + t * 128: c * QH + t * 128 + 128],
                        ln2l[:, c * QH + t * 128: c * QH + t * 128 + 128],
                        weh[sl][:, c * OH:(c + 1) * OH],
                        wel[sl][:, c * OH:(c + 1) * OH],
                        start=(c == 0), stop=(c == DC - 1))
                ev_y = p.inc(chT)
                p.wait("V", ev_y)
                p.op("V", lambda e, hh=hh, t=t, ei=ei, pst=pst:
                     e.scalar_tensor_tensor(
                         out_acc[:, t * D + hh * OH: t * D + (hh + 1) * OH],
                         pst[:, 0:OH],
                         s_all[:, t * NE + ei:t * NE + ei + 1],
                         out_acc[:, t * D + hh * OH: t * D + (hh + 1) * OH],
                         ALU.mult, ALU.add))
                ev_o = p.inc(chV)
                ps_read(pst, ev_o)
                if t == NT - 1:
                    web_free[sl] = p.now(chT)
                if ei == NE - 1:
                    p.wait("S", ev_o)
                    p.op("S", lambda e, t=t, hh=hh: e.dma_start(
                        y[t * 128:(t + 1) * 128, hh * OH:(hh + 1) * OH],
                        out_acc[:, t * D + hh * OH: t * D + (hh + 1) * OH]))
                    p.inc(chO)

            load_task(0, 0)
            for t in range(NT):
                oa = out_acc[:, t * D:(t + 1) * D]

                def wr2(c0, pst, n, evt, hf, t=t):
                    hi = bass.AP(ln2h, c0 * QH + t * 128,
                                 [[NP * QH, 128], [QH, n], [1, 128]])
                    lo = bass.AP(ln2l, c0 * QH + t * 128,
                                 [[NP * QH, 128], [QH, n], [1, 128]])
                    _, evl = cvt(hi, lo,
                                 pst[:, hf * 512:hf * 512 + n * 128],
                                 n * 128, evt)
                    return evl

                ln_tile(oa, wr2, lnsc2, x2_done_V)
                ev_ln2 = p.now(chV)
                pst = psW[1]
                ps_wait("T", pst)
                p.wait("T", ev_ln2)
                p.wait("T", p.now(chA))
                for c in range(DC):
                    mm3(pst[:, 0:NE],
                        ln2h[:, c * QH + t * 128: c * QH + t * 128 + 128],
                        ln2l[:, c * QH + t * 128: c * QH + t * 128 + 128],
                        gwh[:, c * NE:(c + 1) * NE],
                        gwl[:, c * NE:(c + 1) * NE],
                        start=(c == 0), stop=(c == DC - 1))
                ev_gmm = p.inc(chT)
                z0, e0, g0, sl0, w0, sc0 = 0, NE, 2 * NE, 3 * NE, 4 * NE, 6 * NE
                p.wait("V", ev_gmm)
                p.wait("V", evA_gbr)
                p.op("V", lambda e, pst=pst: e.tensor_add(
                    gsc[:, z0:z0 + NE], pst[:, 0:NE], gbr[:, :]))
                ev_z = p.inc(chV)
                ps_read(pst, ev_z)
                p.op("V", lambda e: e.reduce_max(gsc[:, sc0:sc0 + 1],
                                                 gsc[:, z0:z0 + NE], axis=AX,
                                                 negate=True))
                ev_nm = p.inc(chV)
                p.wait("A", ev_nm)
                p.op("A", lambda e: e.activation(gsc[:, e0:e0 + NE],
                                                 gsc[:, z0:z0 + NE], AF.Exp,
                                                 bias=gsc[:, sc0:sc0 + 1],
                                                 scale=1.0))
                ev_ge = p.inc(chA)
                p.wait("V", ev_ge)
                p.op("V", lambda e: e.reduce_sum(gsc[:, sc0 + 1:sc0 + 2],
                                                 gsc[:, e0:e0 + NE], axis=AX))
                p.op("V", lambda e: e.reciprocal(gsc[:, sc0 + 2:sc0 + 3],
                                                 gsc[:, sc0 + 1:sc0 + 2]))
                p.op("V", lambda e: e.tensor_scalar(gsc[:, g0:g0 + NE],
                                                    gsc[:, e0:e0 + NE],
                                                    gsc[:, sc0 + 2:sc0 + 3],
                                                    None, ALU.mult))
                p.op("V", lambda e: e.tensor_single_scalar(
                    gsc[:, sl0:sl0 + NE], gsc[:, g0:g0 + NE], 0.1, ALU.is_ge))
                p.op("V", lambda e: e.tensor_mul(gsc[:, w0:w0 + NE],
                                                 gsc[:, g0:g0 + NE],
                                                 gsc[:, sl0:sl0 + NE]))
                p.op("V", lambda e: e.reduce_sum(gsc[:, sc0 + 3:sc0 + 4],
                                                 gsc[:, w0:w0 + NE], axis=AX))
                p.op("V", lambda e: e.reciprocal(gsc[:, sc0 + 4:sc0 + 5],
                                                 gsc[:, sc0 + 3:sc0 + 4]))
                p.op("V", lambda e, t=t: e.tensor_scalar(
                    s_all[:, t * NE:(t + 1) * NE], gsc[:, w0:w0 + NE],
                    gsc[:, sc0 + 4:sc0 + 5], None, ALU.mult))
                ev_s = p.inc(chV)
                ps_wait("T", psW[2], (0,))
                p.wait("T", ev_s)
                p.op("T", lambda e, t=t: e.transpose(
                    psW[2][0:NE, 0:128], s_all[:, t * NE:(t + 1) * NE],
                    sid[:, :]))
                ev_st = p.inc(chT)
                p.wait("A", ev_st)
                p.op("A", lambda e: e.copy(sTb[0:NE, :], psW[2][0:NE, 0:128]))
                ev_stc = p.inc(chA)
                ps_read(psW[2], ev_stc)
                p.wait("T", ev_stc)
                p.wait("T", ev_be)
                for hh in range(NHALF):
                    pst2 = psPV[hh % 2]
                    ps_wait("T", pst2, (0,))
                    p.op("T", lambda e, hh=hh, pst2=pst2: e.matmul(
                        pst2[:, 0:OH], sTb[0:NE, :],
                        besb[0:NE, hh * OH:(hh + 1) * OH],
                        start=True, stop=True))
                    evb = p.inc(chT)
                    p.wait("V", evb)
                    p.op("V", lambda e, hh=hh, t=t, pst2=pst2: e.tensor_add(
                        out_acc[:, t * D + hh * OH: t * D + (hh + 1) * OH],
                        out_acc[:, t * D + hh * OH: t * D + (hh + 1) * OH],
                        pst2[:, 0:OH]))
                    evv = p.inc(chV)
                    ps_read(pst2, evv, (0,))
                ev_ln2_t[t] = ev_ln2
                emit_task_tile(0, 0, t, ev_ln2)
            ln2_all = p.now(chV)

            load_task(1, 1)
            for ti in range(1, len(tasks)):
                sl = ti % 2
                if ti + 1 < len(tasks):
                    load_task(ti + 1, (ti + 1) % 2)
                for t in range(NT):
                    emit_task_tile(ti, sl, t, ln2_all)

        ev_out = p.now(chO)

        for c in p.chans:
            c.sem = es.enter_context(nc.semaphore(c.name))
        block = es.enter_context(nc.Block())
        eng_map = {"T": block.tensor, "A": block.scalar, "V": block.vector,
                   "S": block.sync, "G": block.gpsimd}

        def runner(eng_name):
            def run(engobj):
                last = None
                for it in p.items[eng_name]:
                    if it[0] == "op":
                        r = it[1](engobj)
                        if r is not None:
                            last = r
                    elif it[0] == "inc":
                        last.then_inc(it[1].sem, it[1].step)
                    elif it[0] == "drain":
                        engobj.drain()
                    else:
                        engobj.wait_ge(it[1].sem, it[2])
                if eng_name == "S":
                    engobj.wait_ge(ev_out[0].sem, ev_out[1])
            return run

        for name, deco in eng_map.items():
            deco(runner(name))
    return nc


def _prep_inputs(cfg, L0_attn_query, Wq, Wk, Wv, Wp, bp, gate_w, gate_b,
                 expert_w, expert_b):
    D, H, NSEQ, QH, NE = (cfg["D"], cfg["H"], cfg["NSEQ"], cfg["QH"], cfg["NE"])
    NP = H // 2
    x = np.ascontiguousarray(np.asarray(L0_attn_query, dtype=np.float32))
    B = x.shape[0]
    head_order = []
    for pi in range(NP):
        head_order += [pi, pi + NP]
    wq_h = np.ascontiguousarray(
        np.asarray(Wq, np.float32).transpose(1, 0, 2)[:, head_order, :].reshape(D, D))
    wk_h = np.ascontiguousarray(
        np.asarray(Wk, np.float32).transpose(1, 0, 2)[:, head_order, :].reshape(D, D))
    wv_h = np.ascontiguousarray(
        np.asarray(Wv, np.float32).transpose(1, 0, 2)[:, head_order, :].reshape(D, D))
    idx = np.empty(D, dtype=np.int64)
    for pi in range(NP):
        for s in range(2):
            h = pi + NP * s
            for e in range(64):
                idx[pi * 128 + s * 64 + e] = e * H + h
    wp_h = np.ascontiguousarray(np.asarray(Wp, np.float32)[idx])
    in_maps = []
    ncore = 2 * B
    for c in range(ncore):
        b, half = c // 2, c % 2
        ours = x[b, half * QH:(half + 1) * QH]
        other = x[b, (1 - half) * QH:(2 - half) * QH]
        xkv = np.ascontiguousarray(np.concatenate([ours, other], axis=0))
        in_maps.append({
            "xin": xkv,
            "wq": wq_h, "wk": wk_h, "wv": wv_h, "wp": wp_h,
            "bp": np.asarray(bp, np.float32).reshape(1, D),
            "gw": np.ascontiguousarray(np.asarray(gate_w, np.float32)),
            "gb": np.asarray(gate_b, np.float32).reshape(1, NE),
            "we": np.ascontiguousarray(np.asarray(expert_w, np.float32)),
            "be": np.ascontiguousarray(np.asarray(expert_b, np.float32)),
            "idn": np.eye(128, dtype=np.float32),
            "e2d": np.repeat(np.eye(2, dtype=np.float32), 64, axis=1),
        })
    return in_maps


def kernel(L0_attn_query, L0_attn_Wq, L0_attn_Wk, L0_attn_Wv, L0_attn_Wp,
           L0_attn_bp, L0_ffn_gate_w, L0_ffn_gate_b, L0_ffn_expert_w,
           L0_ffn_expert_b):
    cfg = CFG_FULL
    in_maps = _prep_inputs(cfg, L0_attn_query, L0_attn_Wq, L0_attn_Wk,
                           L0_attn_Wv, L0_attn_Wp, L0_attn_bp, L0_ffn_gate_w,
                           L0_ffn_gate_b, L0_ffn_expert_w, L0_ffn_expert_b)
    nc = build_program(cfg)
    res = run_bass_kernel_spmd(nc, in_maps, core_ids=list(range(len(in_maps))))
    B, NSEQ, D, QH = 4, cfg["NSEQ"], cfg["D"], cfg["QH"]
    out = np.empty((B, NSEQ, D), dtype=np.float32)
    for c, r in enumerate(res.results):
        b, half = c // 2, c % 2
        out[b, half * QH:(half + 1) * QH] = np.asarray(r["y"], dtype=np.float32)
    return out


# version 32
# speedup vs baseline: 1.0495x; 1.0042x over previous
import numpy as np
from contextlib import ExitStack

import concourse.bass as bass
import concourse.mybir as mybir
from concourse.bass_utils import run_bass_kernel_spmd

F32 = mybir.dt.float32
BF16 = mybir.dt.bfloat16
AF = mybir.ActivationFunctionType
ALU = mybir.AluOpType
AX = mybir.AxisListType.X

ENGS = ("T", "A", "V", "S", "G")

CFG_FULL = dict(D=1024, H=16, NSEQ=4096, QH=2048, MB=512, NE=8, HE=64)
CFG_MINI = dict(D=256, H=4, NSEQ=512, QH=256, MB=256, NE=4, HE=64)


class Chan:
    __slots__ = ("name", "eng", "step", "n", "sem")

    def __init__(self, name, eng, step):
        self.name, self.eng, self.step, self.n, self.sem = name, eng, step, 0, None


class Prog:
    def __init__(self):
        self.items = {e: [] for e in ENGS}
        self.chans = []
        self.lastw = {e: {} for e in ENGS}

    def chan(self, name, eng, dma=False):
        c = Chan(name, eng, 16 if dma else 1)
        self.chans.append(c)
        return c

    def op(self, eng, fn):
        if eng in ("V", "A"):
            self.items[eng].append(("drain",))
        self.items[eng].append(("op", fn))

    def dep(self, eng):
        self.items[eng].append(("drain",))

    def inc(self, ch):
        ch.n += ch.step
        self.items[ch.eng].append(("inc", ch))
        return (ch, ch.n)

    def wait(self, eng, ev):
        if ev is None:
            return
        ch, n = ev
        if n <= 0:
            return
        assert n <= ch.n, (eng, ch.name, n, ch.n)
        if ch.eng == eng and eng in ("T", "A", "V"):
            return
        if self.lastw[eng].get(ch, 0) >= n:
            return
        self.lastw[eng][ch] = n
        self.items[eng].append(("wait", ch, n))

    def waits(self, eng, evs):
        for ev in evs:
            self.wait(eng, ev)

    def now(self, ch):
        return (ch, ch.n)


def build_program(cfg):
    D = cfg["D"]
    H = cfg["H"]
    HE = cfg["HE"]
    NSEQ = cfg["NSEQ"]
    QH = cfg["QH"]
    MB = cfg["MB"]
    NE = cfg["NE"]
    DC = D // 128
    NP = H // 2
    NB = NSEQ // MB
    MC = MB // 128
    NT = QH // 128
    NQW = QH // MB
    QB = min(512, QH)
    NQB = QH // QB
    OH = min(512, D)
    NHALF = D // OH
    MBH = MB // OH if MB >= OH else 1
    assert HE == 64 and D % 128 == 0 and MB % 128 == 0

    nc = bass.Bass(trn_type="TRN2", target_bir_lowering=False)

    xin = nc.dram_tensor("xin", [NSEQ, D], F32, kind="ExternalInput").ap()
    wq = nc.dram_tensor("wq", [D, D], F32, kind="ExternalInput").ap()
    wk = nc.dram_tensor("wk", [D, D], F32, kind="ExternalInput").ap()
    wv = nc.dram_tensor("wv", [D, D], F32, kind="ExternalInput").ap()
    wp = nc.dram_tensor("wp", [D, D], F32, kind="ExternalInput").ap()
    bp = nc.dram_tensor("bp", [1, D], F32, kind="ExternalInput").ap()
    gw = nc.dram_tensor("gw", [D, NE], F32, kind="ExternalInput").ap()
    gb = nc.dram_tensor("gb", [1, NE], F32, kind="ExternalInput").ap()
    we = nc.dram_tensor("we", [NE, D, D], F32, kind="ExternalInput").ap()
    be = nc.dram_tensor("be", [NE, D], F32, kind="ExternalInput").ap()
    idn = nc.dram_tensor("idn", [128, 128], F32, kind="ExternalInput").ap()
    e2d = nc.dram_tensor("e2d", [2, 128], F32, kind="ExternalInput").ap()
    y = nc.dram_tensor("y", [QH, D], F32, kind="ExternalOutput").ap()

    p = Prog()
    chT = p.chan("chT", "T")
    chA = p.chan("chA", "A")
    chV = p.chan("chV", "V")
    chX0 = p.chan("chX0", "S", dma=True)
    chX1 = p.chan("chX1", "S", dma=True)
    chW0 = p.chan("chW0", "S", dma=True)
    chW1 = p.chan("chW1", "S", dma=True)
    chM = p.chan("chM", "S", dma=True)
    chE0 = p.chan("chE0", "S", dma=True)
    chE1 = p.chan("chE1", "S", dma=True)
    chG = p.chan("chG", "G", dma=True)
    chG1 = p.chan("chG1", "G", dma=True)
    chGb = p.chan("chGb", "G", dma=True)
    chC = [p.chan("chC%d" % i, "S", dma=True) for i in range(4)]
    chGc = p.chan("chGc", "G")
    chO = p.chan("chO", "S", dma=True)
    chX = (chX0, chX1)
    chW = (chW0, chW1)
    chE = (chE0, chE1)

    es = ExitStack()
    with es:
        def sb(name, shape, dt=F32, side=None):
            kw = {} if side is None else {"side": side}
            return es.enter_context(nc.sbuf_tensor(name, shape, dt, **kw))

        sid = sb("sid", [128, 128])
        sidb = sb("sidb", [128, 128], BF16)
        cns = sb("cns", [128, 8])
        gstg = sb("gstg", [128, DC * NE])
        gwh = sb("gwh", [128, DC * NE], BF16)
        gwl = sb("gwl", [128, DC * NE], BF16)
        sTb = sb("sTb", [128, 128])
        s_all = sb("s_all", [128, NT * NE])
        gsc = sb("gsc", [128, 6 * NE + 8])
        stats = sb("stats", [128, 16])
        gbsb = sb("gbsb", [1, NE])
        gbr = sb("gbr", [128, NE])
        ones1 = sb("ones1", [1, 128])
        E2 = sb("E2", [2, 128])

        def ps(name, w=1024):
            return es.enter_context(nc.psum_tensor(name, [128, w], F32))

        psW = [ps("psW0"), ps("psW1"), ps("psT")]
        psPV = [ps("psPVa", 512), ps("psPVb", 512)]
        ps_free = {}

        def ps_wait(eng, t, halves=(0, 1)):
            for hf in halves:
                p.waits(eng, ps_free.get((t.name, hf), ()))

        def ps_read(t, ev, halves=(0, 1)):
            for hf in halves:
                ps_free[(t.name, hf)] = [ev]

        def cvt(dst_hi, dst_lo, src_f32, n, evsrc, extra_wait=None):
            p.wait("A", evsrc)
            if extra_wait is not None:
                p.wait("A", extra_wait)
            p.op("A", lambda e: e.copy(dst_hi, src_f32))
            ev_h = p.inc(chA)
            p.wait("V", ev_h)
            p.op("V", lambda e: e.tensor_tensor(dst_lo, src_f32, dst_hi,
                                                ALU.subtract))
            ev_l = p.inc(chV)
            return ev_h, ev_l

        def mm3(out_ap, lh, ll, rh, rl, start, stop):
            p.op("T", lambda e: e.matmul(out_ap, lh, rh, start=start, stop=False))
            p.op("T", lambda e: e.matmul(out_ap, lh, rl, start=False, stop=False))
            p.op("T", lambda e: e.matmul(out_ap, ll, rh, start=False, stop=stop))

        p.op("S", lambda e: e.dma_start(sid[:, :], idn[:, :]))
        evI = p.inc(chM)
        p.op("S", lambda e: e.dma_start(E2[:, :], e2d[:, :]))
        p.inc(chM)
        p.op("S", lambda e: e.dma_start(
            gstg[:, :].rearrange("p (c g) -> p c g", c=DC),
            gw[:, :].rearrange("(c dp) g -> dp c g", dp=128)))
        p.inc(chM)
        p.op("S", lambda e: e.dma_start(gbsb[:, :], gb[:, :]))
        p.inc(chM)
        evM0 = p.now(chM)

        p.op("V", lambda e: e.memset(cns[:, 0:1], -16.0))
        p.op("V", lambda e: e.memset(cns[:, 1:2], 1e-5))
        p.op("V", lambda e: e.memset(ones1[:, :], 1.0))
        p.op("V", lambda e: e.memset(den[:, :], 0.0))
        evZ = p.inc(chV)

        cvt(gwh[:, :], gwl[:, :], gstg[:, :], DC * NE, evM0)
        p.wait("A", evI)
        p.op("A", lambda e: e.copy(sidb[:, :], sid[:, :]))
        evIb = p.inc(chA)

        p.wait("T", evM0)
        p.wait("T", evZ)
        p.op("T", lambda e: e.matmul(psW[1][:, 0:NE], ones1[0:1, :],
                                     gbsb[0:1, :], start=True, stop=True))
        evT = p.inc(chT)
        p.wait("A", evT)
        p.op("A", lambda e: e.copy(gbr[:, :], psW[1][:, 0:NE]))
        evA_gbr = p.inc(chA)
        ps_read(psW[1], evA_gbr)

        lnsc_free = {}

        def ln_tile(xap, dst_writer, lnsc, x_ev):
            p.wait("V", x_ev)
            p.wait("V", lnsc_free.get(lnsc.name))
            p.op("V", lambda e: e.tensor_mul(lnsc[:, :], xap, xap))
            p.op("V", lambda e: e.reduce_sum(stats[:, 0:1], xap, axis=AX))
            p.op("V", lambda e: e.reduce_sum(stats[:, 1:2], lnsc[:, :], axis=AX))
            p.op("V", lambda e: e.tensor_scalar(stats[:, 2:3], stats[:, 0:1],
                                                1.0 / D, None, ALU.mult))
            p.op("V", lambda e: e.tensor_mul(stats[:, 3:4], stats[:, 2:3],
                                             stats[:, 2:3]))
            p.op("V", lambda e: e.scalar_tensor_tensor(
                stats[:, 4:5], stats[:, 1:2], 1.0 / D, stats[:, 3:4],
                ALU.mult, ALU.subtract))
            p.wait("V", evZ)
            p.op("V", lambda e: e.tensor_add(stats[:, 7:8], stats[:, 4:5],
                                             cns[:, 1:2]))
            p.op("V", lambda e: e.reciprocal(stats[:, 8:9], stats[:, 7:8]))
            evv = p.inc(chV)
            p.wait("A", evv)
            p.op("A", lambda e: e.activation(stats[:, 5:6], stats[:, 8:9],
                                             AF.Sqrt))
            eva = p.inc(chA)
            p.wait("V", eva)
            p.op("V", lambda e: e.scalar_tensor_tensor(
                stats[:, 6:7], stats[:, 2:3], -1.0, stats[:, 5:6],
                ALU.mult, ALU.mult))
            p.op("V", lambda e: e.tensor_scalar(lnsc[:, :], xap, stats[:, 5:6],
                                                stats[:, 6:7], ALU.mult, ALU.add))
            evln = p.inc(chV)
            p.wait("T", evln)
            p.wait("T", evI)
            for r0 in range(0, DC, 4):
                hf = (r0 // 4) % 2
                pst = psW[2]
                chunks = list(range(r0, min(r0 + 4, DC)))
                ps_wait("T", pst, (hf,))
                for j, c in enumerate(chunks):
                    p.op("T", lambda e, c=c, j=j, pst=pst, hf=hf: e.transpose(
                        pst[:, hf * 512 + j * 128:hf * 512 + (j + 1) * 128],
                        lnsc[:, c * 128:(c + 1) * 128], sid[:, :]))
                evt = p.inc(chT)
                evr = dst_writer(chunks[0], pst, len(chunks), evt, hf)
                ps_read(pst, evr, (hf,))
            lnsc_free[lnsc.name] = p.now(chT)
            return evln

        HB = QB // 2
        den = sb("den", [128, HB], side="right")
        rdbuf = sb("rdbuf", [128, 2 * QB], side="right")

        with ExitStack() as es1:
            big1 = es1.enter_context(nc.sbuf_tensor("big1", [128, NP * QH], F32))
            p.op("V", lambda e: e.memset(big1[:, :], 0.0))
            evZ2 = p.inc(chV)

            with ExitStack() as esab:
                def sba(name, shape, dt=F32):
                    return esab.enter_context(nc.sbuf_tensor(name, shape, dt))

                xst = sba("xst", [128, D])
                q_h = sba("q_h", [128, NP * QH], BF16)
                q_l = sba("q_l", [128, NP * QH], BF16)
                ln_h = sba("ln_h", [128, DC * MB], BF16)
                ln_l = sba("ln_l", [128, DC * MB], BF16)
                wstg = sba("wstg", [128, DC * 128])
                wkh = sba("wkh", [128, DC * 128], BF16)
                wkl = sba("wkl", [128, DC * 128], BF16)
                wvh = sba("wvh", [128, DC * 128], BF16)
                wvl = sba("wvl", [128, DC * 128], BF16)
                kh2 = [[sba("kh%d%d" % (g, s), [128, MB], BF16)
                        for s in range(2)] for g in range(2)]
                kl2 = [[sba("kl%d%d" % (g, s), [128, MB], BF16)
                        for s in range(2)] for g in range(2)]
                vstg = sba("vstg", [128, MB])
                VS = 68
                vah2 = [[sba("vah%d%d" % (g, s), [128, MC * VS], BF16)
                         for s in range(2)] for g in range(2)]
                val2 = [[sba("val%d%d" % (g, s), [128, MC * VS], BF16)
                         for s in range(2)] for g in range(2)]
                ptf = [sba("ptf%d" % i, [128, QB]) for i in range(2)]
                pth = [[sba("pth%d%d" % (s, j), [128, QB], BF16)
                        for j in range(2)] for s in range(2)]
                ptl = [[sba("ptl%d%d" % (s, j), [128, QB], BF16)
                        for j in range(2)] for s in range(2)]
                stage = [sba("stg%d" % i, [128, 2 * QB]) for i in range(2)]
                lnsc = sba("lnsc_ab", [128, D])

                for g in range(2):
                    for s in range(2):
                        p.op("V", lambda e, g=g, s=s: e.memset(
                            vah2[g][s][:, :], 1.0))
                        p.op("V", lambda e, g=g, s=s: e.memset(
                            val2[g][s][:, :], 0.0))
                        rows0 = slice(64, 128) if s == 0 else slice(0, 64)
                        p.op("V", lambda e, g=g, s=s, rows0=rows0: e.memset(
                            kh2[g][s][rows0, :], 0.0))
                        p.op("V", lambda e, g=g, s=s, rows0=rows0: e.memset(
                            kl2[g][s][rows0, :], 0.0))
                ev_va_init = p.inc(chV)

                xst_free = [None, None]
                wstg_free = [None]
                stage_free = [[], []]
                lnwin_free = []
                q_ready = None
                ptf_free = [None, None]
                pthl_free = [[None, None], [None, None]]

                def load_x_tile(r0, slot_i):
                    p.wait("S", xst_free[0])
                    p.op("S", lambda e, r0=r0: e.dma_start(
                        xst[:, :], xin[r0:r0 + 128, :]))
                    return p.inc(chX[0])

                def ln_window_cbs(row0, nw_tiles):
                    evs = []

                    def mk(tt):
                        def cb():
                            if tt == 0 and lnwin_free:
                                p.wait("A", lnwin_free[-1])
                                p.wait("V", lnwin_free[-1])
                                del lnwin_free[:]
                            evx = load_x_tile(row0 + tt * 128, 0)
                            xap = xst[:, :]

                            def wr(c0, pst, n, evt, hf, tt=tt):
                                hi = bass.AP(ln_h, c0 * MB + tt * 128,
                                             [[DC * MB, 128], [MB, n], [1, 128]])
                                lo = bass.AP(ln_l, c0 * MB + tt * 128,
                                             [[DC * MB, 128], [MB, n], [1, 128]])
                                _, evl = cvt(hi, lo,
                                             pst[:, hf * 512:hf * 512 + n * 128],
                                             n * 128, evt)
                                return evl

                            evln = ln_tile(xap, wr, lnsc, evx)
                            xst_free[0] = evln
                            evs.append(p.now(chV))
                        return cb

                    return evs, [mk(tt) for tt in range(nw_tiles)]

                def ln_window(row0, nw_tiles):
                    evs, cbs = ln_window_cbs(row0, nw_tiles)
                    for cb in cbs:
                        cb()
                    return evs

                def load_w_slice(dram, pcol, slot_i):
                    p.wait("S", wstg_free[0])
                    src = dram[:, pcol * 128:(pcol + 1) * 128].rearrange(
                        "(c dp) e -> dp c e", dp=128)
                    dstv = wstg[:, :].rearrange("p (c e) -> p c e", c=DC)
                    p.op("S", lambda e, src=src, dstv=dstv: e.dma_start(dstv, src))
                    return p.inc(chW[slot_i])

                def gen_mm(pst, wh, wl, w_evl, lnwin_evs, base=0):
                    ps_wait("T", pst, (base // 512,) if MB <= 512 else (0, 1))
                    p.wait("T", w_evl)
                    p.waits("T", lnwin_evs)
                    for c in range(DC):
                        for h in range(MBH):
                            o0 = h * OH
                            w0 = min(OH, MB)
                            mm3(pst[:, base + o0:base + o0 + w0],
                                wh[:, c * 128:(c + 1) * 128],
                                wl[:, c * 128:(c + 1) * 128],
                                bass.AP(ln_h, c * MB + o0,
                                        [[DC * MB, 128], [1, w0]]),
                                bass.AP(ln_l, c * MB + o0,
                                        [[DC * MB, 128], [1, w0]]),
                                start=(c == 0), stop=(c == DC - 1))
                    ev = p.inc(chT)
                    lnwin_free.append(ev)
                    return ev

                for w in range(NQW):
                    lnev = ln_window(w * MB, MB // 128)
                    for pi in range(NP):
                        slot_i = pi % 2
                        wev = load_w_slice(wq, pi, slot_i)
                        wh, wl = (wkh, wkl) if slot_i == 0 else (wvh, wvl)
                        _, wevl = cvt(wh[:, :], wl[:, :], wstg[:, :], DC * 128,
                                      wev)
                        wstg_free[0] = wevl
                        pst = psW[slot_i]
                        evg = gen_mm(pst, wh, wl, wevl, lnev)
                        _, evql = cvt(
                            q_h[:, pi * QH + w * MB: pi * QH + w * MB + MB],
                            q_l[:, pi * QH + w * MB: pi * QH + w * MB + MB],
                            pst[:, 0:MB], MB, evg)
                        ps_read(pst, evql, (0,) if MB <= 512 else (0, 1))
                q_ready = p.now(chV)

                pair_acc_ev = [[None, None] for _ in range(NP)]
                evkl_set = [None, None]
                ev_va_set = [[None, None], [None, None]]
                kread_free = [[], []]
                psT_tr_free = [[], []]

                def make_gen_cbs(mb, pi, seti, lnev):
                    st = {}
                    kh, kl = kh2[seti], kl2[seti]
                    vah, val = vah2[seti], val2[seti]

                    def cb_kload():
                        st["wkev"] = load_w_slice(wk, pi, 0)

                    def cb_kcvt():
                        _, st["wkevl"] = cvt(wkh[:, :], wkl[:, :], wstg[:, :],
                                             DC * 128, st["wkev"])
                        wstg_free[0] = st["wkevl"]

                    def cb_kgen():
                        st["evk"] = gen_mm(psW[2], wkh, wkl, st["wkevl"],
                                           lnev, base=0)

                    def cb_kevac():
                        p.waits("A", kread_free[seti])
                        p.waits("V", kread_free[seti])
                        _, ev1 = cvt(kh[0][0:64, :], kl[0][0:64, :],
                                     psW[2][0:64, 0:MB], MB, st["evk"])
                        _, evkl = cvt(kh[1][64:128, :], kl[1][64:128, :],
                                      psW[2][64:128, 0:MB], MB, st["evk"])
                        ps_read(psW[2], evkl, (0,))
                        evkl_set[seti] = evkl

                    def cb_vload():
                        st["wvev"] = load_w_slice(wv, pi, 1)

                    def cb_vcvt():
                        _, st["wvevl"] = cvt(wvh[:, :], wvl[:, :], wstg[:, :],
                                             DC * 128, st["wvev"])
                        wstg_free[0] = st["wvevl"]

                    def cb_vgen():
                        st["evv"] = gen_mm(psW[2], wvh, wvl, st["wvevl"],
                                           lnev, base=512)

                    def cb_vevac():
                        p.wait("A", st["evv"])
                        p.op("A", lambda e: e.copy(vstg[:, :],
                                                   psW[2][:, 512:512 + MB]))
                        st["evvs"] = p.inc(chA)
                        ps_read(psW[2], st["evvs"], (1,))

                    def mk_vtrans(s):
                        def cb():
                            off = s * 512
                            ps_wait("T", psW[2], (s,))
                            p.wait("T", st["evvs"])
                            p.waits("T", psT_tr_free[s])
                            for c in range(MC):
                                p.op("T", lambda e, s=s, c=c, off=off:
                                     e.transpose(
                                         psW[2][:, off + c * 64:
                                                off + (c + 1) * 64],
                                         vstg[s * 64:(s + 1) * 64,
                                              c * 128:(c + 1) * 128],
                                         sid[s * 64:(s + 1) * 64,
                                             s * 64:(s + 1) * 64]))
                            evt = p.inc(chT)
                            p.waits("A", kread_free[seti])
                            p.waits("V", kread_free[seti])
                            hiap = bass.AP(vah[s], 0,
                                           [[MC * VS, 128], [VS, MC], [1, 64]])
                            loap = bass.AP(val[s], 0,
                                           [[MC * VS, 128], [VS, MC], [1, 64]])
                            _, eva = cvt(hiap, loap,
                                         psW[2][:, off:off + MC * 64],
                                         MC * 64, evt)
                            ps_read(psW[2], eva, (s,))
                            ev_va_set[seti][s] = eva
                        return cb

                    return [cb_kload, cb_kcvt, cb_kgen, cb_kevac,
                            cb_vload, cb_vcvt, cb_vgen, cb_vevac,
                            mk_vtrans(0), mk_vtrans(1)]

                units = [(mb, pi) for mb in range(NB) for pi in range(NP)]
                lnev_cur = None
                lnev_next = None
                ln_next_cbs = []
                for ui, (mb, pi) in enumerate(units):
                    seti = ui % 2
                    if pi == 0:
                        if mb == 0 or lnev_next is None:
                            lnev_cur = ln_window(mb * MB, MB // 128)
                            for cb in make_gen_cbs(mb, 0, seti, lnev_cur):
                                cb()
                        else:
                            lnev_cur = lnev_next
                    evkl = evkl_set[seti]
                    ev_va = ev_va_set[seti]
                    k_h, k_l = kh2[seti], kl2[seti]
                    vah, val = vah2[seti], val2[seti]
                    if pi + 1 < NP:
                        gen_cbs = make_gen_cbs(mb, pi + 1, (ui + 1) % 2,
                                               lnev_cur)
                        if pi + 2 == NP and mb + 1 < NB and NP >= 2:
                            lnev_next, cbs4 = ln_window_cbs((mb + 1) * MB,
                                                            MB // 128)
                            nh = len(cbs4) // 2
                            gen_cbs = gen_cbs + cbs4[:nh]
                            ln_next_cbs = cbs4[nh:]
                    elif mb + 1 < NB and lnev_next is not None:
                        gen_cbs = ln_next_cbs + make_gen_cbs(mb + 1, 0,
                                                             (ui + 1) % 2,
                                                             lnev_next)
                        ln_next_cbs = []
                    else:
                        gen_cbs = []

                    pv_done = [None, None]

                    def emit_S_chain(i, qb, c, s):
                        hs = qb * MC + c
                        par = hs % 2
                        gpar = i % 2
                        off = par * 512
                        pss = psW[s]
                        q0 = pi * QH + qb * QB
                        ps_wait("T", pss, (par,))
                        p.wait("T", evkl)
                        p.wait("T", q_ready)
                        mm3(pss[:, off:off + QB],
                            k_h[s][:, c * 128:(c + 1) * 128],
                            k_l[s][:, c * 128:(c + 1) * 128],
                            q_h[:, q0:q0 + QB],
                            q_l[:, q0:q0 + QB],
                            start=True, stop=True)
                        evs_mm = p.inc(chT)
                        p.wait("A", evs_mm)
                        p.wait("A", evZ)
                        p.wait("A", ptf_free[gpar])
                        p.op("A", lambda e, gpar=gpar, pss=pss, off=off:
                             e.activation(ptf[gpar][:, :],
                                          pss[:, off:off + QB], AF.Exp,
                                          bias=cns[:, 0:1], scale=0.125))
                        ev_exp = p.inc(chA)
                        ps_read(pss, ev_exp, (par,))
                        p.wait("V", ev_exp)
                        p.wait("V", pthl_free[s][par])
                        p.op("V", lambda e, s=s, par=par, gpar=gpar:
                             e.tensor_copy(pth[s][par][:, :],
                                           ptf[gpar][:, :]))
                        p.op("V", lambda e, s=s, par=par, gpar=gpar:
                             e.tensor_tensor(ptl[s][par][:, :],
                                             ptf[gpar][:, :],
                                             pth[s][par][:, :],
                                             ALU.subtract))
                        ev_l = p.inc(chV)
                        ptf_free[gpar] = ev_l
                        return ev_l, par

                    def flush_slot(qb, s, ev_pv):
                        sg = qb % 2
                        chGs = (chG, chG1)[sg]
                        ce = "A" if s == 0 else "V"
                        p.waits(ce, stage_free[sg])
                        p.wait(ce, ev_pv)
                        if s == 0:
                            p.op("A", lambda e, sg=sg: e.copy(
                                stage[sg][0:65, 0:QB], psPV[0][0:65, 0:QB]))
                            evc = p.inc(chA)
                        else:
                            p.op("V", lambda e, sg=sg: e.tensor_copy(
                                stage[sg][0:65, QB:2 * QB], psPV[1][0:65, 0:QB]))
                            evc = p.inc(chV)
                        ps_read(psPV[s], evc, (0,))
                        p.wait("G", evc)
                        p.wait("G", ev_va_init)
                        p.wait("G", evZ2)
                        r0, r1 = (0, 64) if s == 0 else (64, 128)
                        p.op("G", lambda e, pi=pi, qb=qb, sg=sg, s=s,
                             r0=r0, r1=r1: e.dma_start(
                                 big1[r0:r1, pi * QH + qb * QB:
                                      pi * QH + (qb + 1) * QB],
                                 stage[sg][0:64, s * QB:(s + 1) * QB],
                                 accum_op=ALU.add))
                        p.inc(chGs)
                        hh0 = pi if s == 0 else pi + NP
                        p.op("G", lambda e, qb=qb, sg=sg, s=s, hh0=hh0:
                             e.dma_start(
                                 bass.AP(den, (hh0 * NQB + qb) * 2 * HB,
                                         [[HB, 2], [1, HB]]),
                                 stage[sg][64:65, s * QB:(s + 1) * QB],
                                 accum_op=ALU.add))
                        evg = p.inc(chGs)
                        if s == 1:
                            stage_free[sg] = [evg]
                            pair_acc_ev[pi][sg] = evg

                    def emit_PV(st2, chain_ev, par):
                        qb, c, s = st2
                        if c == 0:
                            ps_wait("T", psPV[s], (0,))
                        p.wait("T", chain_ev)
                        p.wait("T", ev_va[s])
                        mm3(psPV[s][0:65, 0:QB],
                            vah[s][:, c * VS:c * VS + 65],
                            val[s][:, c * VS:c * VS + 65],
                            pth[s][par][:, :],
                            ptl[s][par][:, :],
                            start=(c == 0), stop=(c == MC - 1))
                        ev_pv = p.inc(chT)
                        pthl_free[s][par] = ev_pv
                        if c == MC - 1:
                            pv_done[s] = ev_pv
                            flush_slot(qb, s, ev_pv)

                    steps = [(qb, c, s) for qb in range(NQB)
                             for c in range(MC) for s in range(2)]
                    pend = []
                    for i, st2 in enumerate(steps):
                        ch_ev, par = emit_S_chain(i, *st2)
                        pend.append((st2, ch_ev, par))
                        if len(pend) > 3:
                            emit_PV(*pend.pop(0))
                        if gen_cbs and i >= 2 and (i - 2) % 3 == 0:
                            gen_cbs.pop(0)()
                    while gen_cbs:
                        gen_cbs.pop(0)()
                    while pend:
                        emit_PV(*pend.pop(0))
                    kread_free[seti] = [p.now(chT)]
                    psT_tr_free[0] = [p.now(chT)]
                    psT_tr_free[1] = [p.now(chT)]

                bar_pre = [p.now(chT), p.now(chA), p.now(chV), p.now(chG),
                           p.now(chG1)]
                for pe in pair_acc_ev:
                    p.waits("V", pe)
                p.op("V", lambda e: e.reciprocal(den[:, :], den[:, :]))
                ev_rcp = p.inc(chV)

            out_acc = sb("out_acc", [128, NT * D], side="right")
            with ExitStack() as espd:
                def sbp(name, shape, dt=F32):
                    return espd.enter_context(nc.sbuf_tensor(name, shape, dt))

                wph = sbp("wph", [128, DC * D], BF16)
                wpl = sbp("wpl", [128, DC * D], BF16)
                wpstg = sbp("wpstg", [128, 2 * D])
                bpr = sbp("bpr", [128, D])
                bpsb = sbp("bpsb", [1, D])
                xst2 = sbp("xst2", [128, 2 * D])
                fch = [sbp("fch%d" % i, [128, 128], BF16) for i in range(2)]
                fcl = [sbp("fcl%d" % i, [128, 128], BF16) for i in range(2)]
                rdb2 = sbp("rdb2", [128, 2 * QB])

                p.waits("S", bar_pre)
                ev_mult_qb = [None] * NQB
                cslot_free = [None] * 4
                citer = [(qb, pi) for qb in range(NQB) for pi in range(NP)]
                cpos = [0]

                def emit_c(nmax):
                    while cpos[0] < len(citer) and nmax > 0:
                        i = cpos[0]
                        qb, pi = citer[i]
                        cpos[0] += 1
                        nmax -= 1
                        sl = i % 4
                        hf = i % 2
                        buf = rdbuf if sl < 2 else rdb2
                        c0 = (sl % 2) * QB
                        p.wait("S", ev_rcp)
                        p.wait("S", cslot_free[sl])
                        p.op("S", lambda e, pi=pi, qb=qb, buf=buf, c0=c0:
                             e.dma_start(
                                 buf[0:1, c0:c0 + QB],
                                 bass.AP(den, (pi * NQB + qb) * 2 * HB,
                                         [[HB, 2], [1, HB]])))
                        p.inc(chC[sl])
                        p.op("S", lambda e, pi=pi, qb=qb, buf=buf, c0=c0:
                             e.dma_start(
                                 buf[1:2, c0:c0 + QB],
                                 bass.AP(den, ((pi + NP) * NQB + qb) * 2 * HB,
                                         [[HB, 2], [1, HB]])))
                        evd = p.inc(chC[sl])
                        ps_wait("T", psW[2], (hf,))
                        p.wait("T", evd)
                        p.wait("T", evM0)
                        p.op("T", lambda e, buf=buf, c0=c0, hf=hf: e.matmul(
                            psW[2][:, hf * QB:(hf + 1) * QB], E2[0:2, :],
                            buf[0:2, c0:c0 + QB], start=True, stop=True))
                        ev_rep = p.inc(chT)
                        cslot_free[sl] = ev_rep
                        p.wait("V", ev_rep)
                        p.op("V", lambda e, pi=pi, qb=qb, hf=hf: e.tensor_mul(
                            big1[:, pi * QH + qb * QB:
                                 pi * QH + (qb + 1) * QB],
                            big1[:, pi * QH + qb * QB:
                                 pi * QH + (qb + 1) * QB],
                            psW[2][:, hf * QB:(hf + 1) * QB]))
                        ev_mult = p.inc(chV)
                        ps_read(psW[2], ev_mult, (hf,))
                        if pi == NP - 1:
                            ev_mult_qb[qb] = ev_mult

                wps_free = [None, None]
                for c in range(DC):
                    sl = c % 2
                    p.wait("S", wps_free[sl])
                    p.op("S", lambda e, c=c, sl=sl: e.dma_start(
                        wpstg[:, sl * D:(sl + 1) * D],
                        wp[c * 128:(c + 1) * 128, :]))
                    evw = p.inc(chM if sl == 0 else chW1)
                    _, evwl = cvt(wph[:, c * D:(c + 1) * D],
                                  wpl[:, c * D:(c + 1) * D],
                                  wpstg[:, sl * D:(sl + 1) * D], D, evw)
                    wps_free[sl] = evwl
                    emit_c(max(1, len(citer) // DC))
                emit_c(len(citer))
                ev_wp = evwl
                p.op("S", lambda e: e.dma_start(bpsb[:, :], bp[:, :]))
                evbp = p.inc(chW0)
                p.wait("T", evbp)
                p.wait("T", evZ)
                for hh in range(NHALF):
                    pst = psW[hh % 2]
                    ps_wait("T", pst)
                    p.op("T", lambda e, hh=hh, pst=pst: e.matmul(
                        pst[:, 0:OH], ones1[0:1, :],
                        bpsb[0:1, hh * OH:(hh + 1) * OH], start=True,
                        stop=True))
                    evt = p.inc(chT)
                    p.wait("A", evt)
                    p.op("A", lambda e, hh=hh, pst=pst: e.copy(
                        bpr[:, hh * OH:(hh + 1) * OH], pst[:, 0:OH]))
                    eva = p.inc(chA)
                    ps_read(pst, eva)
                ev_bpr = p.now(chA)

                xst2_free = [None, None]
                fc_free = [None, None]
                for t in range(NT):
                    pst = psW[t % 2]
                    ps_wait("T", pst)
                    p.wait("T", ev_wp)
                    for c in range(DC):
                        sl = c % 2
                        p.wait("A", fc_free[sl])
                        _, evf = cvt(fch[sl][:, :], fcl[sl][:, :],
                                     big1[:, c * QH + t * 128:
                                          c * QH + t * 128 + 128],
                                     128, ev_mult_qb[t // (QB // 128)])
                        p.wait("T", evf)
                        for hh in range(NHALF):
                            mm3(pst[:, hh * OH:(hh + 1) * OH],
                                fch[sl][:, :], fcl[sl][:, :],
                                wph[:, c * D + hh * OH: c * D + (hh + 1) * OH],
                                wpl[:, c * D + hh * OH: c * D + (hh + 1) * OH],
                                start=(c == 0), stop=(c == DC - 1))
                        evt = p.inc(chT)
                        fc_free[sl] = evt
                    ev_proj = p.now(chT)
                    sl2 = t % 2
                    p.wait("S", xst2_free[sl2])
                    p.op("S", lambda e, t=t, sl2=sl2: e.dma_start(
                        xst2[:, sl2 * D:(sl2 + 1) * D],
                        xin[t * 128:(t + 1) * 128, :]))
                    evx = p.inc(chX[sl2])
                    oa = out_acc[:, t * D:(t + 1) * D]
                    p.wait("V", ev_proj)
                    p.wait("V", ev_bpr)
                    p.op("V", lambda e, oa=oa, pst=pst: e.tensor_add(
                        oa, pst[0:128, 0:D], bpr[:, :]))
                    ev1 = p.inc(chV)
                    ps_read(pst, ev1)
                    p.wait("V", evx)
                    p.op("V", lambda e, oa=oa, sl2=sl2: e.tensor_add(
                        oa, oa, xst2[:, sl2 * D:(sl2 + 1) * D]))
                    ev_x2 = p.inc(chV)
                    xst2_free[sl2] = ev_x2
                x2_done_V = p.now(chV)
                x2_done_T = p.now(chT)

        with ExitStack() as es2:
            def sb2(name, shape, dt=F32):
                return es2.enter_context(nc.sbuf_tensor(name, shape, dt))

            ln2h = sb2("ln2h", [128, NP * QH], BF16)
            ln2l = sb2("ln2l", [128, NP * QH], BF16)
            besb = sb2("besb", [128, D])
            lnsc2 = sb2("lnsc_d", [128, D])

            p.waits("S", [x2_done_V, x2_done_T])
            p.wait("V", x2_done_T)
            p.wait("A", x2_done_T)
            p.op("S", lambda e: e.dma_start(besb[0:NE, :], be[:, :]))
            ev_be = p.inc(chM)

            espe = es2.enter_context(ExitStack())

            def sbe(name, shape, dt=F32):
                return espe.enter_context(nc.sbuf_tensor(name, shape, dt))

            estg = sbe("estg", [128, DC * OH])
            weh = [sbe("weh%d" % i, [128, DC * OH], BF16) for i in range(2)]
            wel = [sbe("wel%d" % i, [128, DC * OH], BF16) for i in range(2)]
            tasks = [(ei, hh) for ei in range(NE) for hh in range(NHALF)]
            web_free = [None, None]
            estg_free = [None]
            ev_we = [None, None]
            ev_ln2_t = [None] * NT

            def load_task(ti, sl):
                ei, hh = tasks[ti]
                p.wait("S", estg_free[0])
                src3 = we[ei, :, :].rearrange("(c dp) o -> dp c o", dp=128)
                src3 = src3[:, :, hh * OH:(hh + 1) * OH]
                p.op("S", lambda e, src3=src3: e.dma_start(
                    estg[:, :].rearrange("p (c o) -> p c o", c=DC), src3))
                evl = p.inc(chE[sl])
                p.waits("A", [web_free[sl]])
                p.waits("V", [web_free[sl]])
                _, evcl = cvt(weh[sl][:, :], wel[sl][:, :], estg[:, :],
                              DC * OH, evl)
                estg_free[0] = evcl
                ev_we[sl] = evcl

            def emit_task_tile(ti, sl, t, ev_ln2_dep):
                ei, hh = tasks[ti]
                pst = psW[t % 3]
                ps_wait("T", pst)
                p.wait("T", ev_we[sl])
                p.wait("T", ev_ln2_dep)
                for c in range(DC):
                    mm3(pst[:, 0:OH],
                        ln2h[:, c * QH + t * 128: c * QH + t * 128 + 128],
                        ln2l[:, c * QH + t * 128: c * QH + t * 128 + 128],
                        weh[sl][:, c * OH:(c + 1) * OH],
                        wel[sl][:, c * OH:(c + 1) * OH],
                        start=(c == 0), stop=(c == DC - 1))
                ev_y = p.inc(chT)
                p.wait("V", ev_y)
                p.op("V", lambda e, hh=hh, t=t, ei=ei, pst=pst:
                     e.scalar_tensor_tensor(
                         out_acc[:, t * D + hh * OH: t * D + (hh + 1) * OH],
                         pst[:, 0:OH],
                         s_all[:, t * NE + ei:t * NE + ei + 1],
                         out_acc[:, t * D + hh * OH: t * D + (hh + 1) * OH],
                         ALU.mult, ALU.add))
                ev_o = p.inc(chV)
                ps_read(pst, ev_o)
                if t == NT - 1:
                    web_free[sl] = p.now(chT)
                if ei == NE - 1:
                    p.wait("S", ev_o)
                    p.op("S", lambda e, t=t, hh=hh: e.dma_start(
                        y[t * 128:(t + 1) * 128, hh * OH:(hh + 1) * OH],
                        out_acc[:, t * D + hh * OH: t * D + (hh + 1) * OH]))
                    p.inc(chO)

            load_task(0, 0)
            for t in range(NT):
                oa = out_acc[:, t * D:(t + 1) * D]

                def wr2(c0, pst, n, evt, hf, t=t):
                    hi = bass.AP(ln2h, c0 * QH + t * 128,
                                 [[NP * QH, 128], [QH, n], [1, 128]])
                    lo = bass.AP(ln2l, c0 * QH + t * 128,
                                 [[NP * QH, 128], [QH, n], [1, 128]])
                    _, evl = cvt(hi, lo,
                                 pst[:, hf * 512:hf * 512 + n * 128],
                                 n * 128, evt)
                    return evl

                ln_tile(oa, wr2, lnsc2, x2_done_V)
                ev_ln2 = p.now(chV)
                pst = psW[1]
                ps_wait("T", pst)
                p.wait("T", ev_ln2)
                p.wait("T", p.now(chA))
                for c in range(DC):
                    mm3(pst[:, 0:NE],
                        ln2h[:, c * QH + t * 128: c * QH + t * 128 + 128],
                        ln2l[:, c * QH + t * 128: c * QH + t * 128 + 128],
                        gwh[:, c * NE:(c + 1) * NE],
                        gwl[:, c * NE:(c + 1) * NE],
                        start=(c == 0), stop=(c == DC - 1))
                ev_gmm = p.inc(chT)
                z0, e0, g0, sl0, w0, sc0 = 0, NE, 2 * NE, 3 * NE, 4 * NE, 6 * NE
                p.wait("V", ev_gmm)
                p.wait("V", evA_gbr)
                p.op("V", lambda e, pst=pst: e.tensor_add(
                    gsc[:, z0:z0 + NE], pst[:, 0:NE], gbr[:, :]))
                ev_z = p.inc(chV)
                ps_read(pst, ev_z)
                p.op("V", lambda e: e.reduce_max(gsc[:, sc0:sc0 + 1],
                                                 gsc[:, z0:z0 + NE], axis=AX,
                                                 negate=True))
                ev_nm = p.inc(chV)
                p.wait("A", ev_nm)
                p.op("A", lambda e: e.activation(gsc[:, e0:e0 + NE],
                                                 gsc[:, z0:z0 + NE], AF.Exp,
                                                 bias=gsc[:, sc0:sc0 + 1],
                                                 scale=1.0))
                ev_ge = p.inc(chA)
                p.wait("V", ev_ge)
                p.op("V", lambda e: e.reduce_sum(gsc[:, sc0 + 1:sc0 + 2],
                                                 gsc[:, e0:e0 + NE], axis=AX))
                p.op("V", lambda e: e.reciprocal(gsc[:, sc0 + 2:sc0 + 3],
                                                 gsc[:, sc0 + 1:sc0 + 2]))
                p.op("V", lambda e: e.tensor_scalar(gsc[:, g0:g0 + NE],
                                                    gsc[:, e0:e0 + NE],
                                                    gsc[:, sc0 + 2:sc0 + 3],
                                                    None, ALU.mult))
                p.op("V", lambda e: e.tensor_single_scalar(
                    gsc[:, sl0:sl0 + NE], gsc[:, g0:g0 + NE], 0.1, ALU.is_ge))
                p.op("V", lambda e: e.tensor_mul(gsc[:, w0:w0 + NE],
                                                 gsc[:, g0:g0 + NE],
                                                 gsc[:, sl0:sl0 + NE]))
                p.op("V", lambda e: e.reduce_sum(gsc[:, sc0 + 3:sc0 + 4],
                                                 gsc[:, w0:w0 + NE], axis=AX))
                p.op("V", lambda e: e.reciprocal(gsc[:, sc0 + 4:sc0 + 5],
                                                 gsc[:, sc0 + 3:sc0 + 4]))
                p.op("V", lambda e, t=t: e.tensor_scalar(
                    s_all[:, t * NE:(t + 1) * NE], gsc[:, w0:w0 + NE],
                    gsc[:, sc0 + 4:sc0 + 5], None, ALU.mult))
                ev_s = p.inc(chV)
                ps_wait("T", psW[2], (0,))
                p.wait("T", ev_s)
                p.op("T", lambda e, t=t: e.transpose(
                    psW[2][0:NE, 0:128], s_all[:, t * NE:(t + 1) * NE],
                    sid[:, :]))
                ev_st = p.inc(chT)
                p.wait("A", ev_st)
                p.op("A", lambda e: e.copy(sTb[0:NE, :], psW[2][0:NE, 0:128]))
                ev_stc = p.inc(chA)
                ps_read(psW[2], ev_stc)
                p.wait("T", ev_stc)
                p.wait("T", ev_be)
                for hh in range(NHALF):
                    pst2 = psPV[hh % 2]
                    ps_wait("T", pst2, (0,))
                    p.op("T", lambda e, hh=hh, pst2=pst2: e.matmul(
                        pst2[:, 0:OH], sTb[0:NE, :],
                        besb[0:NE, hh * OH:(hh + 1) * OH],
                        start=True, stop=True))
                    evb = p.inc(chT)
                    p.wait("V", evb)
                    p.op("V", lambda e, hh=hh, t=t, pst2=pst2: e.tensor_add(
                        out_acc[:, t * D + hh * OH: t * D + (hh + 1) * OH],
                        out_acc[:, t * D + hh * OH: t * D + (hh + 1) * OH],
                        pst2[:, 0:OH]))
                    evv = p.inc(chV)
                    ps_read(pst2, evv, (0,))
                ev_ln2_t[t] = ev_ln2
                emit_task_tile(0, 0, t, ev_ln2)
            ln2_all = p.now(chV)

            load_task(1, 1)
            for ti in range(1, len(tasks)):
                sl = ti % 2
                if ti + 1 < len(tasks):
                    load_task(ti + 1, (ti + 1) % 2)
                for t in range(NT):
                    emit_task_tile(ti, sl, t, ln2_all)

        ev_out = p.now(chO)

        for c in p.chans:
            c.sem = es.enter_context(nc.semaphore(c.name))
        block = es.enter_context(nc.Block())
        eng_map = {"T": block.tensor, "A": block.scalar, "V": block.vector,
                   "S": block.sync, "G": block.gpsimd}

        def runner(eng_name):
            def run(engobj):
                last = None
                for it in p.items[eng_name]:
                    if it[0] == "op":
                        r = it[1](engobj)
                        if r is not None:
                            last = r
                    elif it[0] == "inc":
                        last.then_inc(it[1].sem, it[1].step)
                    elif it[0] == "drain":
                        engobj.drain()
                    else:
                        engobj.wait_ge(it[1].sem, it[2])
                if eng_name == "S":
                    engobj.wait_ge(ev_out[0].sem, ev_out[1])
            return run

        for name, deco in eng_map.items():
            deco(runner(name))
    return nc


def _prep_inputs(cfg, L0_attn_query, Wq, Wk, Wv, Wp, bp, gate_w, gate_b,
                 expert_w, expert_b):
    D, H, NSEQ, QH, NE = (cfg["D"], cfg["H"], cfg["NSEQ"], cfg["QH"], cfg["NE"])
    NP = H // 2
    x = np.ascontiguousarray(np.asarray(L0_attn_query, dtype=np.float32))
    B = x.shape[0]
    head_order = []
    for pi in range(NP):
        head_order += [pi, pi + NP]
    wq_h = np.ascontiguousarray(
        np.asarray(Wq, np.float32).transpose(1, 0, 2)[:, head_order, :].reshape(D, D))
    wk_h = np.ascontiguousarray(
        np.asarray(Wk, np.float32).transpose(1, 0, 2)[:, head_order, :].reshape(D, D))
    wv_h = np.ascontiguousarray(
        np.asarray(Wv, np.float32).transpose(1, 0, 2)[:, head_order, :].reshape(D, D))
    idx = np.empty(D, dtype=np.int64)
    for pi in range(NP):
        for s in range(2):
            h = pi + NP * s
            for e in range(64):
                idx[pi * 128 + s * 64 + e] = e * H + h
    wp_h = np.ascontiguousarray(np.asarray(Wp, np.float32)[idx])
    in_maps = []
    ncore = 2 * B
    for c in range(ncore):
        b, half = c // 2, c % 2
        ours = x[b, half * QH:(half + 1) * QH]
        other = x[b, (1 - half) * QH:(2 - half) * QH]
        xkv = np.ascontiguousarray(np.concatenate([ours, other], axis=0))
        in_maps.append({
            "xin": xkv,
            "wq": wq_h, "wk": wk_h, "wv": wv_h, "wp": wp_h,
            "bp": np.asarray(bp, np.float32).reshape(1, D),
            "gw": np.ascontiguousarray(np.asarray(gate_w, np.float32)),
            "gb": np.asarray(gate_b, np.float32).reshape(1, NE),
            "we": np.ascontiguousarray(np.asarray(expert_w, np.float32)),
            "be": np.ascontiguousarray(np.asarray(expert_b, np.float32)),
            "idn": np.eye(128, dtype=np.float32),
            "e2d": np.repeat(np.eye(2, dtype=np.float32), 64, axis=1),
        })
    return in_maps


def kernel(L0_attn_query, L0_attn_Wq, L0_attn_Wk, L0_attn_Wv, L0_attn_Wp,
           L0_attn_bp, L0_ffn_gate_w, L0_ffn_gate_b, L0_ffn_expert_w,
           L0_ffn_expert_b):
    cfg = CFG_FULL
    in_maps = _prep_inputs(cfg, L0_attn_query, L0_attn_Wq, L0_attn_Wk,
                           L0_attn_Wv, L0_attn_Wp, L0_attn_bp, L0_ffn_gate_w,
                           L0_ffn_gate_b, L0_ffn_expert_w, L0_ffn_expert_b)
    nc = build_program(cfg)
    res = run_bass_kernel_spmd(nc, in_maps, core_ids=list(range(len(in_maps))))
    B, NSEQ, D, QH = 4, cfg["NSEQ"], cfg["D"], cfg["QH"]
    out = np.empty((B, NSEQ, D), dtype=np.float32)
    for c, r in enumerate(res.results):
        b, half = c // 2, c % 2
        out[b, half * QH:(half + 1) * QH] = np.asarray(r["y"], dtype=np.float32)
    return out


# version 33
# speedup vs baseline: 1.1052x; 1.0531x over previous
import numpy as np
from contextlib import ExitStack

import concourse.bass as bass
import concourse.mybir as mybir
from concourse.bass_utils import run_bass_kernel_spmd

F32 = mybir.dt.float32
BF16 = mybir.dt.bfloat16
AF = mybir.ActivationFunctionType
ALU = mybir.AluOpType
AX = mybir.AxisListType.X

ENGS = ("T", "A", "V", "S", "G")

CFG_FULL = dict(D=1024, H=16, NSEQ=4096, QH=2048, MB=512, NE=8, HE=64)
CFG_MINI = dict(D=256, H=4, NSEQ=512, QH=256, MB=256, NE=4, HE=64)


class Chan:
    __slots__ = ("name", "eng", "step", "n", "sem")

    def __init__(self, name, eng, step):
        self.name, self.eng, self.step, self.n, self.sem = name, eng, step, 0, None


class Prog:
    def __init__(self):
        self.items = {e: [] for e in ENGS}
        self.chans = []
        self.lastw = {e: {} for e in ENGS}

    def chan(self, name, eng, dma=False):
        c = Chan(name, eng, 16 if dma else 1)
        self.chans.append(c)
        return c

    def op(self, eng, fn):
        if eng in ("V", "A"):
            self.items[eng].append(("drain",))
        self.items[eng].append(("op", fn))

    def dep(self, eng):
        self.items[eng].append(("drain",))

    def inc(self, ch):
        ch.n += ch.step
        self.items[ch.eng].append(("inc", ch))
        return (ch, ch.n)

    def wait(self, eng, ev):
        if ev is None:
            return
        ch, n = ev
        if n <= 0:
            return
        assert n <= ch.n, (eng, ch.name, n, ch.n)
        if ch.eng == eng and eng in ("T", "A", "V"):
            return
        if self.lastw[eng].get(ch, 0) >= n:
            return
        self.lastw[eng][ch] = n
        self.items[eng].append(("wait", ch, n))

    def waits(self, eng, evs):
        for ev in evs:
            self.wait(eng, ev)

    def now(self, ch):
        return (ch, ch.n)


def build_program(cfg):
    D = cfg["D"]
    H = cfg["H"]
    HE = cfg["HE"]
    NSEQ = cfg["NSEQ"]
    QH = cfg["QH"]
    MB = cfg["MB"]
    NE = cfg["NE"]
    DC = D // 128
    NP = H // 2
    NB = NSEQ // MB
    MC = MB // 128
    NT = QH // 128
    NQW = QH // MB
    QB = min(512, QH)
    NQB = QH // QB
    OH = min(512, D)
    NHALF = D // OH
    MBH = MB // OH if MB >= OH else 1
    assert HE == 64 and D % 128 == 0 and MB % 128 == 0

    nc = bass.Bass(trn_type="TRN2", target_bir_lowering=False)

    xin = nc.dram_tensor("xin", [NSEQ, D], F32, kind="ExternalInput").ap()
    wq = nc.dram_tensor("wq", [D, D], F32, kind="ExternalInput").ap()
    wk = nc.dram_tensor("wk", [D, D], F32, kind="ExternalInput").ap()
    wv = nc.dram_tensor("wv", [D, D], F32, kind="ExternalInput").ap()
    wp = nc.dram_tensor("wp", [D, D], F32, kind="ExternalInput").ap()
    bp = nc.dram_tensor("bp", [1, D], F32, kind="ExternalInput").ap()
    gw = nc.dram_tensor("gw", [D, NE], F32, kind="ExternalInput").ap()
    gb = nc.dram_tensor("gb", [1, NE], F32, kind="ExternalInput").ap()
    we = nc.dram_tensor("we", [NE, D, D], F32, kind="ExternalInput").ap()
    be = nc.dram_tensor("be", [NE, D], F32, kind="ExternalInput").ap()
    idn = nc.dram_tensor("idn", [128, 128], F32, kind="ExternalInput").ap()
    e2d = nc.dram_tensor("e2d", [2, 128], F32, kind="ExternalInput").ap()
    y = nc.dram_tensor("y", [QH, D], F32, kind="ExternalOutput").ap()

    p = Prog()
    chT = p.chan("chT", "T")
    chA = p.chan("chA", "A")
    chV = p.chan("chV", "V")
    chX0 = p.chan("chX0", "S", dma=True)
    chX1 = p.chan("chX1", "S", dma=True)
    chW0 = p.chan("chW0", "S", dma=True)
    chW1 = p.chan("chW1", "S", dma=True)
    chM = p.chan("chM", "S", dma=True)
    chE0 = p.chan("chE0", "S", dma=True)
    chE1 = p.chan("chE1", "S", dma=True)
    chG = p.chan("chG", "G", dma=True)
    chG1 = p.chan("chG1", "G", dma=True)
    chGb = p.chan("chGb", "G", dma=True)
    chC = [p.chan("chC%d" % i, "S", dma=True) for i in range(4)]
    chGc = p.chan("chGc", "G")
    chO = p.chan("chO", "S", dma=True)
    chX = (chX0, chX1)
    chW = (chW0, chW1)
    chE = (chE0, chE1)

    es = ExitStack()
    with es:
        def sb(name, shape, dt=F32, side=None):
            kw = {} if side is None else {"side": side}
            return es.enter_context(nc.sbuf_tensor(name, shape, dt, **kw))

        sid = sb("sid", [128, 128])
        sidb = sb("sidb", [128, 128], BF16)
        cns = sb("cns", [128, 8])
        gstg = sb("gstg", [128, DC * NE])
        gwh = sb("gwh", [128, DC * NE], BF16)
        gwl = sb("gwl", [128, DC * NE], BF16)
        sTb = sb("sTb", [128, 128])
        s_all = sb("s_all", [128, NT * NE])
        gsc = sb("gsc", [128, 6 * NE + 8])
        stats = sb("stats", [128, 16])
        gbsb = sb("gbsb", [1, NE])
        gbr = sb("gbr", [128, NE])
        ones1 = sb("ones1", [1, 128])
        E2 = sb("E2", [2, 128])

        def ps(name, w=1024):
            return es.enter_context(nc.psum_tensor(name, [128, w], F32))

        psW = [ps("psW0"), ps("psW1"), ps("psT")]
        psPV = [ps("psPVa", 512), ps("psPVb", 512)]
        ps_free = {}

        def ps_wait(eng, t, halves=(0, 1)):
            for hf in halves:
                p.waits(eng, ps_free.get((t.name, hf), ()))

        def ps_read(t, ev, halves=(0, 1)):
            for hf in halves:
                ps_free[(t.name, hf)] = [ev]

        def cvt(dst_hi, dst_lo, src_f32, n, evsrc, extra_wait=None):
            p.wait("A", evsrc)
            if extra_wait is not None:
                p.wait("A", extra_wait)
            p.op("A", lambda e: e.copy(dst_hi, src_f32))
            ev_h = p.inc(chA)
            p.wait("V", ev_h)
            p.op("V", lambda e: e.tensor_tensor(dst_lo, src_f32, dst_hi,
                                                ALU.subtract))
            ev_l = p.inc(chV)
            return ev_h, ev_l

        def mm3(out_ap, lh, ll, rh, rl, start, stop):
            p.op("T", lambda e: e.matmul(out_ap, lh, rh, start=start, stop=False))
            p.op("T", lambda e: e.matmul(out_ap, lh, rl, start=False, stop=False))
            p.op("T", lambda e: e.matmul(out_ap, ll, rh, start=False, stop=stop))

        p.op("S", lambda e: e.dma_start(sid[:, :], idn[:, :]))
        evI = p.inc(chM)
        p.op("S", lambda e: e.dma_start(E2[:, :], e2d[:, :]))
        p.inc(chM)
        p.op("S", lambda e: e.dma_start(
            gstg[:, :].rearrange("p (c g) -> p c g", c=DC),
            gw[:, :].rearrange("(c dp) g -> dp c g", dp=128)))
        p.inc(chM)
        p.op("S", lambda e: e.dma_start(gbsb[:, :], gb[:, :]))
        p.inc(chM)
        evM0 = p.now(chM)

        p.op("V", lambda e: e.memset(cns[:, 0:1], -16.0))
        p.op("V", lambda e: e.memset(cns[:, 1:2], 1e-5))
        p.op("V", lambda e: e.memset(ones1[:, :], 1.0))
        p.op("V", lambda e: e.memset(den[:, :], 0.0))
        evZ = p.inc(chV)

        cvt(gwh[:, :], gwl[:, :], gstg[:, :], DC * NE, evM0)
        p.wait("A", evI)
        p.op("A", lambda e: e.copy(sidb[:, :], sid[:, :]))
        evIb = p.inc(chA)

        p.wait("T", evM0)
        p.wait("T", evZ)
        p.op("T", lambda e: e.matmul(psW[1][:, 0:NE], ones1[0:1, :],
                                     gbsb[0:1, :], start=True, stop=True))
        evT = p.inc(chT)
        p.wait("A", evT)
        p.op("A", lambda e: e.copy(gbr[:, :], psW[1][:, 0:NE]))
        evA_gbr = p.inc(chA)
        ps_read(psW[1], evA_gbr)

        lnsc_free = {}

        def ln_tile(xap, dst_writer, lnsc, x_ev):
            p.wait("V", x_ev)
            p.wait("V", lnsc_free.get(lnsc.name))
            p.op("V", lambda e: e.tensor_mul(lnsc[:, :], xap, xap))
            p.op("V", lambda e: e.reduce_sum(stats[:, 0:1], xap, axis=AX))
            p.op("V", lambda e: e.reduce_sum(stats[:, 1:2], lnsc[:, :], axis=AX))
            p.op("V", lambda e: e.tensor_scalar(stats[:, 2:3], stats[:, 0:1],
                                                1.0 / D, None, ALU.mult))
            p.op("V", lambda e: e.tensor_mul(stats[:, 3:4], stats[:, 2:3],
                                             stats[:, 2:3]))
            p.op("V", lambda e: e.scalar_tensor_tensor(
                stats[:, 4:5], stats[:, 1:2], 1.0 / D, stats[:, 3:4],
                ALU.mult, ALU.subtract))
            p.wait("V", evZ)
            p.op("V", lambda e: e.tensor_add(stats[:, 7:8], stats[:, 4:5],
                                             cns[:, 1:2]))
            p.op("V", lambda e: e.reciprocal(stats[:, 8:9], stats[:, 7:8]))
            evv = p.inc(chV)
            p.wait("A", evv)
            p.op("A", lambda e: e.activation(stats[:, 5:6], stats[:, 8:9],
                                             AF.Sqrt))
            eva = p.inc(chA)
            p.wait("V", eva)
            p.op("V", lambda e: e.scalar_tensor_tensor(
                stats[:, 6:7], stats[:, 2:3], -1.0, stats[:, 5:6],
                ALU.mult, ALU.mult))
            p.op("V", lambda e: e.tensor_scalar(lnsc[:, :], xap, stats[:, 5:6],
                                                stats[:, 6:7], ALU.mult, ALU.add))
            evln = p.inc(chV)
            p.wait("T", evln)
            p.wait("T", evI)
            for r0 in range(0, DC, 4):
                hf = (r0 // 4) % 2
                pst = psW[2]
                chunks = list(range(r0, min(r0 + 4, DC)))
                ps_wait("T", pst, (hf,))
                for j, c in enumerate(chunks):
                    p.op("T", lambda e, c=c, j=j, pst=pst, hf=hf: e.transpose(
                        pst[:, hf * 512 + j * 128:hf * 512 + (j + 1) * 128],
                        lnsc[:, c * 128:(c + 1) * 128], sid[:, :]))
                evt = p.inc(chT)
                evr = dst_writer(chunks[0], pst, len(chunks), evt, hf)
                ps_read(pst, evr, (hf,))
            lnsc_free[lnsc.name] = p.now(chT)
            return evln

        HB = QB // 2
        den = sb("den", [128, HB], side="right")
        rdbuf = sb("rdbuf", [128, 2 * QB], side="right")

        with ExitStack() as es1:
            big1 = es1.enter_context(nc.sbuf_tensor("big1", [128, NP * QH], F32))
            p.op("V", lambda e: e.memset(big1[:, :], 0.0))
            evZ2 = p.inc(chV)

            with ExitStack() as esab:
                def sba(name, shape, dt=F32):
                    return esab.enter_context(nc.sbuf_tensor(name, shape, dt))

                xst = sba("xst", [128, D])
                q_h = sba("q_h", [128, NP * QH], BF16)
                q_l = sba("q_l", [128, NP * QH], BF16)
                ln_h = sba("ln_h", [128, DC * MB], BF16)
                ln_l = sba("ln_l", [128, DC * MB], BF16)
                wstg = sba("wstg", [128, DC * 128])
                wkh = sba("wkh", [128, DC * 128], BF16)
                wkl = sba("wkl", [128, DC * 128], BF16)
                wvh = sba("wvh", [128, DC * 128], BF16)
                wvl = sba("wvl", [128, DC * 128], BF16)
                kh2 = [[sba("kh%d%d" % (g, s), [128, MB], BF16)
                        for s in range(2)] for g in range(2)]
                kl2 = [[sba("kl%d%d" % (g, s), [128, MB], BF16)
                        for s in range(2)] for g in range(2)]
                vstg = sba("vstg", [128, MB])
                VS = 68
                vah2 = [[sba("vah%d%d" % (g, s), [128, MC * VS], BF16)
                         for s in range(2)] for g in range(2)]
                val2 = [[sba("val%d%d" % (g, s), [128, MC * VS], BF16)
                         for s in range(2)] for g in range(2)]
                ptf = [sba("ptf%d" % i, [128, QB]) for i in range(2)]
                pth = [[sba("pth%d%d" % (s, j), [128, QB], BF16)
                        for j in range(2)] for s in range(2)]
                ptl = [[sba("ptl%d%d" % (s, j), [128, QB], BF16)
                        for j in range(2)] for s in range(2)]
                stage = [sba("stg%d" % i, [128, 2 * QB]) for i in range(2)]
                lnsc = sba("lnsc_ab", [128, D])

                for g in range(2):
                    for s in range(2):
                        p.op("V", lambda e, g=g, s=s: e.memset(
                            vah2[g][s][:, :], 1.0))
                        p.op("V", lambda e, g=g, s=s: e.memset(
                            val2[g][s][:, :], 0.0))
                        rows0 = slice(64, 128) if s == 0 else slice(0, 64)
                        p.op("V", lambda e, g=g, s=s, rows0=rows0: e.memset(
                            kh2[g][s][rows0, :], 0.0))
                        p.op("V", lambda e, g=g, s=s, rows0=rows0: e.memset(
                            kl2[g][s][rows0, :], 0.0))
                ev_va_init = p.inc(chV)

                xst_free = [None, None]
                wstg_free = [None]
                stage_free = [[], []]
                lnwin_free = []
                q_ready = None
                ptf_free = [None, None]
                pthl_free = [[None, None], [None, None]]

                def load_x_tile(r0, slot_i):
                    p.wait("S", xst_free[0])
                    p.op("S", lambda e, r0=r0: e.dma_start(
                        xst[:, :], xin[r0:r0 + 128, :]))
                    return p.inc(chX[0])

                def ln_window_cbs(row0, nw_tiles):
                    evs = []

                    def mk(tt):
                        def cb():
                            if tt == 0 and lnwin_free:
                                p.wait("A", lnwin_free[-1])
                                p.wait("V", lnwin_free[-1])
                                del lnwin_free[:]
                            evx = load_x_tile(row0 + tt * 128, 0)
                            xap = xst[:, :]

                            def wr(c0, pst, n, evt, hf, tt=tt):
                                hi = bass.AP(ln_h, c0 * MB + tt * 128,
                                             [[DC * MB, 128], [MB, n], [1, 128]])
                                lo = bass.AP(ln_l, c0 * MB + tt * 128,
                                             [[DC * MB, 128], [MB, n], [1, 128]])
                                _, evl = cvt(hi, lo,
                                             pst[:, hf * 512:hf * 512 + n * 128],
                                             n * 128, evt)
                                return evl

                            evln = ln_tile(xap, wr, lnsc, evx)
                            xst_free[0] = evln
                            evs.append(p.now(chV))
                        return cb

                    return evs, [mk(tt) for tt in range(nw_tiles)]

                def ln_window(row0, nw_tiles):
                    evs, cbs = ln_window_cbs(row0, nw_tiles)
                    for cb in cbs:
                        cb()
                    return evs

                def load_w_slice(dram, pcol, slot_i):
                    p.wait("S", wstg_free[0])
                    src = dram[:, pcol * 128:(pcol + 1) * 128].rearrange(
                        "(c dp) e -> dp c e", dp=128)
                    dstv = wstg[:, :].rearrange("p (c e) -> p c e", c=DC)
                    p.op("S", lambda e, src=src, dstv=dstv: e.dma_start(dstv, src))
                    return p.inc(chW[slot_i])

                def gen_mm(pst, wh, wl, w_evl, lnwin_evs, base=0):
                    ps_wait("T", pst, (base // 512,) if MB <= 512 else (0, 1))
                    p.wait("T", w_evl)
                    p.waits("T", lnwin_evs)
                    for c in range(DC):
                        for h in range(MBH):
                            o0 = h * OH
                            w0 = min(OH, MB)
                            mm3(pst[:, base + o0:base + o0 + w0],
                                wh[:, c * 128:(c + 1) * 128],
                                wl[:, c * 128:(c + 1) * 128],
                                bass.AP(ln_h, c * MB + o0,
                                        [[DC * MB, 128], [1, w0]]),
                                bass.AP(ln_l, c * MB + o0,
                                        [[DC * MB, 128], [1, w0]]),
                                start=(c == 0), stop=(c == DC - 1))
                    ev = p.inc(chT)
                    lnwin_free.append(ev)
                    return ev

                for w in range(NQW):
                    lnev = ln_window(w * MB, MB // 128)
                    for pi in range(NP):
                        slot_i = pi % 2
                        wev = load_w_slice(wq, pi, slot_i)
                        wh, wl = (wkh, wkl) if slot_i == 0 else (wvh, wvl)
                        _, wevl = cvt(wh[:, :], wl[:, :], wstg[:, :], DC * 128,
                                      wev)
                        wstg_free[0] = wevl
                        pst = psW[slot_i]
                        evg = gen_mm(pst, wh, wl, wevl, lnev)
                        _, evql = cvt(
                            q_h[:, pi * QH + w * MB: pi * QH + w * MB + MB],
                            q_l[:, pi * QH + w * MB: pi * QH + w * MB + MB],
                            pst[:, 0:MB], MB, evg)
                        ps_read(pst, evql, (0,) if MB <= 512 else (0, 1))
                q_ready = p.now(chV)

                pair_acc_ev = [[None, None] for _ in range(NP)]
                evkl_set = [None, None]
                ev_va_set = [[None, None], [None, None]]
                kread_free = [[], []]
                psT_tr_free = [[], []]

                def make_gen_cbs(mb, pi, seti, lnev):
                    st = {}
                    kh, kl = kh2[seti], kl2[seti]
                    vah, val = vah2[seti], val2[seti]

                    def cb_kload():
                        st["wkev"] = load_w_slice(wk, pi, 0)

                    def cb_kcvt():
                        _, st["wkevl"] = cvt(wkh[:, :], wkl[:, :], wstg[:, :],
                                             DC * 128, st["wkev"])
                        wstg_free[0] = st["wkevl"]

                    def cb_kgen():
                        st["evk"] = gen_mm(psW[2], wkh, wkl, st["wkevl"],
                                           lnev, base=0)

                    def cb_kevac():
                        p.waits("A", kread_free[seti])
                        p.waits("V", kread_free[seti])
                        _, ev1 = cvt(kh[0][0:64, :], kl[0][0:64, :],
                                     psW[2][0:64, 0:MB], MB, st["evk"])
                        _, evkl = cvt(kh[1][64:128, :], kl[1][64:128, :],
                                      psW[2][64:128, 0:MB], MB, st["evk"])
                        ps_read(psW[2], evkl, (0,))
                        evkl_set[seti] = evkl

                    def cb_vload():
                        st["wvev"] = load_w_slice(wv, pi, 1)

                    def cb_vcvt():
                        _, st["wvevl"] = cvt(wvh[:, :], wvl[:, :], wstg[:, :],
                                             DC * 128, st["wvev"])
                        wstg_free[0] = st["wvevl"]

                    def cb_vgen():
                        st["evv"] = gen_mm(psW[2], wvh, wvl, st["wvevl"],
                                           lnev, base=512)

                    def cb_vevac():
                        p.wait("A", st["evv"])
                        p.op("A", lambda e: e.copy(vstg[:, :],
                                                   psW[2][:, 512:512 + MB]))
                        st["evvs"] = p.inc(chA)
                        ps_read(psW[2], st["evvs"], (1,))

                    def mk_vtrans(s):
                        def cb():
                            off = s * 512
                            ps_wait("T", psW[2], (s,))
                            p.wait("T", st["evvs"])
                            p.waits("T", psT_tr_free[s])
                            for c in range(MC):
                                p.op("T", lambda e, s=s, c=c, off=off:
                                     e.transpose(
                                         psW[2][:, off + c * 64:
                                                off + (c + 1) * 64],
                                         vstg[s * 64:(s + 1) * 64,
                                              c * 128:(c + 1) * 128],
                                         sid[s * 64:(s + 1) * 64,
                                             s * 64:(s + 1) * 64]))
                            evt = p.inc(chT)
                            p.waits("A", kread_free[seti])
                            p.waits("V", kread_free[seti])
                            hiap = bass.AP(vah[s], 0,
                                           [[MC * VS, 128], [VS, MC], [1, 64]])
                            loap = bass.AP(val[s], 0,
                                           [[MC * VS, 128], [VS, MC], [1, 64]])
                            _, eva = cvt(hiap, loap,
                                         psW[2][:, off:off + MC * 64],
                                         MC * 64, evt)
                            ps_read(psW[2], eva, (s,))
                            ev_va_set[seti][s] = eva
                        return cb

                    return [cb_kload, cb_kcvt, cb_kgen, cb_kevac,
                            cb_vload, cb_vcvt, cb_vgen, cb_vevac,
                            mk_vtrans(0), mk_vtrans(1)]

                units = [(mb, pi) for mb in range(NB) for pi in range(NP)]
                lnev_cur = None
                lnev_next = None
                ln_next_cbs = []
                for ui, (mb, pi) in enumerate(units):
                    seti = ui % 2
                    if pi == 0:
                        if mb == 0 or lnev_next is None:
                            lnev_cur = ln_window(mb * MB, MB // 128)
                            for cb in make_gen_cbs(mb, 0, seti, lnev_cur):
                                cb()
                        else:
                            lnev_cur = lnev_next
                    evkl = evkl_set[seti]
                    ev_va = ev_va_set[seti]
                    k_h, k_l = kh2[seti], kl2[seti]
                    vah, val = vah2[seti], val2[seti]
                    if pi + 1 < NP:
                        gen_cbs = make_gen_cbs(mb, pi + 1, (ui + 1) % 2,
                                               lnev_cur)
                        if pi + 2 == NP and mb + 1 < NB and NP >= 2:
                            lnev_next, cbs4 = ln_window_cbs((mb + 1) * MB,
                                                            MB // 128)
                            nh = len(cbs4) // 2
                            gen_cbs = gen_cbs + cbs4[:nh]
                            ln_next_cbs = cbs4[nh:]
                    elif mb + 1 < NB and lnev_next is not None:
                        gen_cbs = ln_next_cbs + make_gen_cbs(mb + 1, 0,
                                                             (ui + 1) % 2,
                                                             lnev_next)
                        ln_next_cbs = []
                    else:
                        gen_cbs = []

                    pv_done = [None, None]

                    def emit_S_chain(i, qb, c, s):
                        hs = qb * MC + c
                        par = hs % 2
                        gpar = i % 2
                        off = par * 512
                        pss = psW[s]
                        q0 = pi * QH + qb * QB
                        ps_wait("T", pss, (par,))
                        p.wait("T", evkl)
                        p.wait("T", q_ready)
                        mm3(pss[:, off:off + QB],
                            k_h[s][:, c * 128:(c + 1) * 128],
                            k_l[s][:, c * 128:(c + 1) * 128],
                            q_h[:, q0:q0 + QB],
                            q_l[:, q0:q0 + QB],
                            start=True, stop=True)
                        evs_mm = p.inc(chT)
                        p.wait("A", evs_mm)
                        p.wait("A", evZ)
                        p.wait("A", ptf_free[gpar])
                        p.op("A", lambda e, gpar=gpar, pss=pss, off=off:
                             e.activation(ptf[gpar][:, :],
                                          pss[:, off:off + QB], AF.Exp,
                                          bias=cns[:, 0:1], scale=0.125))
                        ev_exp = p.inc(chA)
                        ps_read(pss, ev_exp, (par,))
                        p.wait("V", ev_exp)
                        p.wait("V", pthl_free[s][par])
                        p.op("V", lambda e, s=s, par=par, gpar=gpar:
                             e.tensor_copy(pth[s][par][:, :],
                                           ptf[gpar][:, :]))
                        p.op("V", lambda e, s=s, par=par, gpar=gpar:
                             e.tensor_tensor(ptl[s][par][:, :],
                                             ptf[gpar][:, :],
                                             pth[s][par][:, :],
                                             ALU.subtract))
                        ev_l = p.inc(chV)
                        ptf_free[gpar] = ev_l
                        return ev_l, par

                    def flush_slot(qb, s, ev_pv):
                        sg = qb % 2
                        chGs = (chG, chG1)[sg]
                        ce = "A" if s == 0 else "V"
                        p.waits(ce, stage_free[sg])
                        p.wait(ce, ev_pv)
                        if s == 0:
                            p.op("A", lambda e, sg=sg: e.copy(
                                stage[sg][0:65, 0:QB], psPV[0][0:65, 0:QB]))
                            evc = p.inc(chA)
                        else:
                            p.op("V", lambda e, sg=sg: e.tensor_copy(
                                stage[sg][0:65, QB:2 * QB], psPV[1][0:65, 0:QB]))
                            evc = p.inc(chV)
                        ps_read(psPV[s], evc, (0,))
                        p.wait("G", evc)
                        p.wait("G", ev_va_init)
                        p.wait("G", evZ2)
                        r0, r1 = (0, 64) if s == 0 else (64, 128)
                        p.op("G", lambda e, pi=pi, qb=qb, sg=sg, s=s,
                             r0=r0, r1=r1: e.dma_start(
                                 big1[r0:r1, pi * QH + qb * QB:
                                      pi * QH + (qb + 1) * QB],
                                 stage[sg][0:64, s * QB:(s + 1) * QB],
                                 accum_op=ALU.add))
                        p.inc(chGs)
                        hh0 = pi if s == 0 else pi + NP
                        p.op("G", lambda e, qb=qb, sg=sg, s=s, hh0=hh0:
                             e.dma_start(
                                 bass.AP(den, (hh0 * NQB + qb) * 2 * HB,
                                         [[HB, 2], [1, HB]]),
                                 stage[sg][64:65, s * QB:(s + 1) * QB],
                                 accum_op=ALU.add))
                        evg = p.inc(chGs)
                        if s == 1:
                            stage_free[sg] = [evg]
                            pair_acc_ev[pi][sg] = evg

                    def emit_PV(st2, chain_ev, par):
                        qb, c, s = st2
                        if c == 0:
                            ps_wait("T", psPV[s], (0,))
                        p.wait("T", chain_ev)
                        p.wait("T", ev_va[s])
                        mm3(psPV[s][0:65, 0:QB],
                            vah[s][:, c * VS:c * VS + 65],
                            val[s][:, c * VS:c * VS + 65],
                            pth[s][par][:, :],
                            ptl[s][par][:, :],
                            start=(c == 0), stop=(c == MC - 1))
                        ev_pv = p.inc(chT)
                        pthl_free[s][par] = ev_pv
                        if c == MC - 1:
                            pv_done[s] = ev_pv
                            flush_slot(qb, s, ev_pv)

                    steps = [(qb, c, s) for qb in range(NQB)
                             for c in range(MC) for s in range(2)]
                    pend = []
                    for i, st2 in enumerate(steps):
                        ch_ev, par = emit_S_chain(i, *st2)
                        pend.append((st2, ch_ev, par))
                        if len(pend) > 3:
                            emit_PV(*pend.pop(0))
                        if gen_cbs and i >= 2 and (i - 2) % 3 == 0:
                            gen_cbs.pop(0)()
                    while gen_cbs:
                        gen_cbs.pop(0)()
                    while pend:
                        emit_PV(*pend.pop(0))
                    kread_free[seti] = [p.now(chT)]
                    psT_tr_free[0] = [p.now(chT)]
                    psT_tr_free[1] = [p.now(chT)]

                bar_pre = [p.now(chT), p.now(chA), p.now(chV), p.now(chG),
                           p.now(chG1)]
                for pe in pair_acc_ev:
                    p.waits("V", pe)
                p.op("V", lambda e: e.reciprocal(den[:, :], den[:, :]))
                ev_rcp = p.inc(chV)

            out_acc = sb("out_acc", [128, NT * D], side="right")
            with ExitStack() as espd:
                def sbp(name, shape, dt=F32):
                    return espd.enter_context(nc.sbuf_tensor(name, shape, dt))

                wph = sbp("wph", [128, DC * D], BF16)
                wpl = sbp("wpl", [128, DC * D], BF16)
                wpstg = sbp("wpstg", [128, 2 * D])
                bpr = sbp("bpr", [128, D])
                bpsb = sbp("bpsb", [1, D])
                xst2 = sbp("xst2", [128, 2 * D])
                fch = [sbp("fch%d" % i, [128, 128], BF16) for i in range(4)]
                fcl = [sbp("fcl%d" % i, [128, 128], BF16) for i in range(4)]
                rdb2 = sbp("rdb2", [128, 2 * QB])

                p.waits("S", bar_pre)
                ev_mult_qb = [None] * NQB
                cslot_free = [None] * 4
                citer = [(qb, pi) for qb in range(NQB) for pi in range(NP)]
                cpos = [0]

                def emit_c(nmax):
                    while cpos[0] < len(citer) and nmax > 0:
                        i = cpos[0]
                        qb, pi = citer[i]
                        cpos[0] += 1
                        nmax -= 1
                        sl = i % 4
                        hf = i % 2
                        buf = rdbuf if sl < 2 else rdb2
                        c0 = (sl % 2) * QB
                        p.wait("S", ev_rcp)
                        p.wait("S", cslot_free[sl])
                        p.op("S", lambda e, pi=pi, qb=qb, buf=buf, c0=c0:
                             e.dma_start(
                                 buf[0:1, c0:c0 + QB],
                                 bass.AP(den, (pi * NQB + qb) * 2 * HB,
                                         [[HB, 2], [1, HB]])))
                        p.inc(chC[sl])
                        p.op("S", lambda e, pi=pi, qb=qb, buf=buf, c0=c0:
                             e.dma_start(
                                 buf[1:2, c0:c0 + QB],
                                 bass.AP(den, ((pi + NP) * NQB + qb) * 2 * HB,
                                         [[HB, 2], [1, HB]])))
                        evd = p.inc(chC[sl])
                        ps_wait("T", psW[2], (hf,))
                        p.wait("T", evd)
                        p.wait("T", evM0)
                        p.op("T", lambda e, buf=buf, c0=c0, hf=hf: e.matmul(
                            psW[2][:, hf * QB:(hf + 1) * QB], E2[0:2, :],
                            buf[0:2, c0:c0 + QB], start=True, stop=True))
                        ev_rep = p.inc(chT)
                        cslot_free[sl] = ev_rep
                        p.wait("V", ev_rep)
                        p.op("V", lambda e, pi=pi, qb=qb, hf=hf: e.tensor_mul(
                            big1[:, pi * QH + qb * QB:
                                 pi * QH + (qb + 1) * QB],
                            big1[:, pi * QH + qb * QB:
                                 pi * QH + (qb + 1) * QB],
                            psW[2][:, hf * QB:(hf + 1) * QB]))
                        ev_mult = p.inc(chV)
                        ps_read(psW[2], ev_mult, (hf,))
                        if pi == NP - 1:
                            ev_mult_qb[qb] = ev_mult

                wps_free = [None, None]
                for c in range(DC):
                    sl = c % 2
                    p.wait("S", wps_free[sl])
                    p.op("S", lambda e, c=c, sl=sl: e.dma_start(
                        wpstg[:, sl * D:(sl + 1) * D],
                        wp[c * 128:(c + 1) * 128, :]))
                    evw = p.inc(chM if sl == 0 else chW1)
                    _, evwl = cvt(wph[:, c * D:(c + 1) * D],
                                  wpl[:, c * D:(c + 1) * D],
                                  wpstg[:, sl * D:(sl + 1) * D], D, evw)
                    wps_free[sl] = evwl
                    emit_c(max(1, len(citer) // DC))
                emit_c(len(citer))
                ev_wp = evwl
                p.op("S", lambda e: e.dma_start(bpsb[:, :], bp[:, :]))
                evbp = p.inc(chW0)
                p.wait("T", evbp)
                p.wait("T", evZ)
                for hh in range(NHALF):
                    pst = psW[hh % 2]
                    ps_wait("T", pst)
                    p.op("T", lambda e, hh=hh, pst=pst: e.matmul(
                        pst[:, 0:OH], ones1[0:1, :],
                        bpsb[0:1, hh * OH:(hh + 1) * OH], start=True,
                        stop=True))
                    evt = p.inc(chT)
                    p.wait("A", evt)
                    p.op("A", lambda e, hh=hh, pst=pst: e.copy(
                        bpr[:, hh * OH:(hh + 1) * OH], pst[:, 0:OH]))
                    eva = p.inc(chA)
                    ps_read(pst, eva)
                ev_bpr = p.now(chA)

                xst2_free = [None, None]
                fc_free = [None] * 4
                evf_q = []
                NGC = NT * DC

                def emit_fcvt(gc):
                    t2, c2 = gc // DC, gc % DC
                    sl = gc % 4
                    p.wait("A", fc_free[sl])
                    _, evf = cvt(fch[sl][:, :], fcl[sl][:, :],
                                 big1[:, c2 * QH + t2 * 128:
                                      c2 * QH + t2 * 128 + 128],
                                 128, ev_mult_qb[t2 // (QB // 128)])
                    evf_q.append((sl, evf))

                for gc in range(min(2, NGC)):
                    emit_fcvt(gc)
                for t in range(NT):
                    pst = psW[t % 2]
                    ps_wait("T", pst)
                    p.wait("T", ev_wp)
                    for c in range(DC):
                        gc = t * DC + c
                        if gc + 2 < NGC:
                            emit_fcvt(gc + 2)
                        sl, evf = evf_q.pop(0)
                        p.wait("T", evf)
                        for hh in range(NHALF):
                            mm3(pst[:, hh * OH:(hh + 1) * OH],
                                fch[sl][:, :], fcl[sl][:, :],
                                wph[:, c * D + hh * OH: c * D + (hh + 1) * OH],
                                wpl[:, c * D + hh * OH: c * D + (hh + 1) * OH],
                                start=(c == 0), stop=(c == DC - 1))
                        evt = p.inc(chT)
                        fc_free[sl] = evt
                    ev_proj = p.now(chT)
                    sl2 = t % 2
                    p.wait("S", xst2_free[sl2])
                    p.op("S", lambda e, t=t, sl2=sl2: e.dma_start(
                        xst2[:, sl2 * D:(sl2 + 1) * D],
                        xin[t * 128:(t + 1) * 128, :]))
                    evx = p.inc(chX[sl2])
                    oa = out_acc[:, t * D:(t + 1) * D]
                    p.wait("V", ev_proj)
                    p.wait("V", ev_bpr)
                    p.op("V", lambda e, oa=oa, pst=pst: e.tensor_add(
                        oa, pst[0:128, 0:D], bpr[:, :]))
                    ev1 = p.inc(chV)
                    ps_read(pst, ev1)
                    p.wait("V", evx)
                    p.op("V", lambda e, oa=oa, sl2=sl2: e.tensor_add(
                        oa, oa, xst2[:, sl2 * D:(sl2 + 1) * D]))
                    ev_x2 = p.inc(chV)
                    xst2_free[sl2] = ev_x2
                x2_done_V = p.now(chV)
                x2_done_T = p.now(chT)

        with ExitStack() as es2:
            def sb2(name, shape, dt=F32):
                return es2.enter_context(nc.sbuf_tensor(name, shape, dt))

            ln2h = sb2("ln2h", [128, NP * QH], BF16)
            ln2l = sb2("ln2l", [128, NP * QH], BF16)
            besb = sb2("besb", [128, D])
            lnsc2 = sb2("lnsc_d", [128, D])

            p.waits("S", [x2_done_V, x2_done_T])
            p.wait("V", x2_done_T)
            p.wait("A", x2_done_T)
            p.op("S", lambda e: e.dma_start(besb[0:NE, :], be[:, :]))
            ev_be = p.inc(chM)

            espe = es2.enter_context(ExitStack())

            def sbe(name, shape, dt=F32):
                return espe.enter_context(nc.sbuf_tensor(name, shape, dt))

            estg = sbe("estg", [128, DC * OH])
            weh = [sbe("weh%d" % i, [128, DC * OH], BF16) for i in range(2)]
            wel = [sbe("wel%d" % i, [128, DC * OH], BF16) for i in range(2)]
            tasks = [(ei, hh) for ei in range(NE) for hh in range(NHALF)]
            web_free = [None, None]
            estg_free = [None]
            ev_we = [None, None]
            ev_ln2_t = [None] * NT

            def load_task(ti, sl):
                ei, hh = tasks[ti]
                p.wait("S", estg_free[0])
                src3 = we[ei, :, :].rearrange("(c dp) o -> dp c o", dp=128)
                src3 = src3[:, :, hh * OH:(hh + 1) * OH]
                p.op("S", lambda e, src3=src3: e.dma_start(
                    estg[:, :].rearrange("p (c o) -> p c o", c=DC), src3))
                evl = p.inc(chE[sl])
                p.waits("A", [web_free[sl]])
                p.waits("V", [web_free[sl]])
                _, evcl = cvt(weh[sl][:, :], wel[sl][:, :], estg[:, :],
                              DC * OH, evl)
                estg_free[0] = evcl
                ev_we[sl] = evcl

            def emit_task_tile(ti, sl, t, ev_ln2_dep):
                ei, hh = tasks[ti]
                pst = psW[t % 3]
                ps_wait("T", pst)
                p.wait("T", ev_we[sl])
                p.wait("T", ev_ln2_dep)
                for c in range(DC):
                    mm3(pst[:, 0:OH],
                        ln2h[:, c * QH + t * 128: c * QH + t * 128 + 128],
                        ln2l[:, c * QH + t * 128: c * QH + t * 128 + 128],
                        weh[sl][:, c * OH:(c + 1) * OH],
                        wel[sl][:, c * OH:(c + 1) * OH],
                        start=(c == 0), stop=(c == DC - 1))
                ev_y = p.inc(chT)
                p.wait("V", ev_y)
                p.op("V", lambda e, hh=hh, t=t, ei=ei, pst=pst:
                     e.scalar_tensor_tensor(
                         out_acc[:, t * D + hh * OH: t * D + (hh + 1) * OH],
                         pst[:, 0:OH],
                         s_all[:, t * NE + ei:t * NE + ei + 1],
                         out_acc[:, t * D + hh * OH: t * D + (hh + 1) * OH],
                         ALU.mult, ALU.add))
                ev_o = p.inc(chV)
                ps_read(pst, ev_o)
                if t == NT - 1:
                    web_free[sl] = p.now(chT)
                if ei == NE - 1:
                    p.wait("S", ev_o)
                    p.op("S", lambda e, t=t, hh=hh: e.dma_start(
                        y[t * 128:(t + 1) * 128, hh * OH:(hh + 1) * OH],
                        out_acc[:, t * D + hh * OH: t * D + (hh + 1) * OH]))
                    p.inc(chO)

            load_task(0, 0)
            for t in range(NT):
                oa = out_acc[:, t * D:(t + 1) * D]

                def wr2(c0, pst, n, evt, hf, t=t):
                    hi = bass.AP(ln2h, c0 * QH + t * 128,
                                 [[NP * QH, 128], [QH, n], [1, 128]])
                    lo = bass.AP(ln2l, c0 * QH + t * 128,
                                 [[NP * QH, 128], [QH, n], [1, 128]])
                    _, evl = cvt(hi, lo,
                                 pst[:, hf * 512:hf * 512 + n * 128],
                                 n * 128, evt)
                    return evl

                ln_tile(oa, wr2, lnsc2, x2_done_V)
                ev_ln2 = p.now(chV)
                pst = psW[1]
                ps_wait("T", pst)
                p.wait("T", ev_ln2)
                p.wait("T", p.now(chA))
                for c in range(DC):
                    mm3(pst[:, 0:NE],
                        ln2h[:, c * QH + t * 128: c * QH + t * 128 + 128],
                        ln2l[:, c * QH + t * 128: c * QH + t * 128 + 128],
                        gwh[:, c * NE:(c + 1) * NE],
                        gwl[:, c * NE:(c + 1) * NE],
                        start=(c == 0), stop=(c == DC - 1))
                ev_gmm = p.inc(chT)
                z0, e0, g0, sl0, w0, sc0 = 0, NE, 2 * NE, 3 * NE, 4 * NE, 6 * NE
                p.wait("V", ev_gmm)
                p.wait("V", evA_gbr)
                p.op("V", lambda e, pst=pst: e.tensor_add(
                    gsc[:, z0:z0 + NE], pst[:, 0:NE], gbr[:, :]))
                ev_z = p.inc(chV)
                ps_read(pst, ev_z)
                p.op("V", lambda e: e.reduce_max(gsc[:, sc0:sc0 + 1],
                                                 gsc[:, z0:z0 + NE], axis=AX,
                                                 negate=True))
                ev_nm = p.inc(chV)
                p.wait("A", ev_nm)
                p.op("A", lambda e: e.activation(gsc[:, e0:e0 + NE],
                                                 gsc[:, z0:z0 + NE], AF.Exp,
                                                 bias=gsc[:, sc0:sc0 + 1],
                                                 scale=1.0))
                ev_ge = p.inc(chA)
                p.wait("V", ev_ge)
                p.op("V", lambda e: e.reduce_sum(gsc[:, sc0 + 1:sc0 + 2],
                                                 gsc[:, e0:e0 + NE], axis=AX))
                p.op("V", lambda e: e.reciprocal(gsc[:, sc0 + 2:sc0 + 3],
                                                 gsc[:, sc0 + 1:sc0 + 2]))
                p.op("V", lambda e: e.tensor_scalar(gsc[:, g0:g0 + NE],
                                                    gsc[:, e0:e0 + NE],
                                                    gsc[:, sc0 + 2:sc0 + 3],
                                                    None, ALU.mult))
                p.op("V", lambda e: e.tensor_single_scalar(
                    gsc[:, sl0:sl0 + NE], gsc[:, g0:g0 + NE], 0.1, ALU.is_ge))
                p.op("V", lambda e: e.tensor_mul(gsc[:, w0:w0 + NE],
                                                 gsc[:, g0:g0 + NE],
                                                 gsc[:, sl0:sl0 + NE]))
                p.op("V", lambda e: e.reduce_sum(gsc[:, sc0 + 3:sc0 + 4],
                                                 gsc[:, w0:w0 + NE], axis=AX))
                p.op("V", lambda e: e.reciprocal(gsc[:, sc0 + 4:sc0 + 5],
                                                 gsc[:, sc0 + 3:sc0 + 4]))
                p.op("V", lambda e, t=t: e.tensor_scalar(
                    s_all[:, t * NE:(t + 1) * NE], gsc[:, w0:w0 + NE],
                    gsc[:, sc0 + 4:sc0 + 5], None, ALU.mult))
                ev_s = p.inc(chV)
                ps_wait("T", psW[2], (0,))
                p.wait("T", ev_s)
                p.op("T", lambda e, t=t: e.transpose(
                    psW[2][0:NE, 0:128], s_all[:, t * NE:(t + 1) * NE],
                    sid[:, :]))
                ev_st = p.inc(chT)
                p.wait("A", ev_st)
                p.op("A", lambda e: e.copy(sTb[0:NE, :], psW[2][0:NE, 0:128]))
                ev_stc = p.inc(chA)
                ps_read(psW[2], ev_stc)
                p.wait("T", ev_stc)
                p.wait("T", ev_be)
                for hh in range(NHALF):
                    pst2 = psPV[hh % 2]
                    ps_wait("T", pst2, (0,))
                    p.op("T", lambda e, hh=hh, pst2=pst2: e.matmul(
                        pst2[:, 0:OH], sTb[0:NE, :],
                        besb[0:NE, hh * OH:(hh + 1) * OH],
                        start=True, stop=True))
                    evb = p.inc(chT)
                    p.wait("V", evb)
                    p.op("V", lambda e, hh=hh, t=t, pst2=pst2: e.tensor_add(
                        out_acc[:, t * D + hh * OH: t * D + (hh + 1) * OH],
                        out_acc[:, t * D + hh * OH: t * D + (hh + 1) * OH],
                        pst2[:, 0:OH]))
                    evv = p.inc(chV)
                    ps_read(pst2, evv, (0,))
                ev_ln2_t[t] = ev_ln2
                emit_task_tile(0, 0, t, ev_ln2)
            ln2_all = p.now(chV)

            load_task(1, 1)
            for ti in range(1, len(tasks)):
                sl = ti % 2
                if ti + 1 < len(tasks):
                    load_task(ti + 1, (ti + 1) % 2)
                for t in range(NT):
                    emit_task_tile(ti, sl, t, ln2_all)

        ev_out = p.now(chO)

        for c in p.chans:
            c.sem = es.enter_context(nc.semaphore(c.name))
        block = es.enter_context(nc.Block())
        eng_map = {"T": block.tensor, "A": block.scalar, "V": block.vector,
                   "S": block.sync, "G": block.gpsimd}

        def runner(eng_name):
            def run(engobj):
                last = None
                for it in p.items[eng_name]:
                    if it[0] == "op":
                        r = it[1](engobj)
                        if r is not None:
                            last = r
                    elif it[0] == "inc":
                        last.then_inc(it[1].sem, it[1].step)
                    elif it[0] == "drain":
                        engobj.drain()
                    else:
                        engobj.wait_ge(it[1].sem, it[2])
                if eng_name == "S":
                    engobj.wait_ge(ev_out[0].sem, ev_out[1])
            return run

        for name, deco in eng_map.items():
            deco(runner(name))
    return nc


def _prep_inputs(cfg, L0_attn_query, Wq, Wk, Wv, Wp, bp, gate_w, gate_b,
                 expert_w, expert_b):
    D, H, NSEQ, QH, NE = (cfg["D"], cfg["H"], cfg["NSEQ"], cfg["QH"], cfg["NE"])
    NP = H // 2
    x = np.ascontiguousarray(np.asarray(L0_attn_query, dtype=np.float32))
    B = x.shape[0]
    head_order = []
    for pi in range(NP):
        head_order += [pi, pi + NP]
    wq_h = np.ascontiguousarray(
        np.asarray(Wq, np.float32).transpose(1, 0, 2)[:, head_order, :].reshape(D, D))
    wk_h = np.ascontiguousarray(
        np.asarray(Wk, np.float32).transpose(1, 0, 2)[:, head_order, :].reshape(D, D))
    wv_h = np.ascontiguousarray(
        np.asarray(Wv, np.float32).transpose(1, 0, 2)[:, head_order, :].reshape(D, D))
    idx = np.empty(D, dtype=np.int64)
    for pi in range(NP):
        for s in range(2):
            h = pi + NP * s
            for e in range(64):
                idx[pi * 128 + s * 64 + e] = e * H + h
    wp_h = np.ascontiguousarray(np.asarray(Wp, np.float32)[idx])
    in_maps = []
    ncore = 2 * B
    for c in range(ncore):
        b, half = c // 2, c % 2
        ours = x[b, half * QH:(half + 1) * QH]
        other = x[b, (1 - half) * QH:(2 - half) * QH]
        xkv = np.ascontiguousarray(np.concatenate([ours, other], axis=0))
        in_maps.append({
            "xin": xkv,
            "wq": wq_h, "wk": wk_h, "wv": wv_h, "wp": wp_h,
            "bp": np.asarray(bp, np.float32).reshape(1, D),
            "gw": np.ascontiguousarray(np.asarray(gate_w, np.float32)),
            "gb": np.asarray(gate_b, np.float32).reshape(1, NE),
            "we": np.ascontiguousarray(np.asarray(expert_w, np.float32)),
            "be": np.ascontiguousarray(np.asarray(expert_b, np.float32)),
            "idn": np.eye(128, dtype=np.float32),
            "e2d": np.repeat(np.eye(2, dtype=np.float32), 64, axis=1),
        })
    return in_maps


def kernel(L0_attn_query, L0_attn_Wq, L0_attn_Wk, L0_attn_Wv, L0_attn_Wp,
           L0_attn_bp, L0_ffn_gate_w, L0_ffn_gate_b, L0_ffn_expert_w,
           L0_ffn_expert_b):
    cfg = CFG_FULL
    in_maps = _prep_inputs(cfg, L0_attn_query, L0_attn_Wq, L0_attn_Wk,
                           L0_attn_Wv, L0_attn_Wp, L0_attn_bp, L0_ffn_gate_w,
                           L0_ffn_gate_b, L0_ffn_expert_w, L0_ffn_expert_b)
    nc = build_program(cfg)
    res = run_bass_kernel_spmd(nc, in_maps, core_ids=list(range(len(in_maps))))
    B, NSEQ, D, QH = 4, cfg["NSEQ"], cfg["D"], cfg["QH"]
    out = np.empty((B, NSEQ, D), dtype=np.float32)
    for c, r in enumerate(res.results):
        b, half = c // 2, c % 2
        out[b, half * QH:(half + 1) * QH] = np.asarray(r["y"], dtype=np.float32)
    return out


# version 34 (final)
# speedup vs baseline: 1.1215x; 1.0147x over previous
import numpy as np
from contextlib import ExitStack

import concourse.bass as bass
import concourse.mybir as mybir
from concourse.bass_utils import run_bass_kernel_spmd

F32 = mybir.dt.float32
BF16 = mybir.dt.bfloat16
AF = mybir.ActivationFunctionType
ALU = mybir.AluOpType
AX = mybir.AxisListType.X

ENGS = ("T", "A", "V", "S", "G")

CFG_FULL = dict(D=1024, H=16, NSEQ=4096, QH=2048, MB=512, NE=8, HE=64)
CFG_MINI = dict(D=256, H=4, NSEQ=512, QH=256, MB=256, NE=4, HE=64)


class Chan:
    __slots__ = ("name", "eng", "step", "n", "sem")

    def __init__(self, name, eng, step):
        self.name, self.eng, self.step, self.n, self.sem = name, eng, step, 0, None


class Prog:
    def __init__(self):
        self.items = {e: [] for e in ENGS}
        self.chans = []
        self.lastw = {e: {} for e in ENGS}

    def chan(self, name, eng, dma=False):
        c = Chan(name, eng, 16 if dma else 1)
        self.chans.append(c)
        return c

    def op(self, eng, fn):
        if eng in ("V", "A"):
            self.items[eng].append(("drain",))
        self.items[eng].append(("op", fn))

    def dep(self, eng):
        self.items[eng].append(("drain",))

    def inc(self, ch):
        ch.n += ch.step
        self.items[ch.eng].append(("inc", ch))
        return (ch, ch.n)

    def wait(self, eng, ev):
        if ev is None:
            return
        ch, n = ev
        if n <= 0:
            return
        assert n <= ch.n, (eng, ch.name, n, ch.n)
        if ch.eng == eng and eng in ("T", "A", "V"):
            return
        if self.lastw[eng].get(ch, 0) >= n:
            return
        self.lastw[eng][ch] = n
        self.items[eng].append(("wait", ch, n))

    def waits(self, eng, evs):
        for ev in evs:
            self.wait(eng, ev)

    def now(self, ch):
        return (ch, ch.n)


def build_program(cfg):
    D = cfg["D"]
    H = cfg["H"]
    HE = cfg["HE"]
    NSEQ = cfg["NSEQ"]
    QH = cfg["QH"]
    MB = cfg["MB"]
    NE = cfg["NE"]
    DC = D // 128
    NP = H // 2
    NB = NSEQ // MB
    MC = MB // 128
    NT = QH // 128
    NQW = QH // MB
    QB = min(512, QH)
    NQB = QH // QB
    OH = min(512, D)
    NHALF = D // OH
    MBH = MB // OH if MB >= OH else 1
    assert HE == 64 and D % 128 == 0 and MB % 128 == 0

    nc = bass.Bass(trn_type="TRN2", target_bir_lowering=False)

    xin = nc.dram_tensor("xin", [NSEQ, D], F32, kind="ExternalInput").ap()
    wq = nc.dram_tensor("wq", [D, D], F32, kind="ExternalInput").ap()
    wk = nc.dram_tensor("wk", [D, D], F32, kind="ExternalInput").ap()
    wv = nc.dram_tensor("wv", [D, D], F32, kind="ExternalInput").ap()
    wp = nc.dram_tensor("wp", [D, D], F32, kind="ExternalInput").ap()
    bp = nc.dram_tensor("bp", [1, D], F32, kind="ExternalInput").ap()
    gw = nc.dram_tensor("gw", [D, NE], F32, kind="ExternalInput").ap()
    gb = nc.dram_tensor("gb", [1, NE], F32, kind="ExternalInput").ap()
    we = nc.dram_tensor("we", [NE, D, D], F32, kind="ExternalInput").ap()
    be = nc.dram_tensor("be", [NE, D], F32, kind="ExternalInput").ap()
    idn = nc.dram_tensor("idn", [128, 128], F32, kind="ExternalInput").ap()
    e2d = nc.dram_tensor("e2d", [2, 128], F32, kind="ExternalInput").ap()
    y = nc.dram_tensor("y", [QH, D], F32, kind="ExternalOutput").ap()

    p = Prog()
    chT = p.chan("chT", "T")
    chA = p.chan("chA", "A")
    chV = p.chan("chV", "V")
    chX0 = p.chan("chX0", "S", dma=True)
    chX1 = p.chan("chX1", "S", dma=True)
    chW0 = p.chan("chW0", "S", dma=True)
    chW1 = p.chan("chW1", "S", dma=True)
    chM = p.chan("chM", "S", dma=True)
    chE0 = p.chan("chE0", "S", dma=True)
    chE1 = p.chan("chE1", "S", dma=True)
    chG = p.chan("chG", "G", dma=True)
    chG1 = p.chan("chG1", "G", dma=True)
    chGb = p.chan("chGb", "G", dma=True)
    chC = [p.chan("chC%d" % i, "S", dma=True) for i in range(4)]
    chGc = p.chan("chGc", "G")
    chO = p.chan("chO", "S", dma=True)
    chX = (chX0, chX1)
    chW = (chW0, chW1)
    chE = (chE0, chE1)

    es = ExitStack()
    with es:
        def sb(name, shape, dt=F32, side=None):
            kw = {} if side is None else {"side": side}
            return es.enter_context(nc.sbuf_tensor(name, shape, dt, **kw))

        sid = sb("sid", [128, 128])
        sidb = sb("sidb", [128, 128], BF16)
        cns = sb("cns", [128, 8])
        gstg = sb("gstg", [128, DC * NE])
        gwh = sb("gwh", [128, DC * NE], BF16)
        gwl = sb("gwl", [128, DC * NE], BF16)
        sTb = sb("sTb", [128, 128])
        s_all = sb("s_all", [128, NT * NE])
        gsc = sb("gsc", [128, 6 * NE + 8])
        stats = sb("stats", [128, 16])
        gbsb = sb("gbsb", [1, NE])
        gbr = sb("gbr", [128, NE])
        ones1 = sb("ones1", [1, 128])
        E2 = sb("E2", [2, 128])

        def ps(name, w=1024):
            return es.enter_context(nc.psum_tensor(name, [128, w], F32))

        psW = [ps("psW0"), ps("psW1"), ps("psT")]
        psPV = [ps("psPVa", 512), ps("psPVb", 512)]
        ps_free = {}

        def ps_wait(eng, t, halves=(0, 1)):
            for hf in halves:
                p.waits(eng, ps_free.get((t.name, hf), ()))

        def ps_read(t, ev, halves=(0, 1)):
            for hf in halves:
                ps_free[(t.name, hf)] = [ev]

        def cvt(dst_hi, dst_lo, src_f32, n, evsrc, extra_wait=None):
            p.wait("A", evsrc)
            if extra_wait is not None:
                p.wait("A", extra_wait)
            p.op("A", lambda e: e.copy(dst_hi, src_f32))
            ev_h = p.inc(chA)
            p.wait("V", ev_h)
            p.op("V", lambda e: e.tensor_tensor(dst_lo, src_f32, dst_hi,
                                                ALU.subtract))
            ev_l = p.inc(chV)
            return ev_h, ev_l

        def mm3(out_ap, lh, ll, rh, rl, start, stop):
            p.op("T", lambda e: e.matmul(out_ap, lh, rh, start=start, stop=False))
            p.op("T", lambda e: e.matmul(out_ap, lh, rl, start=False, stop=False))
            p.op("T", lambda e: e.matmul(out_ap, ll, rh, start=False, stop=stop))

        p.op("S", lambda e: e.dma_start(sid[:, :], idn[:, :]))
        evI = p.inc(chM)
        p.op("S", lambda e: e.dma_start(E2[:, :], e2d[:, :]))
        p.inc(chM)
        p.op("S", lambda e: e.dma_start(
            gstg[:, :].rearrange("p (c g) -> p c g", c=DC),
            gw[:, :].rearrange("(c dp) g -> dp c g", dp=128)))
        p.inc(chM)
        p.op("S", lambda e: e.dma_start(gbsb[:, :], gb[:, :]))
        p.inc(chM)
        evM0 = p.now(chM)

        p.op("V", lambda e: e.memset(cns[:, 0:1], -16.0))
        p.op("V", lambda e: e.memset(cns[:, 1:2], 1e-5))
        p.op("V", lambda e: e.memset(ones1[:, :], 1.0))
        p.op("V", lambda e: e.memset(den[:, :], 0.0))
        evZ = p.inc(chV)

        cvt(gwh[:, :], gwl[:, :], gstg[:, :], DC * NE, evM0)
        p.wait("A", evI)
        p.op("A", lambda e: e.copy(sidb[:, :], sid[:, :]))
        evIb = p.inc(chA)

        p.wait("T", evM0)
        p.wait("T", evZ)
        p.op("T", lambda e: e.matmul(psW[1][:, 0:NE], ones1[0:1, :],
                                     gbsb[0:1, :], start=True, stop=True))
        evT = p.inc(chT)
        p.wait("A", evT)
        p.op("A", lambda e: e.copy(gbr[:, :], psW[1][:, 0:NE]))
        evA_gbr = p.inc(chA)
        ps_read(psW[1], evA_gbr)

        lnsc_free = {}

        def ln_tile(xap, dst_writer, lnsc, x_ev):
            p.wait("V", x_ev)
            p.wait("V", lnsc_free.get(lnsc.name))
            p.op("V", lambda e: e.tensor_mul(lnsc[:, :], xap, xap))
            p.op("V", lambda e: e.reduce_sum(stats[:, 0:1], xap, axis=AX))
            p.op("V", lambda e: e.reduce_sum(stats[:, 1:2], lnsc[:, :], axis=AX))
            p.op("V", lambda e: e.tensor_scalar(stats[:, 2:3], stats[:, 0:1],
                                                1.0 / D, None, ALU.mult))
            p.op("V", lambda e: e.tensor_mul(stats[:, 3:4], stats[:, 2:3],
                                             stats[:, 2:3]))
            p.op("V", lambda e: e.scalar_tensor_tensor(
                stats[:, 4:5], stats[:, 1:2], 1.0 / D, stats[:, 3:4],
                ALU.mult, ALU.subtract))
            p.wait("V", evZ)
            p.op("V", lambda e: e.tensor_add(stats[:, 7:8], stats[:, 4:5],
                                             cns[:, 1:2]))
            p.op("V", lambda e: e.reciprocal(stats[:, 8:9], stats[:, 7:8]))
            evv = p.inc(chV)
            p.wait("A", evv)
            p.op("A", lambda e: e.activation(stats[:, 5:6], stats[:, 8:9],
                                             AF.Sqrt))
            eva = p.inc(chA)
            p.wait("V", eva)
            p.op("V", lambda e: e.scalar_tensor_tensor(
                stats[:, 6:7], stats[:, 2:3], -1.0, stats[:, 5:6],
                ALU.mult, ALU.mult))
            p.op("V", lambda e: e.tensor_scalar(lnsc[:, :], xap, stats[:, 5:6],
                                                stats[:, 6:7], ALU.mult, ALU.add))
            evln = p.inc(chV)
            p.wait("T", evln)
            p.wait("T", evI)
            for r0 in range(0, DC, 4):
                hf = (r0 // 4) % 2
                pst = psW[2]
                chunks = list(range(r0, min(r0 + 4, DC)))
                ps_wait("T", pst, (hf,))
                for j, c in enumerate(chunks):
                    p.op("T", lambda e, c=c, j=j, pst=pst, hf=hf: e.transpose(
                        pst[:, hf * 512 + j * 128:hf * 512 + (j + 1) * 128],
                        lnsc[:, c * 128:(c + 1) * 128], sid[:, :]))
                evt = p.inc(chT)
                evr = dst_writer(chunks[0], pst, len(chunks), evt, hf)
                ps_read(pst, evr, (hf,))
            lnsc_free[lnsc.name] = p.now(chT)
            return evln

        HB = QB // 2
        den = sb("den", [128, HB], side="right")
        rdbuf = sb("rdbuf", [128, 2 * QB], side="right")

        with ExitStack() as es1:
            big1 = es1.enter_context(nc.sbuf_tensor("big1", [128, NP * QH], F32))
            p.op("V", lambda e: e.memset(big1[:, :], 0.0))
            evZ2 = p.inc(chV)

            with ExitStack() as esab:
                def sba(name, shape, dt=F32):
                    return esab.enter_context(nc.sbuf_tensor(name, shape, dt))

                xst = sba("xst", [128, D])
                q_h = sba("q_h", [128, NP * QH], BF16)
                q_l = sba("q_l", [128, NP * QH], BF16)
                ln_h = sba("ln_h", [128, DC * MB], BF16)
                ln_l = sba("ln_l", [128, DC * MB], BF16)
                wstg = sba("wstg", [128, DC * 128])
                wkh = sba("wkh", [128, DC * 128], BF16)
                wkl = sba("wkl", [128, DC * 128], BF16)
                wvh = sba("wvh", [128, DC * 128], BF16)
                wvl = sba("wvl", [128, DC * 128], BF16)
                kh2 = [[sba("kh%d%d" % (g, s), [128, MB], BF16)
                        for s in range(2)] for g in range(2)]
                kl2 = [[sba("kl%d%d" % (g, s), [128, MB], BF16)
                        for s in range(2)] for g in range(2)]
                vstg = sba("vstg", [128, MB])
                VS = 68
                vah2 = [[sba("vah%d%d" % (g, s), [128, MC * VS], BF16)
                         for s in range(2)] for g in range(2)]
                val2 = [[sba("val%d%d" % (g, s), [128, MC * VS], BF16)
                         for s in range(2)] for g in range(2)]
                ptf = [sba("ptf%d" % i, [128, QB]) for i in range(2)]
                pth = [[sba("pth%d%d" % (s, j), [128, QB], BF16)
                        for j in range(2)] for s in range(2)]
                ptl = [[sba("ptl%d%d" % (s, j), [128, QB], BF16)
                        for j in range(2)] for s in range(2)]
                stage = [sba("stg%d" % i, [128, 2 * QB]) for i in range(2)]
                lnsc = sba("lnsc_ab", [128, D])

                for g in range(2):
                    for s in range(2):
                        p.op("V", lambda e, g=g, s=s: e.memset(
                            vah2[g][s][:, :], 1.0))
                        p.op("V", lambda e, g=g, s=s: e.memset(
                            val2[g][s][:, :], 0.0))
                        rows0 = slice(64, 128) if s == 0 else slice(0, 64)
                        p.op("V", lambda e, g=g, s=s, rows0=rows0: e.memset(
                            kh2[g][s][rows0, :], 0.0))
                        p.op("V", lambda e, g=g, s=s, rows0=rows0: e.memset(
                            kl2[g][s][rows0, :], 0.0))
                ev_va_init = p.inc(chV)

                xst_free = [None, None]
                wstg_free = [None]
                stage_free = [[], []]
                lnwin_free = []
                q_ready = None
                ptf_free = [None, None]
                pthl_free = [[None, None], [None, None]]

                def load_x_tile(r0, slot_i):
                    p.wait("S", xst_free[0])
                    p.op("S", lambda e, r0=r0: e.dma_start(
                        xst[:, :], xin[r0:r0 + 128, :]))
                    return p.inc(chX[0])

                def ln_window_cbs(row0, nw_tiles):
                    evs = []

                    def mk(tt):
                        def cb():
                            if tt == 0 and lnwin_free:
                                p.wait("A", lnwin_free[-1])
                                p.wait("V", lnwin_free[-1])
                                del lnwin_free[:]
                            evx = load_x_tile(row0 + tt * 128, 0)
                            xap = xst[:, :]

                            def wr(c0, pst, n, evt, hf, tt=tt):
                                hi = bass.AP(ln_h, c0 * MB + tt * 128,
                                             [[DC * MB, 128], [MB, n], [1, 128]])
                                lo = bass.AP(ln_l, c0 * MB + tt * 128,
                                             [[DC * MB, 128], [MB, n], [1, 128]])
                                _, evl = cvt(hi, lo,
                                             pst[:, hf * 512:hf * 512 + n * 128],
                                             n * 128, evt)
                                return evl

                            evln = ln_tile(xap, wr, lnsc, evx)
                            xst_free[0] = evln
                            evs.append(p.now(chV))
                        return cb

                    return evs, [mk(tt) for tt in range(nw_tiles)]

                def ln_window(row0, nw_tiles):
                    evs, cbs = ln_window_cbs(row0, nw_tiles)
                    for cb in cbs:
                        cb()
                    return evs

                def load_w_slice(dram, pcol, slot_i):
                    p.wait("S", wstg_free[0])
                    src = dram[:, pcol * 128:(pcol + 1) * 128].rearrange(
                        "(c dp) e -> dp c e", dp=128)
                    dstv = wstg[:, :].rearrange("p (c e) -> p c e", c=DC)
                    p.op("S", lambda e, src=src, dstv=dstv: e.dma_start(dstv, src))
                    return p.inc(chW[slot_i])

                def gen_mm(pst, wh, wl, w_evl, lnwin_evs, base=0):
                    ps_wait("T", pst, (base // 512,) if MB <= 512 else (0, 1))
                    p.wait("T", w_evl)
                    p.waits("T", lnwin_evs)
                    for c in range(DC):
                        for h in range(MBH):
                            o0 = h * OH
                            w0 = min(OH, MB)
                            mm3(pst[:, base + o0:base + o0 + w0],
                                wh[:, c * 128:(c + 1) * 128],
                                wl[:, c * 128:(c + 1) * 128],
                                bass.AP(ln_h, c * MB + o0,
                                        [[DC * MB, 128], [1, w0]]),
                                bass.AP(ln_l, c * MB + o0,
                                        [[DC * MB, 128], [1, w0]]),
                                start=(c == 0), stop=(c == DC - 1))
                    ev = p.inc(chT)
                    lnwin_free.append(ev)
                    return ev

                for w in range(NQW):
                    lnev = ln_window(w * MB, MB // 128)
                    for pi in range(NP):
                        slot_i = pi % 2
                        wev = load_w_slice(wq, pi, slot_i)
                        wh, wl = (wkh, wkl) if slot_i == 0 else (wvh, wvl)
                        _, wevl = cvt(wh[:, :], wl[:, :], wstg[:, :], DC * 128,
                                      wev)
                        wstg_free[0] = wevl
                        pst = psW[slot_i]
                        evg = gen_mm(pst, wh, wl, wevl, lnev)
                        _, evql = cvt(
                            q_h[:, pi * QH + w * MB: pi * QH + w * MB + MB],
                            q_l[:, pi * QH + w * MB: pi * QH + w * MB + MB],
                            pst[:, 0:MB], MB, evg)
                        ps_read(pst, evql, (0,) if MB <= 512 else (0, 1))
                q_ready = p.now(chV)

                pair_acc_ev = [[None, None] for _ in range(NP)]
                evkl_set = [None, None]
                ev_va_set = [[None, None], [None, None]]
                kread_free = [[], []]
                psT_tr_free = [[], []]

                def make_gen_cbs(mb, pi, seti, lnev):
                    st = {}
                    kh, kl = kh2[seti], kl2[seti]
                    vah, val = vah2[seti], val2[seti]

                    def cb_kload():
                        st["wkev"] = load_w_slice(wk, pi, 0)

                    def cb_kcvt():
                        _, st["wkevl"] = cvt(wkh[:, :], wkl[:, :], wstg[:, :],
                                             DC * 128, st["wkev"])
                        wstg_free[0] = st["wkevl"]

                    def cb_kgen():
                        st["evk"] = gen_mm(psW[2], wkh, wkl, st["wkevl"],
                                           lnev, base=0)

                    def cb_kevac():
                        p.waits("A", kread_free[seti])
                        p.waits("V", kread_free[seti])
                        _, ev1 = cvt(kh[0][0:64, :], kl[0][0:64, :],
                                     psW[2][0:64, 0:MB], MB, st["evk"])
                        _, evkl = cvt(kh[1][64:128, :], kl[1][64:128, :],
                                      psW[2][64:128, 0:MB], MB, st["evk"])
                        ps_read(psW[2], evkl, (0,))
                        evkl_set[seti] = evkl

                    def cb_vload():
                        st["wvev"] = load_w_slice(wv, pi, 1)

                    def cb_vcvt():
                        _, st["wvevl"] = cvt(wvh[:, :], wvl[:, :], wstg[:, :],
                                             DC * 128, st["wvev"])
                        wstg_free[0] = st["wvevl"]

                    def cb_vgen():
                        st["evv"] = gen_mm(psW[2], wvh, wvl, st["wvevl"],
                                           lnev, base=512)

                    def cb_vevac():
                        p.wait("A", st["evv"])
                        p.op("A", lambda e: e.copy(vstg[:, :],
                                                   psW[2][:, 512:512 + MB]))
                        st["evvs"] = p.inc(chA)
                        ps_read(psW[2], st["evvs"], (1,))

                    def mk_vtrans(s):
                        def cb():
                            off = s * 512
                            ps_wait("T", psW[2], (s,))
                            p.wait("T", st["evvs"])
                            p.waits("T", psT_tr_free[s])
                            for c in range(MC):
                                p.op("T", lambda e, s=s, c=c, off=off:
                                     e.transpose(
                                         psW[2][:, off + c * 64:
                                                off + (c + 1) * 64],
                                         vstg[s * 64:(s + 1) * 64,
                                              c * 128:(c + 1) * 128],
                                         sid[s * 64:(s + 1) * 64,
                                             s * 64:(s + 1) * 64]))
                            evt = p.inc(chT)
                            p.waits("A", kread_free[seti])
                            p.waits("V", kread_free[seti])
                            hiap = bass.AP(vah[s], 0,
                                           [[MC * VS, 128], [VS, MC], [1, 64]])
                            loap = bass.AP(val[s], 0,
                                           [[MC * VS, 128], [VS, MC], [1, 64]])
                            _, eva = cvt(hiap, loap,
                                         psW[2][:, off:off + MC * 64],
                                         MC * 64, evt)
                            ps_read(psW[2], eva, (s,))
                            ev_va_set[seti][s] = eva
                        return cb

                    return [cb_kload, cb_kcvt, cb_kgen, cb_kevac,
                            cb_vload, cb_vcvt, cb_vgen, cb_vevac,
                            mk_vtrans(0), mk_vtrans(1)]

                units = [(mb, pi) for mb in range(NB) for pi in range(NP)]
                lnev_cur = None
                lnev_next = None
                ln_next_cbs = []
                for ui, (mb, pi) in enumerate(units):
                    seti = ui % 2
                    if pi == 0:
                        if mb == 0 or lnev_next is None:
                            lnev_cur = ln_window(mb * MB, MB // 128)
                            for cb in make_gen_cbs(mb, 0, seti, lnev_cur):
                                cb()
                        else:
                            lnev_cur = lnev_next
                    evkl = evkl_set[seti]
                    ev_va = ev_va_set[seti]
                    k_h, k_l = kh2[seti], kl2[seti]
                    vah, val = vah2[seti], val2[seti]
                    if pi + 1 < NP:
                        gen_cbs = make_gen_cbs(mb, pi + 1, (ui + 1) % 2,
                                               lnev_cur)
                        if pi + 2 == NP and mb + 1 < NB and NP >= 2:
                            lnev_next, cbs4 = ln_window_cbs((mb + 1) * MB,
                                                            MB // 128)
                            nh = len(cbs4) // 2
                            gen_cbs = gen_cbs + cbs4[:nh]
                            ln_next_cbs = cbs4[nh:]
                    elif mb + 1 < NB and lnev_next is not None:
                        gen_cbs = ln_next_cbs + make_gen_cbs(mb + 1, 0,
                                                             (ui + 1) % 2,
                                                             lnev_next)
                        ln_next_cbs = []
                    else:
                        gen_cbs = []

                    pv_done = [None, None]

                    def emit_S_chain(i, qb, c, s):
                        hs = qb * MC + c
                        par = hs % 2
                        gpar = i % 2
                        off = par * 512
                        pss = psW[s]
                        q0 = pi * QH + qb * QB
                        ps_wait("T", pss, (par,))
                        p.wait("T", evkl)
                        p.wait("T", q_ready)
                        mm3(pss[:, off:off + QB],
                            k_h[s][:, c * 128:(c + 1) * 128],
                            k_l[s][:, c * 128:(c + 1) * 128],
                            q_h[:, q0:q0 + QB],
                            q_l[:, q0:q0 + QB],
                            start=True, stop=True)
                        evs_mm = p.inc(chT)
                        p.wait("A", evs_mm)
                        p.wait("A", evZ)
                        p.wait("A", ptf_free[gpar])
                        p.op("A", lambda e, gpar=gpar, pss=pss, off=off:
                             e.activation(ptf[gpar][:, :],
                                          pss[:, off:off + QB], AF.Exp,
                                          bias=cns[:, 0:1], scale=0.125))
                        ev_exp = p.inc(chA)
                        ps_read(pss, ev_exp, (par,))
                        p.wait("V", ev_exp)
                        p.wait("V", pthl_free[s][par])
                        p.op("V", lambda e, s=s, par=par, gpar=gpar:
                             e.tensor_copy(pth[s][par][:, :],
                                           ptf[gpar][:, :]))
                        p.op("V", lambda e, s=s, par=par, gpar=gpar:
                             e.tensor_tensor(ptl[s][par][:, :],
                                             ptf[gpar][:, :],
                                             pth[s][par][:, :],
                                             ALU.subtract))
                        ev_l = p.inc(chV)
                        ptf_free[gpar] = ev_l
                        return ev_l, par

                    def flush_slot(qb, s, ev_pv):
                        sg = qb % 2
                        chGs = (chG, chG1)[sg]
                        ce = "A" if s == 0 else "V"
                        p.waits(ce, stage_free[sg])
                        p.wait(ce, ev_pv)
                        if s == 0:
                            p.op("A", lambda e, sg=sg: e.copy(
                                stage[sg][0:65, 0:QB], psPV[0][0:65, 0:QB]))
                            evc = p.inc(chA)
                        else:
                            p.op("V", lambda e, sg=sg: e.tensor_copy(
                                stage[sg][0:65, QB:2 * QB], psPV[1][0:65, 0:QB]))
                            evc = p.inc(chV)
                        ps_read(psPV[s], evc, (0,))
                        p.wait("G", evc)
                        p.wait("G", ev_va_init)
                        p.wait("G", evZ2)
                        r0, r1 = (0, 64) if s == 0 else (64, 128)
                        p.op("G", lambda e, pi=pi, qb=qb, sg=sg, s=s,
                             r0=r0, r1=r1: e.dma_start(
                                 big1[r0:r1, pi * QH + qb * QB:
                                      pi * QH + (qb + 1) * QB],
                                 stage[sg][0:64, s * QB:(s + 1) * QB],
                                 accum_op=ALU.add))
                        p.inc(chGs)
                        hh0 = pi if s == 0 else pi + NP
                        p.op("G", lambda e, qb=qb, sg=sg, s=s, hh0=hh0:
                             e.dma_start(
                                 bass.AP(den, (hh0 * NQB + qb) * 2 * HB,
                                         [[HB, 2], [1, HB]]),
                                 stage[sg][64:65, s * QB:(s + 1) * QB],
                                 accum_op=ALU.add))
                        evg = p.inc(chGs)
                        if s == 1:
                            stage_free[sg] = [evg]
                            pair_acc_ev[pi][sg] = evg

                    def emit_PV(st2, chain_ev, par):
                        qb, c, s = st2
                        if c == 0:
                            ps_wait("T", psPV[s], (0,))
                        p.wait("T", chain_ev)
                        p.wait("T", ev_va[s])
                        mm3(psPV[s][0:65, 0:QB],
                            vah[s][:, c * VS:c * VS + 65],
                            val[s][:, c * VS:c * VS + 65],
                            pth[s][par][:, :],
                            ptl[s][par][:, :],
                            start=(c == 0), stop=(c == MC - 1))
                        ev_pv = p.inc(chT)
                        pthl_free[s][par] = ev_pv
                        if c == MC - 1:
                            pv_done[s] = ev_pv
                            flush_slot(qb, s, ev_pv)

                    steps = [(qb, c, s) for qb in range(NQB)
                             for c in range(MC) for s in range(2)]
                    pend = []
                    for i, st2 in enumerate(steps):
                        ch_ev, par = emit_S_chain(i, *st2)
                        pend.append((st2, ch_ev, par))
                        if len(pend) > 3:
                            emit_PV(*pend.pop(0))
                        if gen_cbs and i >= 2 and (i - 2) % 3 == 0:
                            gen_cbs.pop(0)()
                    while gen_cbs:
                        gen_cbs.pop(0)()
                    while pend:
                        emit_PV(*pend.pop(0))
                    kread_free[seti] = [p.now(chT)]
                    psT_tr_free[0] = [p.now(chT)]
                    psT_tr_free[1] = [p.now(chT)]

                bar_pre = [p.now(chT), p.now(chA), p.now(chV), p.now(chG),
                           p.now(chG1)]
                for pe in pair_acc_ev:
                    p.waits("V", pe)
                p.op("V", lambda e: e.reciprocal(den[:, :], den[:, :]))
                ev_rcp = p.inc(chV)

            out_acc = sb("out_acc", [128, NT * D], side="right")
            with ExitStack() as espd:
                def sbp(name, shape, dt=F32):
                    return espd.enter_context(nc.sbuf_tensor(name, shape, dt))

                wph = sbp("wph", [128, DC * D], BF16)
                wpl = sbp("wpl", [128, DC * D], BF16)
                wpstg = sbp("wpstg", [128, 2 * D])
                bpr = sbp("bpr", [128, D])
                bpsb = sbp("bpsb", [1, D])
                xst2 = sbp("xst2", [128, 2 * D])
                fch = [sbp("fch%d" % i, [128, 128], BF16) for i in range(4)]
                fcl = [sbp("fcl%d" % i, [128, 128], BF16) for i in range(4)]
                rdb2 = sbp("rdb2", [128, 2 * QB])

                p.waits("S", bar_pre)
                ev_mult_qb = [None] * NQB
                cslot_free = [None] * 4
                citer = [(qb, pi) for qb in range(NQB) for pi in range(NP)]
                cpos = [0]

                def emit_c(nmax):
                    while cpos[0] < len(citer) and nmax > 0:
                        i = cpos[0]
                        qb, pi = citer[i]
                        cpos[0] += 1
                        nmax -= 1
                        sl = i % 4
                        hf = i % 2
                        buf = rdbuf if sl < 2 else rdb2
                        c0 = (sl % 2) * QB
                        p.wait("S", ev_rcp)
                        p.wait("S", cslot_free[sl])
                        p.op("S", lambda e, pi=pi, qb=qb, buf=buf, c0=c0:
                             e.dma_start(
                                 buf[0:1, c0:c0 + QB],
                                 bass.AP(den, (pi * NQB + qb) * 2 * HB,
                                         [[HB, 2], [1, HB]])))
                        p.inc(chC[sl])
                        p.op("S", lambda e, pi=pi, qb=qb, buf=buf, c0=c0:
                             e.dma_start(
                                 buf[1:2, c0:c0 + QB],
                                 bass.AP(den, ((pi + NP) * NQB + qb) * 2 * HB,
                                         [[HB, 2], [1, HB]])))
                        evd = p.inc(chC[sl])
                        ps_wait("T", psW[2], (hf,))
                        p.wait("T", evd)
                        p.wait("T", evM0)
                        p.op("T", lambda e, buf=buf, c0=c0, hf=hf: e.matmul(
                            psW[2][:, hf * QB:(hf + 1) * QB], E2[0:2, :],
                            buf[0:2, c0:c0 + QB], start=True, stop=True))
                        ev_rep = p.inc(chT)
                        cslot_free[sl] = ev_rep
                        p.wait("V", ev_rep)
                        p.op("V", lambda e, pi=pi, qb=qb, hf=hf: e.tensor_mul(
                            big1[:, pi * QH + qb * QB:
                                 pi * QH + (qb + 1) * QB],
                            big1[:, pi * QH + qb * QB:
                                 pi * QH + (qb + 1) * QB],
                            psW[2][:, hf * QB:(hf + 1) * QB]))
                        ev_mult = p.inc(chV)
                        ps_read(psW[2], ev_mult, (hf,))
                        if pi == NP - 1:
                            ev_mult_qb[qb] = ev_mult

                wps_free = [None, None]
                for c in range(DC):
                    sl = c % 2
                    p.wait("S", wps_free[sl])
                    p.op("S", lambda e, c=c, sl=sl: e.dma_start(
                        wpstg[:, sl * D:(sl + 1) * D],
                        wp[c * 128:(c + 1) * 128, :]))
                    evw = p.inc(chM if sl == 0 else chW1)
                    _, evwl = cvt(wph[:, c * D:(c + 1) * D],
                                  wpl[:, c * D:(c + 1) * D],
                                  wpstg[:, sl * D:(sl + 1) * D], D, evw)
                    wps_free[sl] = evwl
                    emit_c(max(1, len(citer) // DC))
                emit_c(len(citer))
                ev_wp = evwl
                p.op("S", lambda e: e.dma_start(bpsb[:, :], bp[:, :]))
                evbp = p.inc(chW0)
                p.wait("T", evbp)
                p.wait("T", evZ)
                for hh in range(NHALF):
                    pst = psW[hh % 2]
                    ps_wait("T", pst)
                    p.op("T", lambda e, hh=hh, pst=pst: e.matmul(
                        pst[:, 0:OH], ones1[0:1, :],
                        bpsb[0:1, hh * OH:(hh + 1) * OH], start=True,
                        stop=True))
                    evt = p.inc(chT)
                    p.wait("A", evt)
                    p.op("A", lambda e, hh=hh, pst=pst: e.copy(
                        bpr[:, hh * OH:(hh + 1) * OH], pst[:, 0:OH]))
                    eva = p.inc(chA)
                    ps_read(pst, eva)
                ev_bpr = p.now(chA)

                xst2_free = [None, None]
                fc_free = [None] * 4
                evf_q = []
                NGC = NT * DC

                def emit_fcvt(gc):
                    t2, c2 = gc // DC, gc % DC
                    sl = gc % 4
                    p.wait("A", fc_free[sl])
                    _, evf = cvt(fch[sl][:, :], fcl[sl][:, :],
                                 big1[:, c2 * QH + t2 * 128:
                                      c2 * QH + t2 * 128 + 128],
                                 128, ev_mult_qb[t2 // (QB // 128)])
                    evf_q.append((sl, evf))

                for gc in range(min(2, NGC)):
                    emit_fcvt(gc)
                for t in range(NT):
                    pst = psW[t % 2]
                    ps_wait("T", pst)
                    p.wait("T", ev_wp)
                    for c in range(DC):
                        gc = t * DC + c
                        if gc + 2 < NGC:
                            emit_fcvt(gc + 2)
                        sl, evf = evf_q.pop(0)
                        p.wait("T", evf)
                        for hh in range(NHALF):
                            mm3(pst[:, hh * OH:(hh + 1) * OH],
                                fch[sl][:, :], fcl[sl][:, :],
                                wph[:, c * D + hh * OH: c * D + (hh + 1) * OH],
                                wpl[:, c * D + hh * OH: c * D + (hh + 1) * OH],
                                start=(c == 0), stop=(c == DC - 1))
                        evt = p.inc(chT)
                        fc_free[sl] = evt
                    ev_proj = p.now(chT)
                    sl2 = t % 2
                    p.wait("S", xst2_free[sl2])
                    p.op("S", lambda e, t=t, sl2=sl2: e.dma_start(
                        xst2[:, sl2 * D:(sl2 + 1) * D],
                        xin[t * 128:(t + 1) * 128, :]))
                    evx = p.inc(chX[sl2])
                    oa = out_acc[:, t * D:(t + 1) * D]
                    p.wait("V", ev_proj)
                    p.wait("V", ev_bpr)
                    p.op("V", lambda e, oa=oa, pst=pst: e.tensor_add(
                        oa, pst[0:128, 0:D], bpr[:, :]))
                    ev1 = p.inc(chV)
                    ps_read(pst, ev1)
                    p.wait("V", evx)
                    p.op("V", lambda e, oa=oa, sl2=sl2: e.tensor_add(
                        oa, oa, xst2[:, sl2 * D:(sl2 + 1) * D]))
                    ev_x2 = p.inc(chV)
                    xst2_free[sl2] = ev_x2
                x2_done_V = p.now(chV)
                x2_done_T = p.now(chT)

        with ExitStack() as es2:
            def sb2(name, shape, dt=F32):
                return es2.enter_context(nc.sbuf_tensor(name, shape, dt))

            ln2h = sb2("ln2h", [128, NP * QH], BF16)
            ln2l = sb2("ln2l", [128, NP * QH], BF16)
            besb = sb2("besb", [128, D])
            lnsc2 = sb2("lnsc_d", [128, D])

            p.waits("S", [x2_done_V, x2_done_T])
            p.wait("V", x2_done_T)
            p.wait("A", x2_done_T)
            p.op("S", lambda e: e.dma_start(besb[0:NE, :], be[:, :]))
            ev_be = p.inc(chM)

            espe = es2.enter_context(ExitStack())

            def sbe(name, shape, dt=F32):
                return espe.enter_context(nc.sbuf_tensor(name, shape, dt))

            estg = sbe("estg", [128, DC * OH])
            weh = [sbe("weh%d" % i, [128, DC * OH], BF16) for i in range(2)]
            wel = [sbe("wel%d" % i, [128, DC * OH], BF16) for i in range(2)]
            tasks = [(ei, hh) for ei in range(NE) for hh in range(NHALF)]
            web_free = [None, None]
            estg_free = [None]
            ev_we = [None, None]
            ev_ln2_t = [None] * NT

            def load_task(ti, sl):
                ei, hh = tasks[ti]
                p.wait("S", estg_free[0])
                src3 = we[ei, :, :].rearrange("(c dp) o -> dp c o", dp=128)
                src3 = src3[:, :, hh * OH:(hh + 1) * OH]
                p.op("S", lambda e, src3=src3: e.dma_start(
                    estg[:, :].rearrange("p (c o) -> p c o", c=DC), src3))
                evl = p.inc(chE[sl])
                p.waits("A", [web_free[sl]])
                p.waits("V", [web_free[sl]])
                _, evcl = cvt(weh[sl][:, :], wel[sl][:, :], estg[:, :],
                              DC * OH, evl)
                estg_free[0] = evcl
                ev_we[sl] = evcl

            def emit_task_tile(ti, sl, t, ev_ln2_dep):
                ei, hh = tasks[ti]
                pst = psW[t % 3]
                ps_wait("T", pst)
                p.wait("T", ev_we[sl])
                p.wait("T", ev_ln2_dep)
                for c in range(DC):
                    mm3(pst[:, 0:OH],
                        ln2h[:, c * QH + t * 128: c * QH + t * 128 + 128],
                        ln2l[:, c * QH + t * 128: c * QH + t * 128 + 128],
                        weh[sl][:, c * OH:(c + 1) * OH],
                        wel[sl][:, c * OH:(c + 1) * OH],
                        start=(c == 0), stop=(c == DC - 1))
                ev_y = p.inc(chT)
                p.wait("V", ev_y)
                p.op("V", lambda e, hh=hh, t=t, ei=ei, pst=pst:
                     e.scalar_tensor_tensor(
                         out_acc[:, t * D + hh * OH: t * D + (hh + 1) * OH],
                         pst[:, 0:OH],
                         s_all[:, t * NE + ei:t * NE + ei + 1],
                         out_acc[:, t * D + hh * OH: t * D + (hh + 1) * OH],
                         ALU.mult, ALU.add))
                ev_o = p.inc(chV)
                ps_read(pst, ev_o)
                if t == NT - 1:
                    web_free[sl] = p.now(chT)
                if ei == NE - 1:
                    p.wait("S", ev_o)
                    p.op("S", lambda e, t=t, hh=hh: e.dma_start(
                        y[t * 128:(t + 1) * 128, hh * OH:(hh + 1) * OH],
                        out_acc[:, t * D + hh * OH: t * D + (hh + 1) * OH]))
                    p.inc(chO)

            load_task(0, 0)
            for t in range(NT):
                oa = out_acc[:, t * D:(t + 1) * D]

                def wr2(c0, pst, n, evt, hf, t=t):
                    hi = bass.AP(ln2h, c0 * QH + t * 128,
                                 [[NP * QH, 128], [QH, n], [1, 128]])
                    lo = bass.AP(ln2l, c0 * QH + t * 128,
                                 [[NP * QH, 128], [QH, n], [1, 128]])
                    _, evl = cvt(hi, lo,
                                 pst[:, hf * 512:hf * 512 + n * 128],
                                 n * 128, evt)
                    return evl

                ln_tile(oa, wr2, lnsc2, x2_done_V)
                ev_ln2 = p.now(chV)
                pst = psW[1]
                ps_wait("T", pst)
                p.wait("T", ev_ln2)
                p.wait("T", p.now(chA))
                for c in range(DC):
                    mm3(pst[:, 0:NE],
                        ln2h[:, c * QH + t * 128: c * QH + t * 128 + 128],
                        ln2l[:, c * QH + t * 128: c * QH + t * 128 + 128],
                        gwh[:, c * NE:(c + 1) * NE],
                        gwl[:, c * NE:(c + 1) * NE],
                        start=(c == 0), stop=(c == DC - 1))
                ev_gmm = p.inc(chT)
                z0, e0, g0, sl0, w0, sc0 = 0, NE, 2 * NE, 3 * NE, 4 * NE, 6 * NE
                p.wait("V", ev_gmm)
                p.wait("V", evA_gbr)
                p.op("V", lambda e, pst=pst: e.tensor_add(
                    gsc[:, z0:z0 + NE], pst[:, 0:NE], gbr[:, :]))
                ev_z = p.inc(chV)
                ps_read(pst, ev_z)
                p.op("V", lambda e: e.reduce_max(gsc[:, sc0:sc0 + 1],
                                                 gsc[:, z0:z0 + NE], axis=AX,
                                                 negate=True))
                ev_nm = p.inc(chV)
                p.wait("A", ev_nm)
                p.op("A", lambda e: e.activation(gsc[:, e0:e0 + NE],
                                                 gsc[:, z0:z0 + NE], AF.Exp,
                                                 bias=gsc[:, sc0:sc0 + 1],
                                                 scale=1.0))
                ev_ge = p.inc(chA)
                p.wait("V", ev_ge)
                p.op("V", lambda e: e.reduce_sum(gsc[:, sc0 + 1:sc0 + 2],
                                                 gsc[:, e0:e0 + NE], axis=AX))
                p.op("V", lambda e: e.reciprocal(gsc[:, sc0 + 2:sc0 + 3],
                                                 gsc[:, sc0 + 1:sc0 + 2]))
                p.op("V", lambda e: e.tensor_scalar(gsc[:, g0:g0 + NE],
                                                    gsc[:, e0:e0 + NE],
                                                    gsc[:, sc0 + 2:sc0 + 3],
                                                    None, ALU.mult))
                p.op("V", lambda e: e.tensor_single_scalar(
                    gsc[:, sl0:sl0 + NE], gsc[:, g0:g0 + NE], 0.1, ALU.is_ge))
                p.op("V", lambda e: e.tensor_mul(gsc[:, w0:w0 + NE],
                                                 gsc[:, g0:g0 + NE],
                                                 gsc[:, sl0:sl0 + NE]))
                p.op("V", lambda e: e.reduce_sum(gsc[:, sc0 + 3:sc0 + 4],
                                                 gsc[:, w0:w0 + NE], axis=AX))
                p.op("V", lambda e: e.reciprocal(gsc[:, sc0 + 4:sc0 + 5],
                                                 gsc[:, sc0 + 3:sc0 + 4]))
                p.op("V", lambda e, t=t: e.tensor_scalar(
                    s_all[:, t * NE:(t + 1) * NE], gsc[:, w0:w0 + NE],
                    gsc[:, sc0 + 4:sc0 + 5], None, ALU.mult))
                ev_s = p.inc(chV)
                emit_task_tile(0, 0, t, ev_ln2)
                ps_wait("T", psW[2], (0,))
                p.wait("T", ev_s)
                p.op("T", lambda e, t=t: e.transpose(
                    psW[2][0:NE, 0:128], s_all[:, t * NE:(t + 1) * NE],
                    sid[:, :]))
                ev_st = p.inc(chT)
                p.wait("A", ev_st)
                p.op("A", lambda e: e.copy(sTb[0:NE, :], psW[2][0:NE, 0:128]))
                ev_stc = p.inc(chA)
                ps_read(psW[2], ev_stc)
                p.wait("T", ev_stc)
                p.wait("T", ev_be)
                for hh in range(NHALF):
                    pst2 = psPV[hh % 2]
                    ps_wait("T", pst2, (0,))
                    p.op("T", lambda e, hh=hh, pst2=pst2: e.matmul(
                        pst2[:, 0:OH], sTb[0:NE, :],
                        besb[0:NE, hh * OH:(hh + 1) * OH],
                        start=True, stop=True))
                    evb = p.inc(chT)
                    p.wait("V", evb)
                    p.op("V", lambda e, hh=hh, t=t, pst2=pst2: e.tensor_add(
                        out_acc[:, t * D + hh * OH: t * D + (hh + 1) * OH],
                        out_acc[:, t * D + hh * OH: t * D + (hh + 1) * OH],
                        pst2[:, 0:OH]))
                    evv = p.inc(chV)
                    ps_read(pst2, evv, (0,))
                ev_ln2_t[t] = ev_ln2
            ln2_all = p.now(chV)

            load_task(1, 1)
            for ti in range(1, len(tasks)):
                sl = ti % 2
                if ti + 1 < len(tasks):
                    load_task(ti + 1, (ti + 1) % 2)
                for t in range(NT):
                    emit_task_tile(ti, sl, t, ln2_all)

        ev_out = p.now(chO)

        for c in p.chans:
            c.sem = es.enter_context(nc.semaphore(c.name))
        block = es.enter_context(nc.Block())
        eng_map = {"T": block.tensor, "A": block.scalar, "V": block.vector,
                   "S": block.sync, "G": block.gpsimd}

        def runner(eng_name):
            def run(engobj):
                last = None
                for it in p.items[eng_name]:
                    if it[0] == "op":
                        r = it[1](engobj)
                        if r is not None:
                            last = r
                    elif it[0] == "inc":
                        last.then_inc(it[1].sem, it[1].step)
                    elif it[0] == "drain":
                        engobj.drain()
                    else:
                        engobj.wait_ge(it[1].sem, it[2])
                if eng_name == "S":
                    engobj.wait_ge(ev_out[0].sem, ev_out[1])
            return run

        for name, deco in eng_map.items():
            deco(runner(name))
    return nc


def _prep_inputs(cfg, L0_attn_query, Wq, Wk, Wv, Wp, bp, gate_w, gate_b,
                 expert_w, expert_b):
    D, H, NSEQ, QH, NE = (cfg["D"], cfg["H"], cfg["NSEQ"], cfg["QH"], cfg["NE"])
    NP = H // 2
    x = np.ascontiguousarray(np.asarray(L0_attn_query, dtype=np.float32))
    B = x.shape[0]
    head_order = []
    for pi in range(NP):
        head_order += [pi, pi + NP]
    wq_h = np.ascontiguousarray(
        np.asarray(Wq, np.float32).transpose(1, 0, 2)[:, head_order, :].reshape(D, D))
    wk_h = np.ascontiguousarray(
        np.asarray(Wk, np.float32).transpose(1, 0, 2)[:, head_order, :].reshape(D, D))
    wv_h = np.ascontiguousarray(
        np.asarray(Wv, np.float32).transpose(1, 0, 2)[:, head_order, :].reshape(D, D))
    idx = np.empty(D, dtype=np.int64)
    for pi in range(NP):
        for s in range(2):
            h = pi + NP * s
            for e in range(64):
                idx[pi * 128 + s * 64 + e] = e * H + h
    wp_h = np.ascontiguousarray(np.asarray(Wp, np.float32)[idx])
    in_maps = []
    ncore = 2 * B
    for c in range(ncore):
        b, half = c // 2, c % 2
        ours = x[b, half * QH:(half + 1) * QH]
        other = x[b, (1 - half) * QH:(2 - half) * QH]
        xkv = np.ascontiguousarray(np.concatenate([ours, other], axis=0))
        in_maps.append({
            "xin": xkv,
            "wq": wq_h, "wk": wk_h, "wv": wv_h, "wp": wp_h,
            "bp": np.asarray(bp, np.float32).reshape(1, D),
            "gw": np.ascontiguousarray(np.asarray(gate_w, np.float32)),
            "gb": np.asarray(gate_b, np.float32).reshape(1, NE),
            "we": np.ascontiguousarray(np.asarray(expert_w, np.float32)),
            "be": np.ascontiguousarray(np.asarray(expert_b, np.float32)),
            "idn": np.eye(128, dtype=np.float32),
            "e2d": np.repeat(np.eye(2, dtype=np.float32), 64, axis=1),
        })
    return in_maps


def kernel(L0_attn_query, L0_attn_Wq, L0_attn_Wk, L0_attn_Wv, L0_attn_Wp,
           L0_attn_bp, L0_ffn_gate_w, L0_ffn_gate_b, L0_ffn_expert_w,
           L0_ffn_expert_b):
    cfg = CFG_FULL
    in_maps = _prep_inputs(cfg, L0_attn_query, L0_attn_Wq, L0_attn_Wk,
                           L0_attn_Wv, L0_attn_Wp, L0_attn_bp, L0_ffn_gate_w,
                           L0_ffn_gate_b, L0_ffn_expert_w, L0_ffn_expert_b)
    nc = build_program(cfg)
    res = run_bass_kernel_spmd(nc, in_maps, core_ids=list(range(len(in_maps))))
    B, NSEQ, D, QH = 4, cfg["NSEQ"], cfg["D"], cfg["QH"]
    out = np.empty((B, NSEQ, D), dtype=np.float32)
    for c, r in enumerate(res.results):
        b, half = c // 2, c % 2
        out[b, half * QH:(half + 1) * QH] = np.asarray(r["y"], dtype=np.float32)
    return out
